# version 15
# baseline (speedup 1.0000x reference)
import numpy as np
from contextlib import ExitStack

import concourse.bass as bass
import concourse.mybir as mybir
from concourse.bass_utils import run_bass_kernel_spmd

F32 = mybir.dt.float32
BF16 = mybir.dt.bfloat16
F8 = mybir.dt.float8e4
AF = mybir.ActivationFunctionType
ALU = mybir.AluOpType
AX = mybir.AxisListType

EMB = 2048
HEADS = 16
KVH = 8
D = 128
IFF = 8192
NJC = IFF // 128
SCALE_S = float(1.0 / np.sqrt(np.float32(2048.0)))
EXP_SHIFT = -5.0
LN_EPS = 1e-5
INV_224 = float(np.float32(1.0) / np.float32(224.0))
QEPS2 = 2e-30


class Ring:

    def __init__(self, mksem, name, slots):
        self.sems = [mksem(name + str(i)) for i in range(slots)]
        self.slots = slots
        self.issued = 0

    def attach(self, instr):
        instr.then_inc(self.sems[self.issued % self.slots], 16)
        self.issued += 1
        return instr

    def wait(self, eng, i):
        eng.wait_ge(self.sems[i % self.slots], 16 * (i // self.slots + 1))

    def wait_all(self, eng):
        n = self.issued
        for si in range(self.slots):
            c = (n - si + self.slots - 1) // self.slots
            if c > 0:
                eng.wait_ge(self.sems[si], 16 * c)


class Fenced:

    def __init__(self, eng, sem, cnt, key):
        self._eng = eng
        self._sem = sem
        self._cnt = cnt
        self._key = key

    def _f(self, ins):
        self._cnt[self._key] += 1
        ins.then_inc(self._sem, 1)
        self._eng.wait_ge(self._sem, self._cnt[self._key])
        return ins

    def wait_ge(self, sem, val):
        self._eng.wait_ge(sem, val)

    def __getattr__(self, name):
        fn = getattr(self._eng, name)

        def call(*a, **k):
            return self._f(fn(*a, **k))
        return call


def _rope_tm(v, ps, cs, sn, out, temps, nh):
    kk = ps[:].rearrange("p (h d) -> p h d", h=nh)
    ka, kb = kk[:, :, 0:64], kk[:, :, 64:128]
    csA = cs[:, 0:64].unsqueeze(1).broadcast_to([128, nh, 64])
    csB = cs[:, 64:128].unsqueeze(1).broadcast_to([128, nh, 64])
    snA = sn[:, 0:64].unsqueeze(1).broadcast_to([128, nh, 64])
    snB = sn[:, 64:128].unsqueeze(1).broadcast_to([128, nh, 64])
    r0 = temps[0][:].rearrange("p (h d) -> p h d", h=nh)
    r1 = temps[1][:].rearrange("p (h d) -> p h d", h=nh)
    ko = out.rearrange("p (h d) -> p h d", h=nh)
    v.tensor_tensor(r0, ka, csA, ALU.mult)
    v.tensor_tensor(r1, kb, snA, ALU.mult)
    v.tensor_tensor(ko[:, :, 0:64], r0, r1, ALU.subtract)
    v.tensor_tensor(r0, kb, csB, ALU.mult)
    v.tensor_tensor(r1, ka, snB, ALU.mult)
    v.tensor_tensor(ko[:, :, 64:128], r0, r1, ALU.add)


ROPE_OPS = 6


def _quant(v, src_f32, nb, tmp_v, tmp_q8, red, s2, rcp, out_bf):
    s3 = src_f32.rearrange("p (b r) -> p b r", r=32)
    v.tensor_reduce(red[:, 0:nb], s3, axis=AX.X, op=ALU.max,
                    apply_absolute_value=True)
    v.tensor_scalar(s2[:, 0:nb], red[:, 0:nb], INV_224, None, ALU.mult)
    v.tensor_scalar_max(s2[:, 0:nb], s2[:, 0:nb], QEPS2)
    v.reciprocal(rcp[:, 0:nb], s2[:, 0:nb])
    rb = rcp[:, 0:nb].unsqueeze(2).broadcast_to([128, nb, 32])
    s2b = s2[:, 0:nb].unsqueeze(2).broadcast_to([128, nb, 32])
    t3 = tmp_v.rearrange("p (b r) -> p b r", r=32)
    q3 = tmp_q8.rearrange("p (b r) -> p b r", r=32)
    v.tensor_tensor(t3, s3, rb, ALU.mult)
    v.tensor_copy(tmp_q8, tmp_v)
    o3 = out_bf.rearrange("p (b r) -> p b r", r=32)
    v.tensor_tensor(o3, q3, s2b, ALU.mult)


QUANT_OPS = 7


def build_nc(S_full=4096, S_own=2048):
    assert S_full % 256 == 0 and S_own % 512 == 0
    NT_F = S_full // 128
    NT_O = S_own // 128
    NG = S_own // 512
    NKC = S_full // 128
    NPAIR = NKC // 2
    NTG = S_own // 512
    NTT = NT_O
    NE = HEADS * NPAIR

    nc = bass.Bass("TRN2", target_bir_lowering=False)

    x_seq = nc.dram_tensor("x_seq", [S_full, EMB], F32, kind="ExternalInput").ap()
    x_own = nc.dram_tensor("x_own", [S_own, EMB], F32, kind="ExternalInput").ap()
    wqt = nc.dram_tensor("wqt", [EMB, EMB], F32, kind="ExternalInput").ap()
    wkt = nc.dram_tensor("wkt", [EMB, KVH * D], F32, kind="ExternalInput").ap()
    wvt = nc.dram_tensor("wvt", [EMB, KVH * D], F32, kind="ExternalInput").ap()
    wot = nc.dram_tensor("wot", [EMB, EMB], F32, kind="ExternalInput").ap()
    gate_w = nc.dram_tensor("gate_w", [IFF, EMB], F32, kind="ExternalInput").ap()
    up_w = nc.dram_tensor("up_w", [IFF, EMB], F32, kind="ExternalInput").ap()
    down_wr = nc.dram_tensor("down_wr", [NJC, EMB, 128], F32, kind="ExternalInput").ap()
    cos_t = nc.dram_tensor("cos_t", [S_full, D], F32, kind="ExternalInput").ap()
    sin_t = nc.dram_tensor("sin_t", [S_full, D], F32, kind="ExternalInput").ap()
    cos_o = nc.dram_tensor("cos_o", [S_own, D], F32, kind="ExternalInput").ap()
    sin_o = nc.dram_tensor("sin_o", [S_own, D], F32, kind="ExternalInput").ap()
    out_own = nc.dram_tensor("out_own", [S_own, EMB], F32, kind="ExternalOutput").ap()
    mid_dram = nc.dram_tensor("mid_dram", [S_own, EMB], F32).ap()
    xmid_dram = nc.dram_tensor("xmid_dram", [S_own, EMB], F32).ap()
    hq_dram = nc.dram_tensor("hq_dram", [NJC, 128, S_own], BF16).ap()
    dq_dram = nc.dram_tensor("dq_dram", [NJC, 128, EMB], BF16).ap()

    with ExitStack() as top:
        E = top.enter_context
        mksem = lambda n: E(nc.semaphore(n))
        ones_bf = E(nc.sbuf_tensor("ones_bf", [128, 128], BF16))
        ones_f32 = E(nc.sbuf_tensor("ones_f32", [128, 128], F32))
        bias_shift = E(nc.sbuf_tensor("bias_shift", [128, 1], F32))
        bias_eps = E(nc.sbuf_tensor("bias_eps", [128, 1], F32))

        eng_names = ["akMM", "avMM", "bWQC", "bS", "bPV", "bTO", "bRS",
                     "gMM", "hMM", "vF", "scF", "gF"]
        S = {n: mksem(n) for n in eng_names}
        C = {n: 0 for n in eng_names}

        R = {}
        for name, slots in [
            ("akW", 1), ("akX", 2), ("akTX", 2), ("akCS", 2), ("akSN", 2), ("akKT", 2),
            ("avW", 1), ("avX", 2), ("avTX", 2),
            ("bX", 2), ("bTX", 2), ("bCS", 2), ("bSN", 2), ("bWQ", 2), ("bQT", 2),
            ("bWO", 2), ("bMS", 2),
            ("fM", 3), ("fX", 3), ("fXM", 2), ("fXQT", 2),
            ("gG", 2), ("gU", 2), ("gD", 2), ("gTG", 2), ("gTU", 2), ("gTD", 2),
            ("gDS", 2), ("gHT", 2), ("gHS", 2),
            ("hH", 3), ("hD", 3), ("hXM", 4), ("hOUT", 2),
        ]:
            R[name] = Ring(mksem, name, slots)

        with nc.Block() as blk:
            @blk.gpsimd
            def _(g):
                g.memset(ones_bf[:], 1.0)
                g.memset(ones_f32[:], 1.0)
                g.memset(bias_shift[:], EXP_SHIFT)
                g.memset(bias_eps[:], LN_EPS)

        with ExitStack() as att_scope:
            Ea = att_scope.enter_context
            k_fm = Ea(nc.sbuf_tensor("k_fm", [128, KVH * S_full], BF16))
            v_tm = Ea(nc.sbuf_tensor("v_tm", [128, NKC * KVH * D], BF16))

            vbAK = C["vF"]

            def akRP(t):
                return vbAK + ROPE_OPS * (t + 1)

            with ExitStack() as pa:
                Ek = pa.enter_context
                wkt_sb = Ek(nc.sbuf_tensor("wkt_sb", [128, 16 * KVH * D], BF16))
                xb = [Ek(nc.sbuf_tensor(f"akxb{i}", [128, EMB], BF16)) for i in range(2)]
                xfm = [Ek(nc.sbuf_tensor(f"akxf{i}", [128, EMB], BF16)) for i in range(2)]
                krot = [Ek(nc.sbuf_tensor(f"akkr{i}", [128, KVH * D], BF16)) for i in range(2)]
                cs_sb = [Ek(nc.sbuf_tensor(f"akcs{i}", [128, D], F32)) for i in range(2)]
                sn_sb = [Ek(nc.sbuf_tensor(f"aksn{i}", [128, D], F32)) for i in range(2)]
                rt = [Ek(nc.sbuf_tensor(f"akrt{i}", [128, KVH * 64], F32)) for i in range(2)]
                ps_k = [Ek(nc.psum_tensor(f"ps_k{i}", [128, KVH * D], F32)) for i in range(2)]

                with nc.Block() as blk:
                    @blk.gpsimd
                    def _(g):
                        R["akW"].attach(
                            g.dma_start(out=wkt_sb[:].rearrange("p (i o) -> p i o", i=16),
                                        in_=wkt.rearrange("(i p) o -> p i o", p=128)))
                        for t in range(NT_F):
                            if t >= 2:
                                R["akTX"].wait(g, t - 2)
                            R["akX"].attach(
                                g.dma_start(out=xb[t % 2][:],
                                            in_=x_seq[t * 128:(t + 1) * 128, :]))
                        R["akX"].wait_all(g)
                        R["akW"].wait_all(g)

                    @blk.sync
                    def _(s):
                        for t in range(NT_F + 1):
                            if t < NT_F:
                                R["akX"].wait(s, t)
                                if t >= 2:
                                    s.wait_ge(S["akMM"], t - 1)
                                R["akTX"].attach(s.dma_start(
                                    out=xfm[t % 2][:].rearrange("p (b f) -> p b f", b=16),
                                    in_=xb[t % 2][:].rearrange("p (b f) -> p b f", b=16),
                                    transpose=True))
                                if t >= 2:
                                    s.wait_ge(S["vF"], akRP(t - 2))
                                R["akCS"].attach(
                                    s.dma_start(out=cs_sb[t % 2][:],
                                                in_=cos_t[t * 128:(t + 1) * 128, :]))
                                R["akSN"].attach(
                                    s.dma_start(out=sn_sb[t % 2][:],
                                                in_=sin_t[t * 128:(t + 1) * 128, :]))
                            if t >= 1:
                                tk = t - 1
                                s.wait_ge(S["vF"], akRP(tk))
                                R["akKT"].attach(s.dma_start(
                                    out=k_fm[:].rearrange("p (h s) -> p h s", h=KVH)
                                        [:, :, tk * 128:(tk + 1) * 128],
                                    in_=krot[tk % 2][:].rearrange("p (h f) -> p h f", h=KVH),
                                    transpose=True))
                        for rn in ("akTX", "akCS", "akSN", "akKT"):
                            R[rn].wait_all(s)

                    @blk.tensor
                    def _(te):
                        R["akW"].wait(te, 0)
                        for t in range(NT_F):
                            R["akTX"].wait(te, t)
                            if t >= 2:
                                te.wait_ge(S["vF"], akRP(t - 2))
                            for i in range(16):
                                st, sp = (i == 0), (i == 15)
                                lhs = xfm[t % 2][:, i * 128:(i + 1) * 128]
                                te.matmul(ps_k[t % 2][:, 0:512], lhs,
                                          wkt_sb[:, i * 1024:i * 1024 + 512],
                                          start=st, stop=sp)
                                ins = te.matmul(ps_k[t % 2][:, 512:1024], lhs,
                                                wkt_sb[:, i * 1024 + 512:(i + 1) * 1024],
                                                start=st, stop=sp)
                                if sp:
                                    ins.then_inc(S["akMM"], 1)
                        C["akMM"] = NT_F

                    @blk.vector
                    def _(v):
                        v = Fenced(v, S["vF"], C, "vF")
                        for t in range(NT_F):
                            v.wait_ge(S["akMM"], t + 1)
                            R["akCS"].wait(v, t)
                            R["akSN"].wait(v, t)
                            if t >= 2:
                                R["akKT"].wait(v, t - 2)
                            _rope_tm(v, ps_k[t % 2], cs_sb[t % 2][:],
                                     sn_sb[t % 2][:], krot[t % 2][:], rt, KVH)
                        assert C["vF"] == vbAK + ROPE_OPS * NT_F

            sbAV = C["scF"]
            with ExitStack() as pv:
                Ev = pv.enter_context
                wvt_sb = Ev(nc.sbuf_tensor("wvt_sb", [128, 16 * KVH * D], BF16))
                xb = [Ev(nc.sbuf_tensor(f"avxb{i}", [128, EMB], BF16)) for i in range(2)]
                xfm = [Ev(nc.sbuf_tensor(f"avxf{i}", [128, EMB], BF16)) for i in range(2)]
                ps_v = [Ev(nc.psum_tensor(f"ps_v{i}", [128, KVH * D], F32)) for i in range(2)]

                with nc.Block() as blk:
                    @blk.gpsimd
                    def _(g):
                        R["avW"].attach(
                            g.dma_start(out=wvt_sb[:].rearrange("p (i o) -> p i o", i=16),
                                        in_=wvt.rearrange("(i p) o -> p i o", p=128)))
                        for t in range(NT_F):
                            if t >= 2:
                                R["avTX"].wait(g, t - 2)
                            R["avX"].attach(
                                g.dma_start(out=xb[t % 2][:],
                                            in_=x_seq[t * 128:(t + 1) * 128, :]))
                        R["avX"].wait_all(g)
                        R["avW"].wait_all(g)

                    @blk.sync
                    def _(s):
                        for t in range(NT_F):
                            R["avX"].wait(s, t)
                            if t >= 2:
                                s.wait_ge(S["avMM"], t - 1)
                            R["avTX"].attach(s.dma_start(
                                out=xfm[t % 2][:].rearrange("p (b f) -> p b f", b=16),
                                in_=xb[t % 2][:].rearrange("p (b f) -> p b f", b=16),
                                transpose=True))
                        R["avTX"].wait_all(s)

                    @blk.tensor
                    def _(te):
                        R["avW"].wait(te, 0)
                        for t in range(NT_F):
                            R["avTX"].wait(te, t)
                            if t >= 2:
                                te.wait_ge(S["scF"], sbAV + t - 1)
                            for i in range(16):
                                st, sp = (i == 0), (i == 15)
                                lhs = xfm[t % 2][:, i * 128:(i + 1) * 128]
                                te.matmul(ps_v[t % 2][:, 0:512], lhs,
                                          wvt_sb[:, i * 1024:i * 1024 + 512],
                                          start=st, stop=sp)
                                ins = te.matmul(ps_v[t % 2][:, 512:1024], lhs,
                                                wvt_sb[:, i * 1024 + 512:(i + 1) * 1024],
                                                start=st, stop=sp)
                                if sp:
                                    ins.then_inc(S["avMM"], 1)
                        C["avMM"] = NT_F

                    @blk.scalar
                    def _(s):
                        s = Fenced(s, S["scF"], C, "scF")
                        for t in range(NT_F):
                            s.wait_ge(S["avMM"], t + 1)
                            s.copy(v_tm[:, t * 1024:(t + 1) * 1024], ps_v[t % 2][:])
                        assert C["scF"] == sbAV + NT_F

            q_fm = Ea(nc.sbuf_tensor("q_fm", [128, HEADS * 512], BF16))
            ctx_fm = Ea(nc.sbuf_tensor("ctx_fm", [128, HEADS * 512], BF16))

            for gi in range(NG):
                q0 = gi * 512
                bWQC0 = gi * 32
                bS0 = bPV0 = gi * NE
                bTO0 = gi * 16
                bRS0 = gi * HEADS

                vbB1 = C["vF"]

                def bQR(tl, _vb=vbB1):
                    return _vb + ROPE_OPS * (tl + 1)

                with ExitStack() as pb, nc.Block() as blk:
                    Eb = pb.enter_context
                    xb2 = [Eb(nc.sbuf_tensor(f"b1xb{gi}_{i}", [128, EMB], BF16)) for i in range(2)]
                    xfm2 = [Eb(nc.sbuf_tensor(f"b1xf{gi}_{i}", [128, EMB], BF16)) for i in range(2)]
                    qrot = [Eb(nc.sbuf_tensor(f"b1qr{gi}_{i}", [128, EMB], BF16)) for i in range(2)]
                    wq_buf = [Eb(nc.sbuf_tensor(f"b1wq{gi}_{i}", [128, EMB], BF16)) for i in range(2)]
                    cs2 = [Eb(nc.sbuf_tensor(f"b1cs{gi}_{i}", [128, D], F32)) for i in range(2)]
                    sn2 = [Eb(nc.sbuf_tensor(f"b1sn{gi}_{i}", [128, D], F32)) for i in range(2)]
                    rt2 = [Eb(nc.sbuf_tensor(f"b1rt{gi}_{i}", [128, HEADS * 64], F32)) for i in range(2)]
                    ps_q = [Eb(nc.psum_tensor(f"b1pq{gi}_{i}", [128, EMB], F32)) for i in range(2)]

                    @blk.gpsimd
                    def _(g):
                        for tl in range(4):
                            if tl >= 2:
                                R["bTX"].wait(g, gi * 4 + tl - 2)
                            R["bX"].attach(
                                g.dma_start(out=xb2[tl % 2][:],
                                            in_=x_own[q0 + tl * 128:q0 + (tl + 1) * 128, :]))
                        for k in range(32):
                            i = k % 16
                            if k >= 2:
                                g.wait_ge(S["bWQC"], bWQC0 + k - 1)
                            R["bWQ"].attach(
                                g.dma_start(out=wq_buf[k % 2][:],
                                            in_=wqt[i * 128:(i + 1) * 128, :]))
                        R["bX"].wait_all(g)
                        R["bWQ"].wait_all(g)

                    @blk.sync
                    def _(s):
                        for tl in range(4):
                            R["bX"].wait(s, gi * 4 + tl)
                            if tl >= 2:
                                s.wait_ge(S["bWQC"], bWQC0 + 16 * ((tl - 2) // 2 + 1))
                            R["bTX"].attach(s.dma_start(
                                out=xfm2[tl % 2][:].rearrange("p (b f) -> p b f", b=16),
                                in_=xb2[tl % 2][:].rearrange("p (b f) -> p b f", b=16),
                                transpose=True))
                            if tl >= 2:
                                s.wait_ge(S["vF"], bQR(tl - 2))
                            R["bCS"].attach(
                                s.dma_start(out=cs2[tl % 2][:],
                                            in_=cos_o[q0 + tl * 128:q0 + (tl + 1) * 128, :]))
                            R["bSN"].attach(
                                s.dma_start(out=sn2[tl % 2][:],
                                            in_=sin_o[q0 + tl * 128:q0 + (tl + 1) * 128, :]))
                            if tl >= 1:
                                tk = tl - 1
                                s.wait_ge(S["vF"], bQR(tk))
                                R["bQT"].attach(s.dma_start(
                                    out=q_fm[:].rearrange("p (h s) -> p h s", h=HEADS)
                                        [:, :, tk * 128:(tk + 1) * 128],
                                    in_=qrot[tk % 2][:].rearrange("p (h f) -> p h f", h=HEADS),
                                    transpose=True))
                        tk = 3
                        s.wait_ge(S["vF"], bQR(tk))
                        R["bQT"].attach(s.dma_start(
                            out=q_fm[:].rearrange("p (h s) -> p h s", h=HEADS)
                                [:, :, tk * 128:(tk + 1) * 128],
                            in_=qrot[tk % 2][:].rearrange("p (h f) -> p h f", h=HEADS),
                            transpose=True))
                        for rn in ("bTX", "bCS", "bSN", "bQT"):
                            R[rn].wait_all(s)

                    @blk.tensor
                    def _(te):
                        for sw in range(2):
                            tls = (2 * sw, 2 * sw + 1)
                            for i in range(16):
                                k = sw * 16 + i
                                R["bWQ"].wait(te, gi * 32 + k)
                                st, sp = (i == 0), (i == 15)
                                for tl in tls:
                                    if st:
                                        R["bTX"].wait(te, gi * 4 + tl)
                                        if tl >= 2:
                                            te.wait_ge(S["vF"], bQR(tl - 2))
                                    lhs = xfm2[tl % 2][:, i * 128:(i + 1) * 128]
                                    last = None
                                    for oc in range(4):
                                        last = te.matmul(
                                            ps_q[tl % 2][:, oc * 512:(oc + 1) * 512], lhs,
                                            wq_buf[k % 2][:, oc * 512:(oc + 1) * 512],
                                            start=st, stop=sp)
                                    if tl == tls[1]:
                                        last.then_inc(S["bWQC"], 1)
                        C["bWQC"] = bWQC0 + 32

                    @blk.vector
                    def _(v):
                        v = Fenced(v, S["vF"], C, "vF")
                        for tl in range(4):
                            v.wait_ge(S["bWQC"], bWQC0 + 16 * (tl // 2 + 1))
                            R["bCS"].wait(v, gi * 4 + tl)
                            R["bSN"].wait(v, gi * 4 + tl)
                            if tl >= 2:
                                R["bQT"].wait(v, gi * 4 + tl - 2)
                            _rope_tm(v, ps_q[tl % 2], cs2[tl % 2][:],
                                     sn2[tl % 2][:], qrot[tl % 2][:], rt2, HEADS)
                        assert C["vF"] == vbB1 + ROPE_OPS * 4

                vbB2 = C["vF"]
                sbB2 = C["scF"]

                HOPS = 2 * NPAIR + 1

                def bVC(h, _vb=vbB2):
                    return _vb + HOPS * (h + 1)

                def vAdd(e, _vb=vbB2):
                    return _vb + HOPS * (e // NPAIR) + 1 + 2 * (e % NPAIR)

                def vAddLast(h, _vb=vbB2):
                    return _vb + HOPS * h + 2 * NPAIR - 1

                def bEX(e, _sb=sbB2):
                    return _sb + e + 1

                def bOE(j, _sb=sbB2):
                    return _sb + NE + j + 1

                with ExitStack() as pb, nc.Block() as blk:
                    Eb = pb.enter_context
                    pT = [Eb(nc.sbuf_tensor(f"b2pT{gi}_{i}", [128, 1024], BF16)) for i in range(3)]
                    pTacc = Eb(nc.sbuf_tensor(f"b2pa{gi}", [128, 512], F32))
                    rbc = Eb(nc.sbuf_tensor(f"b2rb{gi}", [128, 512], F32))
                    wo_buf = [Eb(nc.sbuf_tensor(f"b2wo{gi}_{i}", [128, 16 * 512], BF16))
                              for i in range(2)]
                    midb = [Eb(nc.sbuf_tensor(f"b2mb{gi}_{i}", [128, 512], F32)) for i in range(2)]
                    ps_s = [Eb(nc.psum_tensor(f"b2ps{gi}_{i}", [128, 1024], F32)) for i in range(2)]
                    ps_ctx = Eb(nc.psum_tensor(f"b2pc{gi}", [128, 512], F32))
                    ps_row = Eb(nc.psum_tensor(f"b2prw{gi}", [128, 512], F32))
                    ps_o = [Eb(nc.psum_tensor(f"b2po{gi}_{i}", [128, 512], F32)) for i in range(2)]

                    @blk.gpsimd
                    def _(g):
                        for oc in range(4):
                            if oc >= 2:
                                g.wait_ge(S["bTO"], bTO0 + 4 * (oc - 1))
                            R["bWO"].attach(g.dma_start(
                                out=wo_buf[oc % 2][:].rearrange("p (i o) -> p i o", i=16),
                                in_=wot.rearrange("(i p) o -> p i o", p=128)
                                    [:, :, oc * 512:(oc + 1) * 512]))
                        R["bWO"].wait_all(g)

                    @blk.tensor
                    def _(te):
                        R["bQT"].wait(te, gi * 4 + 3)
                        R["bQT"].wait(te, gi * 4 + 2)

                        def emit_S(e):
                            h, pr = e // NPAIR, e % NPAIR
                            kv = h // 2
                            if e >= 2:
                                te.wait_ge(S["scF"], bEX(e - 2))
                            last = None
                            for half in range(2):
                                kc = 2 * pr + half
                                last = te.matmul(
                                    ps_s[e % 2][:, half * 512:half * 512 + 512],
                                    k_fm[:, kv * S_full + kc * 128:kv * S_full + (kc + 1) * 128],
                                    q_fm[:, h * 512:(h + 1) * 512],
                                    start=True, stop=True)
                            last.then_inc(S["bS"], 1)

                        def emit_PV(e):
                            h, pr = e // NPAIR, e % NPAIR
                            kv = h // 2
                            te.wait_ge(S["scF"], bEX(e))
                            if pr == 0 and h >= 1:
                                te.wait_ge(S["vF"], bVC(h - 1))
                            last = None
                            for half in range(2):
                                kc = 2 * pr + half
                                st, sp = (kc == 0), (kc == NKC - 1)
                                psl = pT[e % 3][:, half * 512:half * 512 + 512]
                                last = te.matmul(
                                    ps_ctx[:],
                                    v_tm[:, kc * 1024 + kv * 128:kc * 1024 + (kv + 1) * 128],
                                    psl, start=st, stop=sp)
                            last.then_inc(S["bPV"], 1)

                        def emit_rowsum(h):
                            te.wait_ge(S["vF"], vAddLast(h))
                            te.matmul(ps_row[:], ones_f32[:], pTacc[:],
                                      start=True, stop=True).then_inc(S["bRS"], 1)

                        for e in range(NE):
                            emit_S(e)
                            if e >= 1:
                                emit_PV(e - 1)
                                if (e - 1) % NPAIR == NPAIR - 1:
                                    emit_rowsum((e - 1) // NPAIR)
                        emit_PV(NE - 1)
                        emit_rowsum(HEADS - 1)
                        C["bS"] = bS0 + NE
                        C["bPV"] = bPV0 + NE
                        C["bRS"] = bRS0 + HEADS

                        te.wait_ge(S["vF"], bVC(HEADS - 1))
                        for oc in range(4):
                            R["bWO"].wait(te, gi * 4 + oc)
                            for qt in range(4):
                                j = 4 * oc + qt
                                if j >= 2:
                                    te.wait_ge(S["scF"], bOE(j - 2))
                                last = None
                                for i in range(16):
                                    last = te.matmul(
                                        ps_o[j % 2][:],
                                        ctx_fm[:, i * 512 + qt * 128:i * 512 + (qt + 1) * 128],
                                        wo_buf[oc % 2][:, i * 512:(i + 1) * 512],
                                        start=(i == 0), stop=(i == 15))
                                last.then_inc(S["bTO"], 1)
                        C["bTO"] = bTO0 + 16

                    @blk.scalar
                    def _(s):
                        s = Fenced(s, S["scF"], C, "scF")
                        for e in range(NE):
                            s.wait_ge(S["bS"], bS0 + e + 1)
                            if e >= 3:
                                s.wait_ge(S["bPV"], bPV0 + e - 2)
                                s.wait_ge(S["vF"], vAdd(e - 3))
                            s.activation(pT[e % 3][:], ps_s[e % 2][:], AF.Exp,
                                         bias=bias_shift[:], scale=SCALE_S)
                        for oc in range(4):
                            for qt in range(4):
                                j = 4 * oc + qt
                                s.wait_ge(S["bTO"], bTO0 + j + 1)
                                if j >= 2:
                                    R["bMS"].wait(s, gi * 16 + j - 2)
                                s.copy(midb[j % 2][:], ps_o[j % 2][:])
                        assert C["scF"] == sbB2 + NE + 16

                    @blk.vector
                    def _(v):
                        v = Fenced(v, S["vF"], C, "vF")
                        for e in range(NE):
                            h, pr = divmod(e, NPAIR)
                            v.wait_ge(S["scF"], bEX(e))
                            if pr == 0:
                                v.tensor_tensor(pTacc[:], pT[e % 3][:, 0:512],
                                                pT[e % 3][:, 512:1024], ALU.add)
                            else:
                                v.tensor_tensor(pTacc[:], pTacc[:],
                                                pT[e % 3][:, 0:512], ALU.add)
                                v.tensor_tensor(pTacc[:], pTacc[:],
                                                pT[e % 3][:, 512:1024], ALU.add)
                            if pr == NPAIR - 1:
                                v.wait_ge(S["bRS"], bRS0 + h + 1)
                                v.reciprocal(rbc[:], ps_row[:])
                                v.wait_ge(S["bPV"], bPV0 + (h + 1) * NPAIR)
                                v.tensor_tensor(ctx_fm[:, h * 512:(h + 1) * 512],
                                                ps_ctx[:], rbc[:], ALU.mult)
                        assert C["vF"] == vbB2 + HOPS * HEADS

                    @blk.sync
                    def _(s):
                        for oc in range(4):
                            for qt in range(4):
                                j = 4 * oc + qt
                                s.wait_ge(S["scF"], bOE(j))
                                R["bMS"].attach(s.dma_start(
                                    out=mid_dram[q0 + qt * 128:q0 + (qt + 1) * 128,
                                                 oc * 512:(oc + 1) * 512],
                                    in_=midb[j % 2][:]))
                        R["bMS"].wait_all(s)

        vb0 = C["vF"]
        sb0 = C["scF"]
        gb0 = C["gF"]

        def fB(t):
            return 0 if t == 0 else 4 + 8 * (t - 1)

        def vCent(t):
            return vb0 + fB(t) + 4

        def vQpre(t):
            return vb0 + (8 * t + 12 if t < NT_O - 1 else 8 * NT_O)

        def scLn(t):
            return sb0 + (1 if t == 0 else 4 * t)

        def sLN3(t):
            return sb0 + (3 if t == 0 else 4 * t + 2)

        def sCast(t):
            return sb0 + (4 * t + 7 if t < NT_O - 1 else 4 * NT_O)

        def gQtt(t):
            return gb0 + 2 * t + 1

        def gDeq(t):
            return gb0 + 2 * t + 2

        ffn_scope = ExitStack()
        Ef = ffn_scope.enter_context
        xq_fm = Ef(nc.sbuf_tensor("xq_fm", [128, 16 * S_own], BF16))
        with ExitStack() as p0:
            E0 = p0.enter_context
            midt = [E0(nc.sbuf_tensor(f"f0m{i}", [128, EMB], F32)) for i in range(3)]
            xt = [E0(nc.sbuf_tensor(f"f0x{i}", [128, EMB], F32)) for i in range(3)]
            pre = E0(nc.sbuf_tensor("f0pre", [128, EMB], F32))
            junk = E0(nc.sbuf_tensor("f0junk", [128, EMB], F32))
            cent = [E0(nc.sbuf_tensor(f"f0c{i}", [128, EMB], F32)) for i in range(2)]
            xmid = [E0(nc.sbuf_tensor(f"f0xm{i}", [128, EMB], F32)) for i in range(2)]
            vq = [E0(nc.sbuf_tensor(f"f0vq{i}", [128, EMB], F32)) for i in range(2)]
            q8 = [E0(nc.sbuf_tensor(f"f0q8{i}", [128, EMB], F8)) for i in range(2)]
            xqt = [E0(nc.sbuf_tensor(f"f0xq{i}", [128, EMB], BF16)) for i in range(2)]
            stat = E0(nc.sbuf_tensor("f0st", [128, 8], F32))
            red = [E0(nc.sbuf_tensor(f"f0rd{i}", [128, 64], F32)) for i in range(2)]
            s2 = [E0(nc.sbuf_tensor(f"f0s2{i}", [128, 64], F32)) for i in range(2)]
            rcp = [E0(nc.sbuf_tensor(f"f0rc{i}", [128, 64], F32)) for i in range(2)]

            with nc.Block() as blk:
                @blk.sync
                def _(s):
                    def emit_transp(t):
                        s.wait_ge(S["gF"], gDeq(t))
                        R["fXQT"].attach(s.dma_start(
                            out=xq_fm[:].rearrange("p (b s) -> p b s", b=16)
                                [:, :, t * 128:(t + 1) * 128],
                            in_=xqt[t % 2][:].rearrange("p (b f) -> p b f", b=16),
                            transpose=True))

                    for t in range(min(3, NT_O)):
                        R["fM"].attach(s.dma_start(out=midt[t % 3][:],
                                                   in_=mid_dram[t * 128:(t + 1) * 128, :]))
                        R["fX"].attach(s.dma_start(out=xt[t % 3][:],
                                                   in_=x_own[t * 128:(t + 1) * 128, :]))
                    for t in range(NT_O):
                        if t + 3 < NT_O:
                            s.wait_ge(S["vF"], vb0 + fB(t) + 1)
                            R["fM"].attach(s.dma_start(
                                out=midt[t % 3][:],
                                in_=mid_dram[(t + 3) * 128:(t + 4) * 128, :]))
                            R["fX"].attach(s.dma_start(
                                out=xt[t % 3][:],
                                in_=x_own[(t + 3) * 128:(t + 4) * 128, :]))
                        s.wait_ge(S["scF"], sLN3(t))
                        R["fXM"].attach(s.dma_start(out=xmid_dram[t * 128:(t + 1) * 128, :],
                                                    in_=xmid[t % 2][:]))
                        if t >= 1:
                            emit_transp(t - 1)
                    emit_transp(NT_O - 1)
                    for rn in ("fM", "fX", "fXM", "fXQT"):
                        R[rn].wait_all(s)

                @blk.vector
                def _(v):
                    v = Fenced(v, S["vF"], C, "vF")

                    def emit_prequant(t):
                        v.wait_ge(S["scF"], sLN3(t))
                        if t >= 2:
                            v.wait_ge(S["gF"], gDeq(t - 2))
                        x3 = xmid[t % 2][:].rearrange("p (b r) -> p b r", r=32)
                        v.tensor_reduce(red[t % 2][:], x3, axis=AX.X, op=ALU.max,
                                        apply_absolute_value=True)
                        v.tensor_scalar(s2[t % 2][:], red[t % 2][:], INV_224,
                                        None, ALU.mult)
                        v.tensor_scalar_max(s2[t % 2][:], s2[t % 2][:], QEPS2)
                        v.reciprocal(rcp[t % 2][:], s2[t % 2][:])

                    for t in range(NT_O):
                        R["fM"].wait(v, t)
                        R["fX"].wait(v, t)
                        v.scalar_tensor_tensor(pre[:], midt[t % 3][:], 1.0,
                                               xt[t % 3][:], ALU.mult, ALU.add,
                                               accum_out=stat[:, 0:1])
                        v.tensor_scalar(stat[:, 1:2], stat[:, 0:1],
                                        1.0 / EMB, None, ALU.mult)
                        if t >= 1:
                            v.wait_ge(S["scF"], scLn(t - 1))
                        v.scalar_tensor_tensor(junk[:], pre[:], stat[:, 1:2],
                                               pre[:], ALU.subtract, ALU.mult,
                                               accum_out=stat[:, 2:3])
                        v.tensor_scalar(cent[t % 2][:], pre[:], stat[:, 1:2], None,
                                        ALU.subtract)
                        if t >= 1:
                            emit_prequant(t - 1)
                    emit_prequant(NT_O - 1)
                    assert C["vF"] == vb0 + 8 * NT_O

                @blk.scalar
                def _(s):
                    s = Fenced(s, S["scF"], C, "scF")

                    def emit_cast(t):
                        s.wait_ge(S["gF"], gQtt(t))
                        if t >= 2:
                            s.wait_ge(S["gF"], gDeq(t - 2))
                        s.copy(q8[t % 2][:], vq[t % 2][:])

                    for t in range(NT_O):
                        s.wait_ge(S["vF"], vCent(t))
                        s.activation(stat[:, 3:4], stat[:, 2:3], AF.Ln,
                                     bias=bias_eps[:], scale=1.0 / EMB)
                        s.activation(stat[:, 4:5], stat[:, 3:4], AF.Exp, scale=-0.5)
                        if t >= 2:
                            R["fXM"].wait(s, t - 2)
                            s.wait_ge(S["gF"], gQtt(t - 2))
                            s.wait_ge(S["vF"], vQpre(t - 2))
                        s.activation(xmid[t % 2][:], cent[t % 2][:], AF.Copy,
                                     scale=stat[:, 4:5])
                        if t >= 1:
                            emit_cast(t - 1)
                    emit_cast(NT_O - 1)
                    assert C["scF"] == sb0 + 4 * NT_O

                @blk.gpsimd
                def _(g):
                    g = Fenced(g, S["gF"], C, "gF")
                    for t in range(NT_O):
                        g.wait_ge(S["vF"], vQpre(t))
                        g.wait_ge(S["scF"], sLN3(t))
                        if t >= 2:
                            g.wait_ge(S["scF"], sCast(t - 2))
                        x3 = xmid[t % 2][:].rearrange("p (b r) -> p b r", r=32)
                        v3 = vq[t % 2][:].rearrange("p (b r) -> p b r", r=32)
                        rb = rcp[t % 2][:].unsqueeze(2).broadcast_to([128, 64, 32])
                        g.tensor_tensor(v3, x3, rb, ALU.mult)
                        g.wait_ge(S["scF"], sCast(t))
                        if t >= 2:
                            R["fXQT"].wait(g, t - 2)
                        q3 = q8[t % 2][:].rearrange("p (b r) -> p b r", r=32)
                        o3 = xqt[t % 2][:].rearrange("p (b r) -> p b r", r=32)
                        s2b = s2[t % 2][:].unsqueeze(2).broadcast_to([128, 64, 32])
                        g.tensor_tensor(o3, q3, s2b, ALU.mult)
                    assert C["gF"] == gb0 + 2 * NT_O

        vb1 = C["vF"]
        sb1 = C["scF"]
        gb1 = C["gF"]

        def vTT(jc, m):
            if jc == 0:
                return vb1 + 4 * (m + 1)
            return vb1 + 12 * jc + 9 * max(0, jc - 2) + 4 * (m + 1)

        def _hS(m):
            if m <= NJC - 3:
                return vb1 + 21 * m + 36
            if m == NJC - 2:
                return vb1 + 21 * NJC - 18
            return vb1 + 21 * NJC - 9

        def vHmultA(m):
            return _hS(m) + 1

        def vHmultB(m):
            return _hS(m) + 2

        def vHpre(m):
            return _hS(m) + 7

        def vHdeq(m):
            return _hS(m) + 9

        def scC(jc, m):
            return sb1 + 3 * jc + 2 * max(0, jc - 1) + m + 1

        def scSiluA(m):
            return sb1 + (5 * m + 7 if m < NJC - 1 else 5 * NJC - 1)

        def scSiluB(m):
            return sb1 + (5 * m + 8 if m < NJC - 1 else 5 * NJC)

        def gTT(jc, m):
            return gb1 + (m + 1 if jc == 0 else 6 * jc + m + 1)

        def gDeq(jc, m):
            return gb1 + 6 * jc + m + 4

        with ExitStack() as p1:
            E1 = p1.enter_context
            gw = [E1(nc.sbuf_tensor(f"f1g{i}", [128, EMB], F32)) for i in range(2)]
            uw = [E1(nc.sbuf_tensor(f"f1u{i}", [128, EMB], F32)) for i in range(2)]
            dw = [E1(nc.sbuf_tensor(f"f1d{i}", [128, EMB], F32)) for i in range(2)]
            gq = [E1(nc.sbuf_tensor(f"f1gq{i}", [128, EMB], BF16)) for i in range(2)]
            uq = [E1(nc.sbuf_tensor(f"f1uq{i}", [128, EMB], BF16)) for i in range(2)]
            dq = [E1(nc.sbuf_tensor(f"f1dq{i}", [128, EMB], BF16)) for i in range(2)]
            guf = [E1(nc.sbuf_tensor(f"f1guf{i}", [128, 16 * 256], BF16)) for i in range(2)]
            dqf = [E1(nc.sbuf_tensor(f"f1dqf{i}", [128, EMB], BF16)) for i in range(2)]
            sil = E1(nc.sbuf_tensor("f1sil", [128, NTT * 128], F32))
            hf = E1(nc.sbuf_tensor("f1hf", [128, NTT * 128], F32))
            hqt = [E1(nc.sbuf_tensor(f"f1hqt{i}", [128, NTT * 128], BF16)) for i in range(2)]
            hqf = [E1(nc.sbuf_tensor(f"f1hqf{i}", [128, S_own], BF16)) for i in range(2)]
            Q8 = {m: E1(nc.sbuf_tensor(f"f1q8_{m}", [128, EMB], F8)) for m in "gudh"}
            NSB = {"g": 2, "u": 2, "d": 2, "h": 1}
            RD = {m: [E1(nc.sbuf_tensor(f"f1rd_{m}{i}", [128, 64], F32))
                      for i in range(NSB[m])] for m in "gudh"}
            RC = {m: [E1(nc.sbuf_tensor(f"f1rc_{m}{i}", [128, 64], F32))
                      for i in range(NSB[m])] for m in "gudh"}
            S2 = {m: [E1(nc.sbuf_tensor(f"f1s2_{m}{i}", [128, 64], F32))
                      for i in range(NSB[m])] for m in "gudh"}
            ps_gu = [E1(nc.psum_tensor(f"f1pgu{i}", [128, NTT * 128], F32))
                     for i in range(2)]
            HTT = NTT // 2

            def qreduce(eng, src, nb, m, bi):
                s3 = src.rearrange("p (b r) -> p b r", r=32)
                eng.tensor_reduce(RD[m][bi][:, 0:nb], s3, axis=AX.X, op=ALU.max,
                                  apply_absolute_value=True)

            def qsmalls(v, nb, m, bi):
                v.tensor_scalar(S2[m][bi][:, 0:nb], RD[m][bi][:, 0:nb], INV_224,
                                None, ALU.mult)
                v.tensor_scalar_max(S2[m][bi][:, 0:nb], S2[m][bi][:, 0:nb], QEPS2)
                v.reciprocal(RC[m][bi][:, 0:nb], S2[m][bi][:, 0:nb])

            def qtt(eng, src, nb, m, bi):
                s3 = src.rearrange("p (b r) -> p b r", r=32)
                rb = RC[m][bi][:, 0:nb].unsqueeze(2).broadcast_to([128, nb, 32])
                eng.tensor_tensor(s3, s3, rb, ALU.mult)

            def qpre(v, src, nb, m, bi):
                qreduce(v, src, nb, m, bi)
                qsmalls(v, nb, m, bi)
                qtt(v, src, nb, m, bi)

            def deq(eng, out_bf, nb, m, bi):
                s2b = S2[m][bi][:, 0:nb].unsqueeze(2).broadcast_to([128, nb, 32])
                q3 = Q8[m][:, 0:nb * 32].rearrange("p (b r) -> p b r", r=32)
                o3 = out_bf.rearrange("p (b r) -> p b r", r=32)
                eng.tensor_tensor(o3, q3, s2b, ALU.mult)

            with nc.Block() as blk:
                @blk.sync
                def _(s):
                    def emit_loads(jc):
                        if jc >= 2:
                            for mi in range(3):
                                s.wait_ge(S["scF"], scC(jc - 2, mi))
                        R["gG"].attach(s.dma_start(out=gw[jc % 2][:],
                                                   in_=gate_w[jc * 128:(jc + 1) * 128, :]))
                        R["gU"].attach(s.dma_start(out=uw[jc % 2][:],
                                                   in_=up_w[jc * 128:(jc + 1) * 128, :]))
                        R["gD"].attach(s.dma_start(
                            out=dw[jc % 2][:].rearrange("p (o j) -> p o j", o=16),
                            in_=down_wr[jc].rearrange("(o p) j -> p o j", p=128)))

                    def emit_hsync(m):
                        s.wait_ge(S["vF"], vHdeq(m))
                        if m >= 2:
                            R["gHS"].wait(s, m - 2)
                        R["gHT"].attach(s.dma_start(
                            out=hqf[m % 2][:].rearrange("p (b s) -> p b s", b=NTT),
                            in_=hqt[m % 2][:].rearrange("p (b f) -> p b f", b=NTT),
                            transpose=True))
                        R["gHT"].wait(s, m)
                        R["gHS"].attach(s.dma_start(out=hq_dram[m], in_=hqf[m % 2][:]))

                    emit_loads(0)
                    emit_loads(1)
                    for jc in range(NJC):
                        s.wait_ge(S["gF"], gDeq(jc, 0))
                        if jc >= 2:
                            s.wait_ge(S["gMM"], 2 * jc - 2)
                        R["gTG"].attach(s.dma_start(
                            out=guf[jc % 2][:].rearrange("p (i c) -> p i c", i=16)[:, :, 0:128],
                            in_=gq[jc % 2][:].rearrange("p (i c) -> p i c", i=16),
                            transpose=True))
                        s.wait_ge(S["gF"], gDeq(jc, 1))
                        R["gTU"].attach(s.dma_start(
                            out=guf[jc % 2][:].rearrange("p (i c) -> p i c", i=16)[:, :, 128:256],
                            in_=uq[jc % 2][:].rearrange("p (i c) -> p i c", i=16),
                            transpose=True))
                        s.wait_ge(S["gF"], gDeq(jc, 2))
                        if jc >= 2:
                            R["gDS"].wait(s, jc - 2)
                        R["gTD"].attach(s.dma_start(
                            out=dqf[jc % 2][:].rearrange("p (i c) -> p i c", i=16),
                            in_=dq[jc % 2][:].rearrange("p (i c) -> p i c", i=16),
                            transpose=True))
                        R["gTD"].wait(s, jc)
                        R["gDS"].attach(s.dma_start(out=dq_dram[jc], in_=dqf[jc % 2][:]))
                        if jc + 2 < NJC:
                            emit_loads(jc + 2)
                        if jc >= 2:
                            emit_hsync(jc - 2)
                    emit_hsync(NJC - 2)
                    emit_hsync(NJC - 1)
                    for rn in ("gG", "gU", "gD", "gTG", "gTU", "gTD", "gDS", "gHT", "gHS"):
                        R[rn].wait_all(s)

                @blk.vector
                def _(v):
                    v = Fenced(v, S["vF"], C, "vF")

                    def emit_hmult(m, hh):
                        v.wait_ge(S["scF"], (scSiluA if hh == 0 else scSiluB)(m))
                        pg = ps_gu[hh][:].rearrange("p (t c) -> p t c", t=HTT)
                        hs = hf[:, hh * HTT * 128:(hh + 1) * HTT * 128]
                        ss = sil[:, hh * HTT * 128:(hh + 1) * HTT * 128]
                        v.tensor_tensor(
                            hs.rearrange("p (t c) -> p t c", t=HTT),
                            ss.rearrange("p (t c) -> p t c", t=HTT),
                            pg[:, :, 128:256], ALU.mult)

                    def emit_hquant(m):
                        qpre(v, hf[:], NTT * 4, "h", 0)
                        v.tensor_copy(Q8["h"][:, 0:NTT * 128], hf[:])
                        if m >= 2:
                            R["gHT"].wait(v, m - 2)
                        deq(v, hqt[m % 2][:], NTT * 4, "h", 0)

                    def emit_qpre(v, jc, mi, m):
                        srcb = {"g": gw, "u": uw, "d": dw}[m][jc % 2]
                        R[{"g": "gG", "u": "gU", "d": "gD"}[m]].wait(v, jc)
                        if jc >= 2:
                            v.wait_ge(S["gF"], gDeq(jc - 2, mi))
                        qreduce(v, srcb[:], 64, m, jc % 2)
                        qsmalls(v, 64, m, jc % 2)

                    for mi, m in enumerate("gud"):
                        emit_qpre(v, 0, mi, m)
                    for jc in range(NJC):
                        if jc + 1 < NJC:
                            for mi, m in enumerate("gud"):
                                emit_qpre(v, jc + 1, mi, m)
                        if jc >= 1:
                            emit_hmult(jc - 1, 0)
                            emit_hmult(jc - 1, 1)
                            emit_hquant(jc - 1)
                    emit_hmult(NJC - 1, 0)
                    emit_hmult(NJC - 1, 1)
                    emit_hquant(NJC - 1)
                    assert C["vF"] == vb1 + 21 * NJC

                @blk.scalar
                def _(s):
                    s = Fenced(s, S["scF"], C, "scF")

                    def emit_silu(m, hh):
                        s.wait_ge(S["gMM"], 2 * m + hh + 1)
                        pg = ps_gu[hh][:].rearrange("p (t c) -> p t c", t=HTT)
                        ss = sil[:, hh * HTT * 128:(hh + 1) * HTT * 128]
                        s.activation(ss.rearrange("p (t c) -> p t c", t=HTT),
                                     pg[:, :, 0:128], AF.Silu)

                    for jc in range(NJC):
                        for mi, (m, srcr) in enumerate(zip("gud", (gw, uw, dw))):
                            s.wait_ge(S["gF"], gTT(jc, mi))
                            if jc >= 1:
                                s.wait_ge(S["gF"], gDeq(jc - 1, mi))
                            s.copy(Q8[m][:], srcr[jc % 2][:])
                        if jc >= 1:
                            emit_silu(jc - 1, 0)
                            emit_silu(jc - 1, 1)
                    emit_silu(NJC - 1, 0)
                    emit_silu(NJC - 1, 1)
                    assert C["scF"] == sb1 + 5 * NJC

                @blk.gpsimd
                def _(g):
                    g = Fenced(g, S["gF"], C, "gF")
                    for mi, (m, srcr) in enumerate(zip("gud", (gw, uw, dw))):
                        g.wait_ge(S["vF"], vTT(0, mi))
                        qtt(g, srcr[0][:], 64, m, 0)
                    for jc in range(NJC):
                        for mi, (m, outr) in enumerate(zip("gud", (gq, uq, dq))):
                            g.wait_ge(S["scF"], scC(jc, mi))
                            if jc >= 2:
                                R[{"g": "gTG", "u": "gTU", "d": "gTD"}[m]].wait(g, jc - 2)
                            deq(g, outr[jc % 2][:], 64, m, jc % 2)
                        if jc + 1 < NJC:
                            for mi, (m, srcr) in enumerate(zip("gud", (gw, uw, dw))):
                                g.wait_ge(S["vF"], vTT(jc + 1, mi))
                                qtt(g, srcr[(jc + 1) % 2][:], 64, m, (jc + 1) % 2)
                    assert C["gF"] == gb1 + 6 * NJC

                @blk.tensor
                def _(te):
                    R["fXQT"].wait_all(te)
                    for jc in range(NJC):
                        R["gTG"].wait(te, jc)
                        R["gTU"].wait(te, jc)
                        for hh in range(2):
                            if jc >= 1:
                                te.wait_ge(S["vF"],
                                           (vHmultA if hh == 0 else vHmultB)(jc - 1))
                                te.wait_ge(S["scF"],
                                           (scSiluA if hh == 0 else scSiluB)(jc - 1))
                            ins = None
                            for ttl in range(HTT):
                                tt = hh * HTT + ttl
                                for i in range(16):
                                    st, sp = (i == 0), (i == 15)
                                    ins = te.matmul(
                                        ps_gu[hh][:, ttl * 256:(ttl + 1) * 256],
                                        xq_fm[:, i * S_own + tt * 128:i * S_own + (tt + 1) * 128],
                                        guf[jc % 2][:, i * 256:(i + 1) * 256],
                                        start=st, stop=sp)
                            ins.then_inc(S["gMM"], 1)
                    C["gMM"] = 2 * NJC

        NB2 = NJC // 8
        F2_V = 8 + 16
        vb2 = C["vF"]
        sb2 = C["scF"]

        def hEV(it):
            tg, oh = divmod(it, 2)
            return vb2 + F2_V * tg + 4 * (oh + 1)

        def hLN2(u):
            tg, tt = divmod(u, 4)
            return vb2 + F2_V * tg + 8 + 4 * (tt + 1)

        def hSC(u):
            return sb2 + 3 * (u + 1)

        ffn_scope.close()
        with ExitStack() as p2:
            E2 = p2.enter_context
            hqb = [E2(nc.sbuf_tensor(f"f2h{i}", [128, 8 * 512], BF16)) for i in range(3)]
            dqb = [E2(nc.sbuf_tensor(f"f2d{i}", [128, 8 * 1024], BF16)) for i in range(3)]
            mid2 = E2(nc.sbuf_tensor("f2mid", [128, 4 * EMB], F32))
            xm2 = E2(nc.sbuf_tensor("f2xm", [128, 4 * EMB], F32))
            junk2 = E2(nc.sbuf_tensor("f2junk", [128, EMB], F32))
            cent2 = E2(nc.sbuf_tensor("f2cent", [128, EMB], F32))
            outb = [E2(nc.sbuf_tensor(f"f2ob{i}", [128, EMB], F32)) for i in range(2)]
            stat2 = E2(nc.sbuf_tensor("f2st", [128, 32], F32))
            ps_f2 = [E2(nc.psum_tensor(f"f2p{i}", [128, 1024], F32)) for i in range(4)]

            NIT = NTG * 2
            with nc.Block() as blk:
                @blk.sync
                def _(s):
                    for tg in range(NTG):
                        for oh in range(2):
                            it = tg * 2 + oh
                            for bi in range(NB2):
                                bu = it * NB2 + bi
                                if bu >= 3:
                                    s.wait_ge(S["hMM"], (bu - 2) * 8)
                                R["hH"].attach(s.dma_start(
                                    out=hqb[bu % 3][:].rearrange("p (j t) -> p j t", j=8),
                                    in_=hq_dram[bi * 8:(bi + 1) * 8]
                                        [:, :, tg * 512:(tg + 1) * 512]
                                        .rearrange("j p t -> p j t")))
                                R["hD"].attach(s.dma_start(
                                    out=dqb[bu % 3][:].rearrange("p (j o) -> p j o", j=8),
                                    in_=dq_dram[bi * 8:(bi + 1) * 8]
                                        [:, :, oh * 1024:(oh + 1) * 1024]
                                        .rearrange("j p o -> p j o")))
                            if oh == 0:
                                if tg >= 1:
                                    s.wait_ge(S["vF"], hEV(2 * tg - 1))
                                for tt in range(4):
                                    R["hXM"].attach(s.dma_start(
                                        out=xm2[:, tt * EMB:(tt + 1) * EMB],
                                        in_=xmid_dram[tg * 512 + tt * 128:
                                                      tg * 512 + (tt + 1) * 128, :]))
                    for rn in ("hH", "hD", "hXM"):
                        R[rn].wait_all(s)
                    R["hOUT"].wait_all(s)

                @blk.tensor
                def _(te):
                    for tg in range(NTG):
                        for oh in range(2):
                            it = tg * 2 + oh
                            if it >= 1:
                                te.wait_ge(S["vF"], hEV(it - 1))
                            for bi in range(NB2):
                                bu = it * NB2 + bi
                                R["hH"].wait(te, bu)
                                R["hD"].wait(te, bu)
                                for j in range(8):
                                    jc = bi * 8 + j
                                    st, sp = (jc == 0), (jc == NJC - 1)
                                    last = None
                                    for tt in range(4):
                                        te.matmul(ps_f2[tt][:, 0:512],
                                                  hqb[bu % 3][:, (j * 4 + tt) * 128:
                                                              (j * 4 + tt + 1) * 128],
                                                  dqb[bu % 3][:, j * 1024:j * 1024 + 512],
                                                  start=st, stop=sp)
                                        last = te.matmul(
                                            ps_f2[tt][:, 512:1024],
                                            hqb[bu % 3][:, (j * 4 + tt) * 128:
                                                        (j * 4 + tt + 1) * 128],
                                            dqb[bu % 3][:, j * 1024 + 512:(j + 1) * 1024],
                                            start=st, stop=sp)
                                    last.then_inc(S["hMM"], 1)
                    C["hMM"] = NIT * NJC

                @blk.vector
                def _(v):
                    v = Fenced(v, S["vF"], C, "vF")
                    for tg in range(NTG):
                        for tt in range(4):
                            R["hXM"].wait(v, tg * 4 + tt)
                        for oh in range(2):
                            it = tg * 2 + oh
                            v.wait_ge(S["hMM"], (it + 1) * NJC)
                            for tt in range(4):
                                v.scalar_tensor_tensor(
                                    mid2[:, tt * EMB + oh * 1024:tt * EMB + (oh + 1) * 1024],
                                    ps_f2[tt][:], 1.0,
                                    xm2[:, tt * EMB + oh * 1024:tt * EMB + (oh + 1) * 1024],
                                    ALU.mult, ALU.add,
                                    accum_out=stat2[:, tt * 2 + oh:tt * 2 + oh + 1])
                        for tt in range(4):
                            base = tt * 2
                            v.tensor_tensor(stat2[:, 16 + tt:17 + tt],
                                            stat2[:, base:base + 1],
                                            stat2[:, base + 1:base + 2], ALU.add)
                            v.tensor_scalar(stat2[:, 25:26], stat2[:, 16 + tt:17 + tt],
                                            1.0 / EMB, None, ALU.mult)
                            m2 = mid2[:, tt * EMB:(tt + 1) * EMB]
                            v.scalar_tensor_tensor(junk2[:], m2, stat2[:, 25:26],
                                                   m2, ALU.subtract, ALU.mult,
                                                   accum_out=stat2[:, 26:27])
                            v.tensor_scalar(cent2[:], m2, stat2[:, 25:26], None,
                                            ALU.subtract)
                            v.wait_ge(S["scF"], hSC(tg * 4 + tt))
                    assert C["vF"] == vb2 + F2_V * NTG

                @blk.scalar
                def _(s):
                    raw = s
                    s = Fenced(s, S["scF"], C, "scF")
                    for tg in range(NTG):
                        for tt in range(4):
                            u = tg * 4 + tt
                            s.wait_ge(S["vF"], hLN2(u))
                            s.activation(stat2[:, 27:28], stat2[:, 26:27], AF.Ln,
                                         bias=bias_eps[:], scale=1.0 / EMB)
                            s.activation(stat2[:, 28:29], stat2[:, 27:28], AF.Exp,
                                         scale=-0.5)
                            if u >= 2:
                                R["hOUT"].wait(s, u - 2)
                            s.activation(outb[tt % 2][:], cent2[:], AF.Copy,
                                         scale=stat2[:, 28:29])
                            R["hOUT"].attach(raw.dma_start(
                                out=out_own[tg * 512 + tt * 128:
                                            tg * 512 + (tt + 1) * 128, :],
                                in_=outb[tt % 2][:]))
                    assert C["scF"] == sb2 + 3 * NTG * 4

    return nc


def kernel(L0_attn_x, L0_attn_W_Q, L0_attn_W_K, L0_attn_W_V, L0_attn_W_O,
           L0_ffn_gate_w, L0_ffn_up_w, L0_ffn_down_w):
    S_full, S_own = 4096, 2048
    x = np.ascontiguousarray(np.asarray(L0_attn_x, dtype=np.float32))
    WQ = np.asarray(L0_attn_W_Q, dtype=np.float32)
    WK = np.asarray(L0_attn_W_K, dtype=np.float32)
    WV = np.asarray(L0_attn_W_V, dtype=np.float32)
    WO = np.asarray(L0_attn_W_O, dtype=np.float32)
    gwv = np.ascontiguousarray(np.asarray(L0_ffn_gate_w, dtype=np.float32))
    uwv = np.ascontiguousarray(np.asarray(L0_ffn_up_w, dtype=np.float32))
    dwv = np.asarray(L0_ffn_down_w, dtype=np.float32)

    perm_q = (np.arange(128)[None, :] * 16 + np.arange(16)[:, None]).reshape(-1)
    perm_k = (np.arange(128)[None, :] * 8 + np.arange(8)[:, None]).reshape(-1)
    wqt = np.ascontiguousarray(WQ[perm_q, :].T)
    wkt = np.ascontiguousarray(WK[perm_k, :].T)
    wvt = np.ascontiguousarray(WV[perm_k, :].T)
    wot = np.ascontiguousarray(WO.T)
    down_wr = np.ascontiguousarray(dwv.reshape(EMB, NJC, 128).transpose(1, 0, 2))

    half = 64
    inv_freq = (1.0 / 10000.0 ** (np.arange(0, half, dtype=np.float32) * 2.0 / 128.0)
                ).astype(np.float32)
    pos = np.arange(S_full, dtype=np.float32)
    ang = pos[:, None] * inv_freq[None, :]
    cos_t = np.ascontiguousarray(
        np.concatenate([np.cos(ang), np.cos(ang)], axis=1).astype(np.float32))
    sin_t = np.ascontiguousarray(
        np.concatenate([np.sin(ang), np.sin(ang)], axis=1).astype(np.float32))

    nc = build_nc(S_full, S_own)
    in_maps = []
    for c in range(8):
        b, hh = c // 2, c % 2
        in_maps.append({
            "x_seq": np.ascontiguousarray(x[b]),
            "x_own": np.ascontiguousarray(x[b, hh * S_own:(hh + 1) * S_own]),
            "wqt": wqt, "wkt": wkt, "wvt": wvt, "wot": wot,
            "gate_w": gwv, "up_w": uwv, "down_wr": down_wr,
            "cos_t": cos_t, "sin_t": sin_t,
            "cos_o": np.ascontiguousarray(cos_t[hh * S_own:(hh + 1) * S_own]),
            "sin_o": np.ascontiguousarray(sin_t[hh * S_own:(hh + 1) * S_own]),
        })
    res = run_bass_kernel_spmd(nc, in_maps, core_ids=list(range(8)))
    out = np.empty((4, S_full, EMB), dtype=np.float32)
    for c in range(8):
        b, hh = c // 2, c % 2
        out[b, hh * S_own:(hh + 1) * S_own] = res.results[c]["out_own"]
    return out


# version 18
# speedup vs baseline: 1.0592x; 1.0592x over previous
import numpy as np
from contextlib import ExitStack

import concourse.bass as bass
import concourse.mybir as mybir
from concourse.bass_utils import run_bass_kernel_spmd

F32 = mybir.dt.float32
BF16 = mybir.dt.bfloat16
F8 = mybir.dt.float8e4
AF = mybir.ActivationFunctionType
ALU = mybir.AluOpType
AX = mybir.AxisListType

EMB = 2048
HEADS = 16
KVH = 8
D = 128
IFF = 8192
NJC = IFF // 128
SCALE_S = float(1.0 / np.sqrt(np.float32(2048.0)))
EXP_SHIFT = -5.0
LN_EPS = 1e-5
INV_224 = float(np.float32(1.0) / np.float32(224.0))
QEPS2 = 2e-30


class Ring:

    def __init__(self, mksem, name, slots):
        self.sems = [mksem(name + str(i)) for i in range(slots)]
        self.slots = slots
        self.issued = 0

    def attach(self, instr):
        instr.then_inc(self.sems[self.issued % self.slots], 16)
        self.issued += 1
        return instr

    def wait(self, eng, i):
        eng.wait_ge(self.sems[i % self.slots], 16 * (i // self.slots + 1))

    def wait_all(self, eng):
        n = self.issued
        for si in range(self.slots):
            c = (n - si + self.slots - 1) // self.slots
            if c > 0:
                eng.wait_ge(self.sems[si], 16 * c)


class Fenced:

    def __init__(self, eng, sem, cnt, key):
        self._eng = eng
        self._sem = sem
        self._cnt = cnt
        self._key = key

    def _f(self, ins):
        self._cnt[self._key] += 1
        ins.then_inc(self._sem, 1)
        self._eng.wait_ge(self._sem, self._cnt[self._key])
        return ins

    def wait_ge(self, sem, val):
        self._eng.wait_ge(sem, val)

    def __getattr__(self, name):
        fn = getattr(self._eng, name)

        def call(*a, **k):
            return self._f(fn(*a, **k))
        return call


def _rope_tm(v, ps, cs, sn, out, temps, nh):
    kk = ps[:].rearrange("p (h d) -> p h d", h=nh)
    ka, kb = kk[:, :, 0:64], kk[:, :, 64:128]
    csA = cs[:, 0:64].unsqueeze(1).broadcast_to([128, nh, 64])
    csB = cs[:, 64:128].unsqueeze(1).broadcast_to([128, nh, 64])
    snA = sn[:, 0:64].unsqueeze(1).broadcast_to([128, nh, 64])
    snB = sn[:, 64:128].unsqueeze(1).broadcast_to([128, nh, 64])
    r0 = temps[0][:].rearrange("p (h d) -> p h d", h=nh)
    r1 = temps[1][:].rearrange("p (h d) -> p h d", h=nh)
    ko = out.rearrange("p (h d) -> p h d", h=nh)
    v.tensor_tensor(r0, ka, csA, ALU.mult)
    v.tensor_tensor(r1, kb, snA, ALU.mult)
    v.tensor_tensor(ko[:, :, 0:64], r0, r1, ALU.subtract)
    v.tensor_tensor(r0, kb, csB, ALU.mult)
    v.tensor_tensor(r1, ka, snB, ALU.mult)
    v.tensor_tensor(ko[:, :, 64:128], r0, r1, ALU.add)


ROPE_OPS = 6


def _quant(v, src_f32, nb, tmp_v, tmp_q8, red, s2, rcp, out_bf):
    s3 = src_f32.rearrange("p (b r) -> p b r", r=32)
    v.tensor_reduce(red[:, 0:nb], s3, axis=AX.X, op=ALU.max,
                    apply_absolute_value=True)
    v.tensor_scalar(s2[:, 0:nb], red[:, 0:nb], INV_224, None, ALU.mult)
    v.tensor_scalar_max(s2[:, 0:nb], s2[:, 0:nb], QEPS2)
    v.reciprocal(rcp[:, 0:nb], s2[:, 0:nb])
    rb = rcp[:, 0:nb].unsqueeze(2).broadcast_to([128, nb, 32])
    s2b = s2[:, 0:nb].unsqueeze(2).broadcast_to([128, nb, 32])
    t3 = tmp_v.rearrange("p (b r) -> p b r", r=32)
    q3 = tmp_q8.rearrange("p (b r) -> p b r", r=32)
    v.tensor_tensor(t3, s3, rb, ALU.mult)
    v.tensor_copy(tmp_q8, tmp_v)
    o3 = out_bf.rearrange("p (b r) -> p b r", r=32)
    v.tensor_tensor(o3, q3, s2b, ALU.mult)


QUANT_OPS = 7


def build_nc(S_full=4096, S_own=2048):
    assert S_full % 256 == 0 and S_own % 512 == 0
    NT_F = S_full // 128
    NT_O = S_own // 128
    NG = S_own // 512
    NKC = S_full // 128
    NPAIR = NKC // 2
    NTG = S_own // 512
    NTT = NT_O
    NE = HEADS * NPAIR

    nc = bass.Bass("TRN2", target_bir_lowering=False)

    x_seq = nc.dram_tensor("x_seq", [S_full, EMB], F32, kind="ExternalInput").ap()
    x_own = nc.dram_tensor("x_own", [S_own, EMB], F32, kind="ExternalInput").ap()
    wqt = nc.dram_tensor("wqt", [EMB, EMB], F32, kind="ExternalInput").ap()
    wkt = nc.dram_tensor("wkt", [EMB, KVH * D], F32, kind="ExternalInput").ap()
    wvt = nc.dram_tensor("wvt", [EMB, KVH * D], F32, kind="ExternalInput").ap()
    wot = nc.dram_tensor("wot", [EMB, EMB], F32, kind="ExternalInput").ap()
    gate_w = nc.dram_tensor("gate_w", [IFF, EMB], F32, kind="ExternalInput").ap()
    up_w = nc.dram_tensor("up_w", [IFF, EMB], F32, kind="ExternalInput").ap()
    down_wr = nc.dram_tensor("down_wr", [NJC, EMB, 128], F32, kind="ExternalInput").ap()
    cos_t = nc.dram_tensor("cos_t", [S_full, D], F32, kind="ExternalInput").ap()
    sin_t = nc.dram_tensor("sin_t", [S_full, D], F32, kind="ExternalInput").ap()
    cos_o = nc.dram_tensor("cos_o", [S_own, D], F32, kind="ExternalInput").ap()
    sin_o = nc.dram_tensor("sin_o", [S_own, D], F32, kind="ExternalInput").ap()
    out_own = nc.dram_tensor("out_own", [S_own, EMB], F32, kind="ExternalOutput").ap()
    mid_dram = nc.dram_tensor("mid_dram", [S_own, EMB], F32).ap()
    wq_bf_dram = nc.dram_tensor("wq_bf_dram", [EMB, EMB], BF16).ap()
    xmid_dram = nc.dram_tensor("xmid_dram", [S_own, EMB], F32).ap()
    hq_dram = nc.dram_tensor("hq_dram", [NJC, 128, S_own], BF16).ap()
    dq_dram = nc.dram_tensor("dq_dram", [NJC, 128, EMB], BF16).ap()

    with ExitStack() as top:
        E = top.enter_context
        mksem = lambda n: E(nc.semaphore(n))
        ones_bf = E(nc.sbuf_tensor("ones_bf", [128, 128], BF16))
        bias_shift = E(nc.sbuf_tensor("bias_shift", [128, 1], F32))
        bias_eps = E(nc.sbuf_tensor("bias_eps", [128, 1], F32))

        eng_names = ["akMM", "avMM", "bWQC", "bS", "bPV", "bTO",
                     "gMM", "hMM", "vF", "scF", "gF"]
        S = {n: mksem(n) for n in eng_names}
        C = {n: 0 for n in eng_names}

        R = {}
        for name, slots in [
            ("akW", 1), ("akX", 2), ("akTX", 2), ("akCS", 2), ("akSN", 2), ("akKT", 2),
            ("akWQ", 2), ("akWS", 2),
            ("avW", 1), ("avX", 2), ("avTX", 2),
            ("bX", 2), ("bTX", 2), ("bCS", 2), ("bSN", 2), ("bWQ", 2), ("bQT", 2),
            ("bWO", 2), ("bMS", 2),
            ("fM", 3), ("fX", 3), ("fXM", 2), ("fXQT", 2),
            ("gG", 2), ("gU", 2), ("gD", 2), ("gTG", 2), ("gTU", 2), ("gTD", 2),
            ("gDS", 2), ("gHT", 2), ("gHS", 2),
            ("hH", 3), ("hD", 3), ("hXM", 4), ("hOUT", 2),
        ]:
            R[name] = Ring(mksem, name, slots)

        with nc.Block() as blk:
            @blk.gpsimd
            def _(g):
                g.memset(ones_bf[:], 1.0)
                g.memset(bias_shift[:], EXP_SHIFT)
                g.memset(bias_eps[:], LN_EPS)

        with ExitStack() as att_scope:
            Ea = att_scope.enter_context
            k_fm = Ea(nc.sbuf_tensor("k_fm", [128, KVH * S_full], BF16))
            v_tm = Ea(nc.sbuf_tensor("v_tm", [128, NKC * KVH * D], BF16))

            vbAK = C["vF"]

            def akRP(t):
                return vbAK + ROPE_OPS * (t + 1)

            with ExitStack() as pa:
                Ek = pa.enter_context
                wkt_sb = Ek(nc.sbuf_tensor("wkt_sb", [128, 16 * KVH * D], BF16))
                xb = [Ek(nc.sbuf_tensor(f"akxb{i}", [128, EMB], BF16)) for i in range(2)]
                xfm = [Ek(nc.sbuf_tensor(f"akxf{i}", [128, EMB], BF16)) for i in range(2)]
                krot = [Ek(nc.sbuf_tensor(f"akkr{i}", [128, KVH * D], BF16)) for i in range(2)]
                cs_sb = [Ek(nc.sbuf_tensor(f"akcs{i}", [128, D], F32)) for i in range(2)]
                sn_sb = [Ek(nc.sbuf_tensor(f"aksn{i}", [128, D], F32)) for i in range(2)]
                rt = [Ek(nc.sbuf_tensor(f"akrt{i}", [128, KVH * 64], F32)) for i in range(2)]
                wqstg = [Ek(nc.sbuf_tensor(f"akwq{i}", [128, EMB], BF16)) for i in range(2)]
                ps_k = [Ek(nc.psum_tensor(f"ps_k{i}", [128, KVH * D], F32)) for i in range(2)]

                with nc.Block() as blk:
                    @blk.gpsimd
                    def _(g):
                        R["akW"].attach(
                            g.dma_start(out=wkt_sb[:].rearrange("p (i o) -> p i o", i=16),
                                        in_=wkt.rearrange("(i p) o -> p i o", p=128)))
                        for t in range(NT_F):
                            if t >= 2:
                                R["akTX"].wait(g, t - 2)
                            R["akX"].attach(
                                g.dma_start(out=xb[t % 2][:],
                                            in_=x_seq[t * 128:(t + 1) * 128, :]))
                        for i in range(16):
                            if i >= 2:
                                R["akWS"].wait(g, i - 2)
                            R["akWQ"].attach(
                                g.dma_start(out=wqstg[i % 2][:],
                                            in_=wqt[i * 128:(i + 1) * 128, :]))
                        R["akX"].wait_all(g)
                        R["akW"].wait_all(g)

                    @blk.sync
                    def _(s):
                        for t in range(NT_F + 1):
                            if t < NT_F:
                                R["akX"].wait(s, t)
                                if t >= 2:
                                    s.wait_ge(S["akMM"], t - 1)
                                R["akTX"].attach(s.dma_start(
                                    out=xfm[t % 2][:].rearrange("p (b f) -> p b f", b=16),
                                    in_=xb[t % 2][:].rearrange("p (b f) -> p b f", b=16),
                                    transpose=True))
                                if t >= 2:
                                    s.wait_ge(S["vF"], akRP(t - 2))
                                R["akCS"].attach(
                                    s.dma_start(out=cs_sb[t % 2][:],
                                                in_=cos_t[t * 128:(t + 1) * 128, :]))
                                R["akSN"].attach(
                                    s.dma_start(out=sn_sb[t % 2][:],
                                                in_=sin_t[t * 128:(t + 1) * 128, :]))
                            if t >= 1:
                                tk = t - 1
                                s.wait_ge(S["vF"], akRP(tk))
                                R["akKT"].attach(s.dma_start(
                                    out=k_fm[:].rearrange("p (h s) -> p h s", h=KVH)
                                        [:, :, tk * 128:(tk + 1) * 128],
                                    in_=krot[tk % 2][:].rearrange("p (h f) -> p h f", h=KVH),
                                    transpose=True))
                        for i in range(16):
                            R["akWQ"].wait(s, i)
                            R["akWS"].attach(s.dma_start(
                                out=wq_bf_dram[i * 128:(i + 1) * 128, :],
                                in_=wqstg[i % 2][:]))
                        for rn in ("akTX", "akCS", "akSN", "akKT", "akWQ", "akWS"):
                            R[rn].wait_all(s)

                    @blk.tensor
                    def _(te):
                        R["akW"].wait(te, 0)
                        for t in range(NT_F):
                            R["akTX"].wait(te, t)
                            if t >= 2:
                                te.wait_ge(S["vF"], akRP(t - 2))
                            for i in range(16):
                                st, sp = (i == 0), (i == 15)
                                lhs = xfm[t % 2][:, i * 128:(i + 1) * 128]
                                te.matmul(ps_k[t % 2][:, 0:512], lhs,
                                          wkt_sb[:, i * 1024:i * 1024 + 512],
                                          start=st, stop=sp)
                                ins = te.matmul(ps_k[t % 2][:, 512:1024], lhs,
                                                wkt_sb[:, i * 1024 + 512:(i + 1) * 1024],
                                                start=st, stop=sp)
                                if sp:
                                    ins.then_inc(S["akMM"], 1)
                        C["akMM"] = NT_F

                    @blk.vector
                    def _(v):
                        v = Fenced(v, S["vF"], C, "vF")
                        for t in range(NT_F):
                            v.wait_ge(S["akMM"], t + 1)
                            R["akCS"].wait(v, t)
                            R["akSN"].wait(v, t)
                            if t >= 2:
                                R["akKT"].wait(v, t - 2)
                            _rope_tm(v, ps_k[t % 2], cs_sb[t % 2][:],
                                     sn_sb[t % 2][:], krot[t % 2][:], rt, KVH)
                        assert C["vF"] == vbAK + ROPE_OPS * NT_F

            sbAV = C["scF"]
            with ExitStack() as pv:
                Ev = pv.enter_context
                wvt_sb = Ev(nc.sbuf_tensor("wvt_sb", [128, 16 * KVH * D], BF16))
                xb = [Ev(nc.sbuf_tensor(f"avxb{i}", [128, EMB], BF16)) for i in range(2)]
                xfm = [Ev(nc.sbuf_tensor(f"avxf{i}", [128, EMB], BF16)) for i in range(2)]
                ps_v = [Ev(nc.psum_tensor(f"ps_v{i}", [128, KVH * D], F32)) for i in range(2)]

                with nc.Block() as blk:
                    @blk.gpsimd
                    def _(g):
                        R["avW"].attach(
                            g.dma_start(out=wvt_sb[:].rearrange("p (i o) -> p i o", i=16),
                                        in_=wvt.rearrange("(i p) o -> p i o", p=128)))
                        for t in range(NT_F):
                            if t >= 2:
                                R["avTX"].wait(g, t - 2)
                            R["avX"].attach(
                                g.dma_start(out=xb[t % 2][:],
                                            in_=x_seq[t * 128:(t + 1) * 128, :]))
                        R["avX"].wait_all(g)
                        R["avW"].wait_all(g)

                    @blk.sync
                    def _(s):
                        for t in range(NT_F):
                            R["avX"].wait(s, t)
                            if t >= 2:
                                s.wait_ge(S["avMM"], t - 1)
                            R["avTX"].attach(s.dma_start(
                                out=xfm[t % 2][:].rearrange("p (b f) -> p b f", b=16),
                                in_=xb[t % 2][:].rearrange("p (b f) -> p b f", b=16),
                                transpose=True))
                        R["avTX"].wait_all(s)

                    @blk.tensor
                    def _(te):
                        R["avW"].wait(te, 0)
                        for t in range(NT_F):
                            R["avTX"].wait(te, t)
                            if t >= 2:
                                te.wait_ge(S["scF"], sbAV + t - 1)
                            for i in range(16):
                                st, sp = (i == 0), (i == 15)
                                lhs = xfm[t % 2][:, i * 128:(i + 1) * 128]
                                te.matmul(ps_v[t % 2][:, 0:512], lhs,
                                          wvt_sb[:, i * 1024:i * 1024 + 512],
                                          start=st, stop=sp)
                                ins = te.matmul(ps_v[t % 2][:, 512:1024], lhs,
                                                wvt_sb[:, i * 1024 + 512:(i + 1) * 1024],
                                                start=st, stop=sp)
                                if sp:
                                    ins.then_inc(S["avMM"], 1)
                        C["avMM"] = NT_F

                    @blk.scalar
                    def _(s):
                        s = Fenced(s, S["scF"], C, "scF")
                        for t in range(NT_F):
                            s.wait_ge(S["avMM"], t + 1)
                            s.copy(v_tm[:, t * 1024:(t + 1) * 1024], ps_v[t % 2][:])
                        assert C["scF"] == sbAV + NT_F

            q_fm = Ea(nc.sbuf_tensor("q_fm", [128, HEADS * 512], BF16))
            ctx_fm = Ea(nc.sbuf_tensor("ctx_fm", [128, HEADS * 512], BF16))

            for gi in range(NG):
                q0 = gi * 512
                bWQC0 = gi * 32
                bS0 = bPV0 = gi * NE
                bTO0 = gi * 16

                vbB1 = C["vF"]

                def bQR(tl, _vb=vbB1):
                    return _vb + ROPE_OPS * (tl + 1)

                with ExitStack() as pb, nc.Block() as blk:
                    Eb = pb.enter_context
                    xb2 = [Eb(nc.sbuf_tensor(f"b1xb{gi}_{i}", [128, EMB], BF16)) for i in range(2)]
                    xfm2 = [Eb(nc.sbuf_tensor(f"b1xf{gi}_{i}", [128, EMB], BF16)) for i in range(2)]
                    qrot = [Eb(nc.sbuf_tensor(f"b1qr{gi}_{i}", [128, EMB], BF16)) for i in range(2)]
                    wq_buf = [Eb(nc.sbuf_tensor(f"b1wq{gi}_{i}", [128, EMB], BF16)) for i in range(2)]
                    cs2 = [Eb(nc.sbuf_tensor(f"b1cs{gi}_{i}", [128, D], F32)) for i in range(2)]
                    sn2 = [Eb(nc.sbuf_tensor(f"b1sn{gi}_{i}", [128, D], F32)) for i in range(2)]
                    rt2 = [Eb(nc.sbuf_tensor(f"b1rt{gi}_{i}", [128, HEADS * 64], F32)) for i in range(2)]
                    ps_q = [Eb(nc.psum_tensor(f"b1pq{gi}_{i}", [128, EMB], F32)) for i in range(2)]

                    @blk.gpsimd
                    def _(g):
                        for tl in range(4):
                            if tl >= 2:
                                R["bTX"].wait(g, gi * 4 + tl - 2)
                            R["bX"].attach(
                                g.dma_start(out=xb2[tl % 2][:],
                                            in_=x_own[q0 + tl * 128:q0 + (tl + 1) * 128, :]))
                        for k in range(32):
                            i = k % 16
                            if k >= 2:
                                g.wait_ge(S["bWQC"], bWQC0 + k - 1)
                            R["bWQ"].attach(
                                g.dma_start(out=wq_buf[k % 2][:],
                                            in_=wq_bf_dram[i * 128:(i + 1) * 128, :]))
                        R["bX"].wait_all(g)
                        R["bWQ"].wait_all(g)

                    @blk.sync
                    def _(s):
                        for tl in range(4):
                            R["bX"].wait(s, gi * 4 + tl)
                            if tl >= 2:
                                s.wait_ge(S["bWQC"], bWQC0 + 16 * ((tl - 2) // 2 + 1))
                            R["bTX"].attach(s.dma_start(
                                out=xfm2[tl % 2][:].rearrange("p (b f) -> p b f", b=16),
                                in_=xb2[tl % 2][:].rearrange("p (b f) -> p b f", b=16),
                                transpose=True))
                            if tl >= 2:
                                s.wait_ge(S["vF"], bQR(tl - 2))
                            R["bCS"].attach(
                                s.dma_start(out=cs2[tl % 2][:],
                                            in_=cos_o[q0 + tl * 128:q0 + (tl + 1) * 128, :]))
                            R["bSN"].attach(
                                s.dma_start(out=sn2[tl % 2][:],
                                            in_=sin_o[q0 + tl * 128:q0 + (tl + 1) * 128, :]))
                            if tl >= 1:
                                tk = tl - 1
                                s.wait_ge(S["vF"], bQR(tk))
                                R["bQT"].attach(s.dma_start(
                                    out=q_fm[:].rearrange("p (h s) -> p h s", h=HEADS)
                                        [:, :, tk * 128:(tk + 1) * 128],
                                    in_=qrot[tk % 2][:].rearrange("p (h f) -> p h f", h=HEADS),
                                    transpose=True))
                        tk = 3
                        s.wait_ge(S["vF"], bQR(tk))
                        R["bQT"].attach(s.dma_start(
                            out=q_fm[:].rearrange("p (h s) -> p h s", h=HEADS)
                                [:, :, tk * 128:(tk + 1) * 128],
                            in_=qrot[tk % 2][:].rearrange("p (h f) -> p h f", h=HEADS),
                            transpose=True))
                        for rn in ("bTX", "bCS", "bSN", "bQT"):
                            R[rn].wait_all(s)

                    @blk.tensor
                    def _(te):
                        for sw in range(2):
                            tls = (2 * sw, 2 * sw + 1)
                            for i in range(16):
                                k = sw * 16 + i
                                R["bWQ"].wait(te, gi * 32 + k)
                                st, sp = (i == 0), (i == 15)
                                for tl in tls:
                                    if st:
                                        R["bTX"].wait(te, gi * 4 + tl)
                                        if tl >= 2:
                                            te.wait_ge(S["vF"], bQR(tl - 2))
                                    lhs = xfm2[tl % 2][:, i * 128:(i + 1) * 128]
                                    last = None
                                    for oc in range(4):
                                        last = te.matmul(
                                            ps_q[tl % 2][:, oc * 512:(oc + 1) * 512], lhs,
                                            wq_buf[k % 2][:, oc * 512:(oc + 1) * 512],
                                            start=st, stop=sp)
                                    if tl == tls[1]:
                                        last.then_inc(S["bWQC"], 1)
                        C["bWQC"] = bWQC0 + 32

                    @blk.vector
                    def _(v):
                        v = Fenced(v, S["vF"], C, "vF")
                        for tl in range(4):
                            v.wait_ge(S["bWQC"], bWQC0 + 16 * (tl // 2 + 1))
                            R["bCS"].wait(v, gi * 4 + tl)
                            R["bSN"].wait(v, gi * 4 + tl)
                            if tl >= 2:
                                R["bQT"].wait(v, gi * 4 + tl - 2)
                            _rope_tm(v, ps_q[tl % 2], cs2[tl % 2][:],
                                     sn2[tl % 2][:], qrot[tl % 2][:], rt2, HEADS)
                        assert C["vF"] == vbB1 + ROPE_OPS * 4

                vbB2 = C["vF"]
                sbB2 = C["scF"]

                def bRC(h, _vb=vbB2):
                    return _vb + 2 * h + 1

                def bVC(h, _vb=vbB2):
                    return _vb + 2 * (h + 1)

                def bEX(e, _sb=sbB2):
                    return _sb + e + 1

                def bOE(j, _sb=sbB2):
                    return _sb + NE + j + 1

                with ExitStack() as pb, nc.Block() as blk:
                    Eb = pb.enter_context
                    pT = [Eb(nc.sbuf_tensor(f"b2pT{gi}_{i}", [128, 1024], BF16)) for i in range(2)]
                    rbc = Eb(nc.sbuf_tensor(f"b2rb{gi}", [128, 512], F32))
                    wo_buf = [Eb(nc.sbuf_tensor(f"b2wo{gi}_{i}", [128, 16 * 512], BF16))
                              for i in range(2)]
                    midb = [Eb(nc.sbuf_tensor(f"b2mb{gi}_{i}", [128, 512], F32)) for i in range(2)]
                    ps_s = [Eb(nc.psum_tensor(f"b2ps{gi}_{i}", [128, 1024], F32)) for i in range(2)]
                    ps_ctx = Eb(nc.psum_tensor(f"b2pc{gi}", [128, 512], F32))
                    ps_row = Eb(nc.psum_tensor(f"b2prw{gi}", [128, 512], F32))
                    ps_o = [Eb(nc.psum_tensor(f"b2po{gi}_{i}", [128, 512], F32)) for i in range(2)]

                    @blk.gpsimd
                    def _(g):
                        for oc in range(4):
                            if oc >= 2:
                                g.wait_ge(S["bTO"], bTO0 + 4 * (oc - 1))
                            R["bWO"].attach(g.dma_start(
                                out=wo_buf[oc % 2][:].rearrange("p (i o) -> p i o", i=16),
                                in_=wot.rearrange("(i p) o -> p i o", p=128)
                                    [:, :, oc * 512:(oc + 1) * 512]))
                        R["bWO"].wait_all(g)

                    @blk.tensor
                    def _(te):
                        R["bQT"].wait(te, gi * 4 + 3)
                        R["bQT"].wait(te, gi * 4 + 2)

                        def emit_S(e):
                            h, pr = e // NPAIR, e % NPAIR
                            kv = h // 2
                            if e >= 2:
                                te.wait_ge(S["scF"], bEX(e - 2))
                            last = None
                            for half in range(2):
                                kc = 2 * pr + half
                                last = te.matmul(
                                    ps_s[e % 2][:, half * 512:half * 512 + 512],
                                    k_fm[:, kv * S_full + kc * 128:kv * S_full + (kc + 1) * 128],
                                    q_fm[:, h * 512:(h + 1) * 512],
                                    start=True, stop=True)
                            last.then_inc(S["bS"], 1)

                        def emit_PV(e):
                            h, pr = e // NPAIR, e % NPAIR
                            kv = h // 2
                            te.wait_ge(S["scF"], bEX(e))
                            if pr == 0 and h >= 1:
                                te.wait_ge(S["vF"], bVC(h - 1))
                            last = None
                            for half in range(2):
                                kc = 2 * pr + half
                                st, sp = (kc == 0), (kc == NKC - 1)
                                psl = pT[e % 2][:, half * 512:half * 512 + 512]
                                te.matmul(ps_ctx[:],
                                          v_tm[:, kc * 1024 + kv * 128:kc * 1024 + (kv + 1) * 128],
                                          psl, start=st, stop=sp)
                                last = te.matmul(ps_row[:], ones_bf[:], psl,
                                                 start=st, stop=sp)
                            last.then_inc(S["bPV"], 1)

                        for e in range(NE):
                            emit_S(e)
                            if e >= 1:
                                emit_PV(e - 1)
                        emit_PV(NE - 1)
                        C["bS"] = bS0 + NE
                        C["bPV"] = bPV0 + NE

                        te.wait_ge(S["vF"], bVC(HEADS - 1))
                        for oc in range(4):
                            R["bWO"].wait(te, gi * 4 + oc)
                            for qt in range(4):
                                j = 4 * oc + qt
                                if j >= 2:
                                    te.wait_ge(S["scF"], bOE(j - 2))
                                last = None
                                for i in range(16):
                                    last = te.matmul(
                                        ps_o[j % 2][:],
                                        ctx_fm[:, i * 512 + qt * 128:i * 512 + (qt + 1) * 128],
                                        wo_buf[oc % 2][:, i * 512:(i + 1) * 512],
                                        start=(i == 0), stop=(i == 15))
                                last.then_inc(S["bTO"], 1)
                        C["bTO"] = bTO0 + 16

                    @blk.scalar
                    def _(s):
                        s = Fenced(s, S["scF"], C, "scF")
                        for e in range(NE):
                            s.wait_ge(S["bS"], bS0 + e + 1)
                            if e >= 2:
                                s.wait_ge(S["bPV"], bPV0 + e - 1)
                            s.activation(pT[e % 2][:], ps_s[e % 2][:], AF.Exp,
                                         bias=bias_shift[:], scale=SCALE_S)
                        for oc in range(4):
                            for qt in range(4):
                                j = 4 * oc + qt
                                s.wait_ge(S["bTO"], bTO0 + j + 1)
                                if j >= 2:
                                    R["bMS"].wait(s, gi * 16 + j - 2)
                                s.copy(midb[j % 2][:], ps_o[j % 2][:])
                        assert C["scF"] == sbB2 + NE + 16

                    @blk.vector
                    def _(v):
                        v = Fenced(v, S["vF"], C, "vF")
                        for h in range(HEADS):
                            v.wait_ge(S["bPV"], bPV0 + (h + 1) * NPAIR)
                            v.reciprocal(rbc[:], ps_row[:])
                            v.tensor_tensor(ctx_fm[:, h * 512:(h + 1) * 512],
                                            ps_ctx[:], rbc[:], ALU.mult)
                        assert C["vF"] == vbB2 + 2 * HEADS

                    @blk.sync
                    def _(s):
                        for oc in range(4):
                            for qt in range(4):
                                j = 4 * oc + qt
                                s.wait_ge(S["scF"], bOE(j))
                                R["bMS"].attach(s.dma_start(
                                    out=mid_dram[q0 + qt * 128:q0 + (qt + 1) * 128,
                                                 oc * 512:(oc + 1) * 512],
                                    in_=midb[j % 2][:]))
                        R["bMS"].wait_all(s)

        vb0 = C["vF"]
        sb0 = C["scF"]
        gb0 = C["gF"]

        def fB(t):
            return 0 if t == 0 else 4 + 8 * (t - 1)

        def vCent(t):
            return vb0 + fB(t) + 4

        def vQpre(t):
            return vb0 + (8 * t + 12 if t < NT_O - 1 else 8 * NT_O)

        def scLn(t):
            return sb0 + (1 if t == 0 else 4 * t)

        def sLN3(t):
            return sb0 + (3 if t == 0 else 4 * t + 2)

        def sCast(t):
            return sb0 + (4 * t + 7 if t < NT_O - 1 else 4 * NT_O)

        def gQtt(t):
            return gb0 + 2 * t + 1

        def gDeq(t):
            return gb0 + 2 * t + 2

        ffn_scope = ExitStack()
        Ef = ffn_scope.enter_context
        xq_fm = Ef(nc.sbuf_tensor("xq_fm", [128, 16 * S_own], BF16))
        with ExitStack() as p0:
            E0 = p0.enter_context
            midt = [E0(nc.sbuf_tensor(f"f0m{i}", [128, EMB], F32)) for i in range(3)]
            xt = [E0(nc.sbuf_tensor(f"f0x{i}", [128, EMB], F32)) for i in range(3)]
            pre = E0(nc.sbuf_tensor("f0pre", [128, EMB], F32))
            junk = E0(nc.sbuf_tensor("f0junk", [128, EMB], F32))
            cent = [E0(nc.sbuf_tensor(f"f0c{i}", [128, EMB], F32)) for i in range(2)]
            xmid = [E0(nc.sbuf_tensor(f"f0xm{i}", [128, EMB], F32)) for i in range(2)]
            vq = [E0(nc.sbuf_tensor(f"f0vq{i}", [128, EMB], F32)) for i in range(2)]
            q8 = [E0(nc.sbuf_tensor(f"f0q8{i}", [128, EMB], F8)) for i in range(2)]
            xqt = [E0(nc.sbuf_tensor(f"f0xq{i}", [128, EMB], BF16)) for i in range(2)]
            stat = E0(nc.sbuf_tensor("f0st", [128, 8], F32))
            red = [E0(nc.sbuf_tensor(f"f0rd{i}", [128, 64], F32)) for i in range(2)]
            s2 = [E0(nc.sbuf_tensor(f"f0s2{i}", [128, 64], F32)) for i in range(2)]
            rcp = [E0(nc.sbuf_tensor(f"f0rc{i}", [128, 64], F32)) for i in range(2)]

            with nc.Block() as blk:
                @blk.sync
                def _(s):
                    def emit_transp(t):
                        s.wait_ge(S["gF"], gDeq(t))
                        R["fXQT"].attach(s.dma_start(
                            out=xq_fm[:].rearrange("p (b s) -> p b s", b=16)
                                [:, :, t * 128:(t + 1) * 128],
                            in_=xqt[t % 2][:].rearrange("p (b f) -> p b f", b=16),
                            transpose=True))

                    for t in range(min(3, NT_O)):
                        R["fM"].attach(s.dma_start(out=midt[t % 3][:],
                                                   in_=mid_dram[t * 128:(t + 1) * 128, :]))
                        R["fX"].attach(s.dma_start(out=xt[t % 3][:],
                                                   in_=x_own[t * 128:(t + 1) * 128, :]))
                    for t in range(NT_O):
                        if t + 3 < NT_O:
                            s.wait_ge(S["vF"], vb0 + fB(t) + 1)
                            R["fM"].attach(s.dma_start(
                                out=midt[t % 3][:],
                                in_=mid_dram[(t + 3) * 128:(t + 4) * 128, :]))
                            R["fX"].attach(s.dma_start(
                                out=xt[t % 3][:],
                                in_=x_own[(t + 3) * 128:(t + 4) * 128, :]))
                        s.wait_ge(S["scF"], sLN3(t))
                        R["fXM"].attach(s.dma_start(out=xmid_dram[t * 128:(t + 1) * 128, :],
                                                    in_=xmid[t % 2][:]))
                        if t >= 1:
                            emit_transp(t - 1)
                    emit_transp(NT_O - 1)
                    for rn in ("fM", "fX", "fXM", "fXQT"):
                        R[rn].wait_all(s)

                @blk.vector
                def _(v):
                    v = Fenced(v, S["vF"], C, "vF")

                    def emit_prequant(t):
                        v.wait_ge(S["scF"], sLN3(t))
                        if t >= 2:
                            v.wait_ge(S["gF"], gDeq(t - 2))
                        x3 = xmid[t % 2][:].rearrange("p (b r) -> p b r", r=32)
                        v.tensor_reduce(red[t % 2][:], x3, axis=AX.X, op=ALU.max,
                                        apply_absolute_value=True)
                        v.tensor_scalar(s2[t % 2][:], red[t % 2][:], INV_224,
                                        None, ALU.mult)
                        v.tensor_scalar_max(s2[t % 2][:], s2[t % 2][:], QEPS2)
                        v.reciprocal(rcp[t % 2][:], s2[t % 2][:])

                    for t in range(NT_O):
                        R["fM"].wait(v, t)
                        R["fX"].wait(v, t)
                        v.scalar_tensor_tensor(pre[:], midt[t % 3][:], 1.0,
                                               xt[t % 3][:], ALU.mult, ALU.add,
                                               accum_out=stat[:, 0:1])
                        v.tensor_scalar(stat[:, 1:2], stat[:, 0:1],
                                        1.0 / EMB, None, ALU.mult)
                        if t >= 1:
                            v.wait_ge(S["scF"], scLn(t - 1))
                        v.scalar_tensor_tensor(junk[:], pre[:], stat[:, 1:2],
                                               pre[:], ALU.subtract, ALU.mult,
                                               accum_out=stat[:, 2:3])
                        v.tensor_scalar(cent[t % 2][:], pre[:], stat[:, 1:2], None,
                                        ALU.subtract)
                        if t >= 1:
                            emit_prequant(t - 1)
                    emit_prequant(NT_O - 1)
                    assert C["vF"] == vb0 + 8 * NT_O

                @blk.scalar
                def _(s):
                    s = Fenced(s, S["scF"], C, "scF")

                    def emit_cast(t):
                        s.wait_ge(S["gF"], gQtt(t))
                        if t >= 2:
                            s.wait_ge(S["gF"], gDeq(t - 2))
                        s.copy(q8[t % 2][:], vq[t % 2][:])

                    for t in range(NT_O):
                        s.wait_ge(S["vF"], vCent(t))
                        s.activation(stat[:, 3:4], stat[:, 2:3], AF.Ln,
                                     bias=bias_eps[:], scale=1.0 / EMB)
                        s.activation(stat[:, 4:5], stat[:, 3:4], AF.Exp, scale=-0.5)
                        if t >= 2:
                            R["fXM"].wait(s, t - 2)
                            s.wait_ge(S["gF"], gQtt(t - 2))
                            s.wait_ge(S["vF"], vQpre(t - 2))
                        s.activation(xmid[t % 2][:], cent[t % 2][:], AF.Copy,
                                     scale=stat[:, 4:5])
                        if t >= 1:
                            emit_cast(t - 1)
                    emit_cast(NT_O - 1)
                    assert C["scF"] == sb0 + 4 * NT_O

                @blk.gpsimd
                def _(g):
                    g = Fenced(g, S["gF"], C, "gF")
                    for t in range(NT_O):
                        g.wait_ge(S["vF"], vQpre(t))
                        g.wait_ge(S["scF"], sLN3(t))
                        if t >= 2:
                            g.wait_ge(S["scF"], sCast(t - 2))
                        x3 = xmid[t % 2][:].rearrange("p (b r) -> p b r", r=32)
                        v3 = vq[t % 2][:].rearrange("p (b r) -> p b r", r=32)
                        rb = rcp[t % 2][:].unsqueeze(2).broadcast_to([128, 64, 32])
                        g.tensor_tensor(v3, x3, rb, ALU.mult)
                        g.wait_ge(S["scF"], sCast(t))
                        if t >= 2:
                            R["fXQT"].wait(g, t - 2)
                        q3 = q8[t % 2][:].rearrange("p (b r) -> p b r", r=32)
                        o3 = xqt[t % 2][:].rearrange("p (b r) -> p b r", r=32)
                        s2b = s2[t % 2][:].unsqueeze(2).broadcast_to([128, 64, 32])
                        g.tensor_tensor(o3, q3, s2b, ALU.mult)
                    assert C["gF"] == gb0 + 2 * NT_O

        vb1 = C["vF"]
        sb1 = C["scF"]
        gb1 = C["gF"]

        def vTT(jc, m):
            if jc == 0:
                return vb1 + 4 * (m + 1)
            return vb1 + 12 * jc + 9 * max(0, jc - 2) + 4 * (m + 1)

        def _hS(m):
            if m <= NJC - 3:
                return vb1 + 21 * m + 36
            if m == NJC - 2:
                return vb1 + 21 * NJC - 18
            return vb1 + 21 * NJC - 9

        def vHmultA(m):
            return _hS(m) + 1

        def vHmultB(m):
            return _hS(m) + 2

        def vHpre(m):
            return _hS(m) + 7

        def vHdeq(m):
            return _hS(m) + 9

        def scC(jc, m):
            return sb1 + 3 * jc + 2 * max(0, jc - 1) + m + 1

        def scSiluA(m):
            return sb1 + (5 * m + 7 if m < NJC - 1 else 5 * NJC - 1)

        def scSiluB(m):
            return sb1 + (5 * m + 8 if m < NJC - 1 else 5 * NJC)

        def gTT(jc, m):
            return gb1 + (m + 1 if jc == 0 else 6 * jc + m + 1)

        def gDeq(jc, m):
            return gb1 + 6 * jc + m + 4

        with ExitStack() as p1:
            E1 = p1.enter_context
            gw = [E1(nc.sbuf_tensor(f"f1g{i}", [128, EMB], F32)) for i in range(2)]
            uw = [E1(nc.sbuf_tensor(f"f1u{i}", [128, EMB], F32)) for i in range(2)]
            dw = [E1(nc.sbuf_tensor(f"f1d{i}", [128, EMB], F32)) for i in range(2)]
            gq = [E1(nc.sbuf_tensor(f"f1gq{i}", [128, EMB], BF16)) for i in range(2)]
            uq = [E1(nc.sbuf_tensor(f"f1uq{i}", [128, EMB], BF16)) for i in range(2)]
            dq = [E1(nc.sbuf_tensor(f"f1dq{i}", [128, EMB], BF16)) for i in range(2)]
            guf = [E1(nc.sbuf_tensor(f"f1guf{i}", [128, 16 * 256], BF16)) for i in range(2)]
            dqf = [E1(nc.sbuf_tensor(f"f1dqf{i}", [128, EMB], BF16)) for i in range(2)]
            sil = E1(nc.sbuf_tensor("f1sil", [128, NTT * 128], F32))
            hf = E1(nc.sbuf_tensor("f1hf", [128, NTT * 128], F32))
            hqt = [E1(nc.sbuf_tensor(f"f1hqt{i}", [128, NTT * 128], BF16)) for i in range(2)]
            hqf = [E1(nc.sbuf_tensor(f"f1hqf{i}", [128, S_own], BF16)) for i in range(2)]
            Q8 = {m: E1(nc.sbuf_tensor(f"f1q8_{m}", [128, EMB], F8)) for m in "gudh"}
            NSB = {"g": 2, "u": 2, "d": 2, "h": 1}
            RD = {m: [E1(nc.sbuf_tensor(f"f1rd_{m}{i}", [128, 64], F32))
                      for i in range(NSB[m])] for m in "gudh"}
            RC = {m: [E1(nc.sbuf_tensor(f"f1rc_{m}{i}", [128, 64], F32))
                      for i in range(NSB[m])] for m in "gudh"}
            S2 = {m: [E1(nc.sbuf_tensor(f"f1s2_{m}{i}", [128, 64], F32))
                      for i in range(NSB[m])] for m in "gudh"}
            ps_gu = [E1(nc.psum_tensor(f"f1pgu{i}", [128, NTT * 128], F32))
                     for i in range(2)]
            HTT = NTT // 2

            def qreduce(eng, src, nb, m, bi):
                s3 = src.rearrange("p (b r) -> p b r", r=32)
                eng.tensor_reduce(RD[m][bi][:, 0:nb], s3, axis=AX.X, op=ALU.max,
                                  apply_absolute_value=True)

            def qsmalls(v, nb, m, bi):
                v.tensor_scalar(S2[m][bi][:, 0:nb], RD[m][bi][:, 0:nb], INV_224,
                                None, ALU.mult)
                v.tensor_scalar_max(S2[m][bi][:, 0:nb], S2[m][bi][:, 0:nb], QEPS2)
                v.reciprocal(RC[m][bi][:, 0:nb], S2[m][bi][:, 0:nb])

            def qtt(eng, src, nb, m, bi):
                s3 = src.rearrange("p (b r) -> p b r", r=32)
                rb = RC[m][bi][:, 0:nb].unsqueeze(2).broadcast_to([128, nb, 32])
                eng.tensor_tensor(s3, s3, rb, ALU.mult)

            def qpre(v, src, nb, m, bi):
                qreduce(v, src, nb, m, bi)
                qsmalls(v, nb, m, bi)
                qtt(v, src, nb, m, bi)

            def deq(eng, out_bf, nb, m, bi):
                s2b = S2[m][bi][:, 0:nb].unsqueeze(2).broadcast_to([128, nb, 32])
                q3 = Q8[m][:, 0:nb * 32].rearrange("p (b r) -> p b r", r=32)
                o3 = out_bf.rearrange("p (b r) -> p b r", r=32)
                eng.tensor_tensor(o3, q3, s2b, ALU.mult)

            with nc.Block() as blk:
                @blk.sync
                def _(s):
                    def emit_loads(jc):
                        if jc >= 2:
                            for mi in range(3):
                                s.wait_ge(S["scF"], scC(jc - 2, mi))
                        R["gG"].attach(s.dma_start(out=gw[jc % 2][:],
                                                   in_=gate_w[jc * 128:(jc + 1) * 128, :]))
                        R["gU"].attach(s.dma_start(out=uw[jc % 2][:],
                                                   in_=up_w[jc * 128:(jc + 1) * 128, :]))
                        R["gD"].attach(s.dma_start(
                            out=dw[jc % 2][:].rearrange("p (o j) -> p o j", o=16),
                            in_=down_wr[jc].rearrange("(o p) j -> p o j", p=128)))

                    def emit_hsync(m):
                        s.wait_ge(S["vF"], vHdeq(m))
                        if m >= 2:
                            R["gHS"].wait(s, m - 2)
                        R["gHT"].attach(s.dma_start(
                            out=hqf[m % 2][:].rearrange("p (b s) -> p b s", b=NTT),
                            in_=hqt[m % 2][:].rearrange("p (b f) -> p b f", b=NTT),
                            transpose=True))
                        R["gHT"].wait(s, m)
                        R["gHS"].attach(s.dma_start(out=hq_dram[m], in_=hqf[m % 2][:]))

                    emit_loads(0)
                    emit_loads(1)
                    for jc in range(NJC):
                        s.wait_ge(S["gF"], gDeq(jc, 0))
                        if jc >= 2:
                            s.wait_ge(S["gMM"], 2 * jc - 2)
                        R["gTG"].attach(s.dma_start(
                            out=guf[jc % 2][:].rearrange("p (i c) -> p i c", i=16)[:, :, 0:128],
                            in_=gq[jc % 2][:].rearrange("p (i c) -> p i c", i=16),
                            transpose=True))
                        s.wait_ge(S["gF"], gDeq(jc, 1))
                        R["gTU"].attach(s.dma_start(
                            out=guf[jc % 2][:].rearrange("p (i c) -> p i c", i=16)[:, :, 128:256],
                            in_=uq[jc % 2][:].rearrange("p (i c) -> p i c", i=16),
                            transpose=True))
                        s.wait_ge(S["gF"], gDeq(jc, 2))
                        if jc >= 2:
                            R["gDS"].wait(s, jc - 2)
                        R["gTD"].attach(s.dma_start(
                            out=dqf[jc % 2][:].rearrange("p (i c) -> p i c", i=16),
                            in_=dq[jc % 2][:].rearrange("p (i c) -> p i c", i=16),
                            transpose=True))
                        R["gTD"].wait(s, jc)
                        R["gDS"].attach(s.dma_start(out=dq_dram[jc], in_=dqf[jc % 2][:]))
                        if jc + 2 < NJC:
                            emit_loads(jc + 2)
                        if jc >= 2:
                            emit_hsync(jc - 2)
                    emit_hsync(NJC - 2)
                    emit_hsync(NJC - 1)
                    for rn in ("gG", "gU", "gD", "gTG", "gTU", "gTD", "gDS", "gHT", "gHS"):
                        R[rn].wait_all(s)

                @blk.vector
                def _(v):
                    v = Fenced(v, S["vF"], C, "vF")

                    def emit_hmult(m, hh):
                        v.wait_ge(S["scF"], (scSiluA if hh == 0 else scSiluB)(m))
                        pg = ps_gu[hh][:].rearrange("p (t c) -> p t c", t=HTT)
                        hs = hf[:, hh * HTT * 128:(hh + 1) * HTT * 128]
                        ss = sil[:, hh * HTT * 128:(hh + 1) * HTT * 128]
                        v.tensor_tensor(
                            hs.rearrange("p (t c) -> p t c", t=HTT),
                            ss.rearrange("p (t c) -> p t c", t=HTT),
                            pg[:, :, 128:256], ALU.mult)

                    def emit_hquant(m):
                        qpre(v, hf[:], NTT * 4, "h", 0)
                        v.tensor_copy(Q8["h"][:, 0:NTT * 128], hf[:])
                        if m >= 2:
                            R["gHT"].wait(v, m - 2)
                        deq(v, hqt[m % 2][:], NTT * 4, "h", 0)

                    def emit_qpre(v, jc, mi, m):
                        srcb = {"g": gw, "u": uw, "d": dw}[m][jc % 2]
                        R[{"g": "gG", "u": "gU", "d": "gD"}[m]].wait(v, jc)
                        if jc >= 2:
                            v.wait_ge(S["gF"], gDeq(jc - 2, mi))
                        qreduce(v, srcb[:], 64, m, jc % 2)
                        qsmalls(v, 64, m, jc % 2)

                    for mi, m in enumerate("gud"):
                        emit_qpre(v, 0, mi, m)
                    for jc in range(NJC):
                        if jc + 1 < NJC:
                            for mi, m in enumerate("gud"):
                                emit_qpre(v, jc + 1, mi, m)
                        if jc >= 1:
                            emit_hmult(jc - 1, 0)
                            emit_hmult(jc - 1, 1)
                            emit_hquant(jc - 1)
                    emit_hmult(NJC - 1, 0)
                    emit_hmult(NJC - 1, 1)
                    emit_hquant(NJC - 1)
                    assert C["vF"] == vb1 + 21 * NJC

                @blk.scalar
                def _(s):
                    s = Fenced(s, S["scF"], C, "scF")

                    def emit_silu(m, hh):
                        s.wait_ge(S["gMM"], 2 * m + hh + 1)
                        pg = ps_gu[hh][:].rearrange("p (t c) -> p t c", t=HTT)
                        ss = sil[:, hh * HTT * 128:(hh + 1) * HTT * 128]
                        s.activation(ss.rearrange("p (t c) -> p t c", t=HTT),
                                     pg[:, :, 0:128], AF.Silu)

                    for jc in range(NJC):
                        for mi, (m, srcr) in enumerate(zip("gud", (gw, uw, dw))):
                            s.wait_ge(S["gF"], gTT(jc, mi))
                            if jc >= 1:
                                s.wait_ge(S["gF"], gDeq(jc - 1, mi))
                            s.copy(Q8[m][:], srcr[jc % 2][:])
                        if jc >= 1:
                            emit_silu(jc - 1, 0)
                            emit_silu(jc - 1, 1)
                    emit_silu(NJC - 1, 0)
                    emit_silu(NJC - 1, 1)
                    assert C["scF"] == sb1 + 5 * NJC

                @blk.gpsimd
                def _(g):
                    g = Fenced(g, S["gF"], C, "gF")
                    for mi, (m, srcr) in enumerate(zip("gud", (gw, uw, dw))):
                        g.wait_ge(S["vF"], vTT(0, mi))
                        qtt(g, srcr[0][:], 64, m, 0)
                    for jc in range(NJC):
                        for mi, (m, outr) in enumerate(zip("gud", (gq, uq, dq))):
                            g.wait_ge(S["scF"], scC(jc, mi))
                            if jc >= 2:
                                R[{"g": "gTG", "u": "gTU", "d": "gTD"}[m]].wait(g, jc - 2)
                            deq(g, outr[jc % 2][:], 64, m, jc % 2)
                        if jc + 1 < NJC:
                            for mi, (m, srcr) in enumerate(zip("gud", (gw, uw, dw))):
                                g.wait_ge(S["vF"], vTT(jc + 1, mi))
                                qtt(g, srcr[(jc + 1) % 2][:], 64, m, (jc + 1) % 2)
                    assert C["gF"] == gb1 + 6 * NJC

                @blk.tensor
                def _(te):
                    R["fXQT"].wait_all(te)
                    for jc in range(NJC):
                        R["gTG"].wait(te, jc)
                        R["gTU"].wait(te, jc)
                        for hh in range(2):
                            if jc >= 1:
                                te.wait_ge(S["vF"],
                                           (vHmultA if hh == 0 else vHmultB)(jc - 1))
                                te.wait_ge(S["scF"],
                                           (scSiluA if hh == 0 else scSiluB)(jc - 1))
                            ins = None
                            for ttl in range(HTT):
                                tt = hh * HTT + ttl
                                for i in range(16):
                                    st, sp = (i == 0), (i == 15)
                                    ins = te.matmul(
                                        ps_gu[hh][:, ttl * 256:(ttl + 1) * 256],
                                        xq_fm[:, i * S_own + tt * 128:i * S_own + (tt + 1) * 128],
                                        guf[jc % 2][:, i * 256:(i + 1) * 256],
                                        start=st, stop=sp)
                            ins.then_inc(S["gMM"], 1)
                    C["gMM"] = 2 * NJC

        NB2 = NJC // 8
        F2_V = 8 + 16
        vb2 = C["vF"]
        sb2 = C["scF"]

        def hEV(it):
            tg, oh = divmod(it, 2)
            return vb2 + F2_V * tg + 4 * (oh + 1)

        def hLN2(u):
            tg, tt = divmod(u, 4)
            return vb2 + F2_V * tg + 8 + 4 * (tt + 1)

        def hSC(u):
            return sb2 + 3 * (u + 1)

        ffn_scope.close()
        with ExitStack() as p2:
            E2 = p2.enter_context
            hqb = [E2(nc.sbuf_tensor(f"f2h{i}", [128, 8 * 512], BF16)) for i in range(3)]
            dqb = [E2(nc.sbuf_tensor(f"f2d{i}", [128, 8 * 1024], BF16)) for i in range(3)]
            mid2 = E2(nc.sbuf_tensor("f2mid", [128, 4 * EMB], F32))
            xm2 = E2(nc.sbuf_tensor("f2xm", [128, 4 * EMB], F32))
            junk2 = E2(nc.sbuf_tensor("f2junk", [128, EMB], F32))
            cent2 = E2(nc.sbuf_tensor("f2cent", [128, EMB], F32))
            outb = [E2(nc.sbuf_tensor(f"f2ob{i}", [128, EMB], F32)) for i in range(2)]
            stat2 = E2(nc.sbuf_tensor("f2st", [128, 32], F32))
            ps_f2 = [E2(nc.psum_tensor(f"f2p{i}", [128, 1024], F32)) for i in range(4)]

            NIT = NTG * 2
            with nc.Block() as blk:
                @blk.sync
                def _(s):
                    for tg in range(NTG):
                        for oh in range(2):
                            it = tg * 2 + oh
                            for bi in range(NB2):
                                bu = it * NB2 + bi
                                if bu >= 3:
                                    s.wait_ge(S["hMM"], (bu - 2) * 8)
                                R["hH"].attach(s.dma_start(
                                    out=hqb[bu % 3][:].rearrange("p (j t) -> p j t", j=8),
                                    in_=hq_dram[bi * 8:(bi + 1) * 8]
                                        [:, :, tg * 512:(tg + 1) * 512]
                                        .rearrange("j p t -> p j t")))
                                R["hD"].attach(s.dma_start(
                                    out=dqb[bu % 3][:].rearrange("p (j o) -> p j o", j=8),
                                    in_=dq_dram[bi * 8:(bi + 1) * 8]
                                        [:, :, oh * 1024:(oh + 1) * 1024]
                                        .rearrange("j p o -> p j o")))
                            if oh == 0:
                                if tg >= 1:
                                    s.wait_ge(S["vF"], hEV(2 * tg - 1))
                                for tt in range(4):
                                    R["hXM"].attach(s.dma_start(
                                        out=xm2[:, tt * EMB:(tt + 1) * EMB],
                                        in_=xmid_dram[tg * 512 + tt * 128:
                                                      tg * 512 + (tt + 1) * 128, :]))
                    for rn in ("hH", "hD", "hXM"):
                        R[rn].wait_all(s)
                    R["hOUT"].wait_all(s)

                @blk.tensor
                def _(te):
                    for tg in range(NTG):
                        for oh in range(2):
                            it = tg * 2 + oh
                            if it >= 1:
                                te.wait_ge(S["vF"], hEV(it - 1))
                            for bi in range(NB2):
                                bu = it * NB2 + bi
                                R["hH"].wait(te, bu)
                                R["hD"].wait(te, bu)
                                for j in range(8):
                                    jc = bi * 8 + j
                                    st, sp = (jc == 0), (jc == NJC - 1)
                                    last = None
                                    for tt in range(4):
                                        te.matmul(ps_f2[tt][:, 0:512],
                                                  hqb[bu % 3][:, (j * 4 + tt) * 128:
                                                              (j * 4 + tt + 1) * 128],
                                                  dqb[bu % 3][:, j * 1024:j * 1024 + 512],
                                                  start=st, stop=sp)
                                        last = te.matmul(
                                            ps_f2[tt][:, 512:1024],
                                            hqb[bu % 3][:, (j * 4 + tt) * 128:
                                                        (j * 4 + tt + 1) * 128],
                                            dqb[bu % 3][:, j * 1024 + 512:(j + 1) * 1024],
                                            start=st, stop=sp)
                                    last.then_inc(S["hMM"], 1)
                    C["hMM"] = NIT * NJC

                @blk.vector
                def _(v):
                    v = Fenced(v, S["vF"], C, "vF")
                    for tg in range(NTG):
                        for tt in range(4):
                            R["hXM"].wait(v, tg * 4 + tt)
                        for oh in range(2):
                            it = tg * 2 + oh
                            v.wait_ge(S["hMM"], (it + 1) * NJC)
                            for tt in range(4):
                                v.scalar_tensor_tensor(
                                    mid2[:, tt * EMB + oh * 1024:tt * EMB + (oh + 1) * 1024],
                                    ps_f2[tt][:], 1.0,
                                    xm2[:, tt * EMB + oh * 1024:tt * EMB + (oh + 1) * 1024],
                                    ALU.mult, ALU.add,
                                    accum_out=stat2[:, tt * 2 + oh:tt * 2 + oh + 1])
                        for tt in range(4):
                            base = tt * 2
                            v.tensor_tensor(stat2[:, 16 + tt:17 + tt],
                                            stat2[:, base:base + 1],
                                            stat2[:, base + 1:base + 2], ALU.add)
                            v.tensor_scalar(stat2[:, 25:26], stat2[:, 16 + tt:17 + tt],
                                            1.0 / EMB, None, ALU.mult)
                            m2 = mid2[:, tt * EMB:(tt + 1) * EMB]
                            v.scalar_tensor_tensor(junk2[:], m2, stat2[:, 25:26],
                                                   m2, ALU.subtract, ALU.mult,
                                                   accum_out=stat2[:, 26:27])
                            v.tensor_scalar(cent2[:], m2, stat2[:, 25:26], None,
                                            ALU.subtract)
                            v.wait_ge(S["scF"], hSC(tg * 4 + tt))
                    assert C["vF"] == vb2 + F2_V * NTG

                @blk.scalar
                def _(s):
                    raw = s
                    s = Fenced(s, S["scF"], C, "scF")
                    for tg in range(NTG):
                        for tt in range(4):
                            u = tg * 4 + tt
                            s.wait_ge(S["vF"], hLN2(u))
                            s.activation(stat2[:, 27:28], stat2[:, 26:27], AF.Ln,
                                         bias=bias_eps[:], scale=1.0 / EMB)
                            s.activation(stat2[:, 28:29], stat2[:, 27:28], AF.Exp,
                                         scale=-0.5)
                            if u >= 2:
                                R["hOUT"].wait(s, u - 2)
                            s.activation(outb[tt % 2][:], cent2[:], AF.Copy,
                                         scale=stat2[:, 28:29])
                            R["hOUT"].attach(raw.dma_start(
                                out=out_own[tg * 512 + tt * 128:
                                            tg * 512 + (tt + 1) * 128, :],
                                in_=outb[tt % 2][:]))
                    assert C["scF"] == sb2 + 3 * NTG * 4

    return nc


def kernel(L0_attn_x, L0_attn_W_Q, L0_attn_W_K, L0_attn_W_V, L0_attn_W_O,
           L0_ffn_gate_w, L0_ffn_up_w, L0_ffn_down_w):
    S_full, S_own = 4096, 2048
    x = np.ascontiguousarray(np.asarray(L0_attn_x, dtype=np.float32))
    WQ = np.asarray(L0_attn_W_Q, dtype=np.float32)
    WK = np.asarray(L0_attn_W_K, dtype=np.float32)
    WV = np.asarray(L0_attn_W_V, dtype=np.float32)
    WO = np.asarray(L0_attn_W_O, dtype=np.float32)
    gwv = np.ascontiguousarray(np.asarray(L0_ffn_gate_w, dtype=np.float32))
    uwv = np.ascontiguousarray(np.asarray(L0_ffn_up_w, dtype=np.float32))
    dwv = np.asarray(L0_ffn_down_w, dtype=np.float32)

    perm_q = (np.arange(128)[None, :] * 16 + np.arange(16)[:, None]).reshape(-1)
    perm_k = (np.arange(128)[None, :] * 8 + np.arange(8)[:, None]).reshape(-1)
    wqt = np.ascontiguousarray(WQ[perm_q, :].T)
    wkt = np.ascontiguousarray(WK[perm_k, :].T)
    wvt = np.ascontiguousarray(WV[perm_k, :].T)
    wot = np.ascontiguousarray(WO.T)
    down_wr = np.ascontiguousarray(dwv.reshape(EMB, NJC, 128).transpose(1, 0, 2))

    half = 64
    inv_freq = (1.0 / 10000.0 ** (np.arange(0, half, dtype=np.float32) * 2.0 / 128.0)
                ).astype(np.float32)
    pos = np.arange(S_full, dtype=np.float32)
    ang = pos[:, None] * inv_freq[None, :]
    cos_t = np.ascontiguousarray(
        np.concatenate([np.cos(ang), np.cos(ang)], axis=1).astype(np.float32))
    sin_t = np.ascontiguousarray(
        np.concatenate([np.sin(ang), np.sin(ang)], axis=1).astype(np.float32))

    nc = build_nc(S_full, S_own)
    in_maps = []
    for c in range(8):
        b, hh = c // 2, c % 2
        in_maps.append({
            "x_seq": np.ascontiguousarray(x[b]),
            "x_own": np.ascontiguousarray(x[b, hh * S_own:(hh + 1) * S_own]),
            "wqt": wqt, "wkt": wkt, "wvt": wvt, "wot": wot,
            "gate_w": gwv, "up_w": uwv, "down_wr": down_wr,
            "cos_t": cos_t, "sin_t": sin_t,
            "cos_o": np.ascontiguousarray(cos_t[hh * S_own:(hh + 1) * S_own]),
            "sin_o": np.ascontiguousarray(sin_t[hh * S_own:(hh + 1) * S_own]),
        })
    res = run_bass_kernel_spmd(nc, in_maps, core_ids=list(range(8)))
    out = np.empty((4, S_full, EMB), dtype=np.float32)
    for c in range(8):
        b, hh = c // 2, c % 2
        out[b, hh * S_own:(hh + 1) * S_own] = res.results[c]["out_own"]
    return out


# version 19
# speedup vs baseline: 1.0686x; 1.0088x over previous
import numpy as np
from contextlib import ExitStack

import concourse.bass as bass
import concourse.mybir as mybir
from concourse.bass_utils import run_bass_kernel_spmd

F32 = mybir.dt.float32
BF16 = mybir.dt.bfloat16
F8 = mybir.dt.float8e4
AF = mybir.ActivationFunctionType
ALU = mybir.AluOpType
AX = mybir.AxisListType

EMB = 2048
HEADS = 16
KVH = 8
D = 128
IFF = 8192
NJC = IFF // 128
SCALE_S = float(1.0 / np.sqrt(np.float32(2048.0)))
EXP_SHIFT = -5.0
LN_EPS = 1e-5
INV_224 = float(np.float32(1.0) / np.float32(224.0))
QEPS2 = 2e-30


class Ring:

    def __init__(self, mksem, name, slots):
        self.sems = [mksem(name + str(i)) for i in range(slots)]
        self.slots = slots
        self.issued = 0

    def attach(self, instr):
        instr.then_inc(self.sems[self.issued % self.slots], 16)
        self.issued += 1
        return instr

    def wait(self, eng, i):
        eng.wait_ge(self.sems[i % self.slots], 16 * (i // self.slots + 1))

    def wait_all(self, eng):
        n = self.issued
        for si in range(self.slots):
            c = (n - si + self.slots - 1) // self.slots
            if c > 0:
                eng.wait_ge(self.sems[si], 16 * c)


class Fenced:

    def __init__(self, eng, sem, cnt, key):
        self._eng = eng
        self._sem = sem
        self._cnt = cnt
        self._key = key

    def _f(self, ins):
        self._cnt[self._key] += 1
        ins.then_inc(self._sem, 1)
        self._eng.wait_ge(self._sem, self._cnt[self._key])
        return ins

    def wait_ge(self, sem, val):
        self._eng.wait_ge(sem, val)

    def __getattr__(self, name):
        fn = getattr(self._eng, name)

        def call(*a, **k):
            return self._f(fn(*a, **k))
        return call


def _rope_tm(v, ps, cs, sn, out, temps, nh):
    kk = ps[:].rearrange("p (h d) -> p h d", h=nh)
    ka, kb = kk[:, :, 0:64], kk[:, :, 64:128]
    csA = cs[:, 0:64].unsqueeze(1).broadcast_to([128, nh, 64])
    csB = cs[:, 64:128].unsqueeze(1).broadcast_to([128, nh, 64])
    snA = sn[:, 0:64].unsqueeze(1).broadcast_to([128, nh, 64])
    snB = sn[:, 64:128].unsqueeze(1).broadcast_to([128, nh, 64])
    r0 = temps[0][:].rearrange("p (h d) -> p h d", h=nh)
    r1 = temps[1][:].rearrange("p (h d) -> p h d", h=nh)
    ko = out.rearrange("p (h d) -> p h d", h=nh)
    v.tensor_tensor(r0, ka, csA, ALU.mult)
    v.tensor_tensor(r1, kb, snA, ALU.mult)
    v.tensor_tensor(ko[:, :, 0:64], r0, r1, ALU.subtract)
    v.tensor_tensor(r0, kb, csB, ALU.mult)
    v.tensor_tensor(r1, ka, snB, ALU.mult)
    v.tensor_tensor(ko[:, :, 64:128], r0, r1, ALU.add)


ROPE_OPS = 6


def _quant(v, src_f32, nb, tmp_v, tmp_q8, red, s2, rcp, out_bf):
    s3 = src_f32.rearrange("p (b r) -> p b r", r=32)
    v.tensor_reduce(red[:, 0:nb], s3, axis=AX.X, op=ALU.max,
                    apply_absolute_value=True)
    v.tensor_scalar(s2[:, 0:nb], red[:, 0:nb], INV_224, None, ALU.mult)
    v.tensor_scalar_max(s2[:, 0:nb], s2[:, 0:nb], QEPS2)
    v.reciprocal(rcp[:, 0:nb], s2[:, 0:nb])
    rb = rcp[:, 0:nb].unsqueeze(2).broadcast_to([128, nb, 32])
    s2b = s2[:, 0:nb].unsqueeze(2).broadcast_to([128, nb, 32])
    t3 = tmp_v.rearrange("p (b r) -> p b r", r=32)
    q3 = tmp_q8.rearrange("p (b r) -> p b r", r=32)
    v.tensor_tensor(t3, s3, rb, ALU.mult)
    v.tensor_copy(tmp_q8, tmp_v)
    o3 = out_bf.rearrange("p (b r) -> p b r", r=32)
    v.tensor_tensor(o3, q3, s2b, ALU.mult)


QUANT_OPS = 7


def build_nc(S_full=4096, S_own=2048):
    assert S_full % 256 == 0 and S_own % 512 == 0
    NT_F = S_full // 128
    NT_O = S_own // 128
    NG = S_own // 512
    NKC = S_full // 128
    NPAIR = NKC // 2
    NTG = S_own // 512
    NTT = NT_O
    NE = HEADS * NPAIR

    nc = bass.Bass("TRN2", target_bir_lowering=False)

    x_seq = nc.dram_tensor("x_seq", [S_full, EMB], F32, kind="ExternalInput").ap()
    x_own = nc.dram_tensor("x_own", [S_own, EMB], F32, kind="ExternalInput").ap()
    wqt = nc.dram_tensor("wqt", [EMB, EMB], F32, kind="ExternalInput").ap()
    wkt = nc.dram_tensor("wkt", [EMB, KVH * D], F32, kind="ExternalInput").ap()
    wvt = nc.dram_tensor("wvt", [EMB, KVH * D], F32, kind="ExternalInput").ap()
    wot = nc.dram_tensor("wot", [EMB, EMB], F32, kind="ExternalInput").ap()
    gate_w = nc.dram_tensor("gate_w", [IFF, EMB], F32, kind="ExternalInput").ap()
    up_w = nc.dram_tensor("up_w", [IFF, EMB], F32, kind="ExternalInput").ap()
    down_wr = nc.dram_tensor("down_wr", [NJC, EMB, 128], F32, kind="ExternalInput").ap()
    cos_t = nc.dram_tensor("cos_t", [S_full, D], F32, kind="ExternalInput").ap()
    sin_t = nc.dram_tensor("sin_t", [S_full, D], F32, kind="ExternalInput").ap()
    cos_o = nc.dram_tensor("cos_o", [S_own, D], F32, kind="ExternalInput").ap()
    sin_o = nc.dram_tensor("sin_o", [S_own, D], F32, kind="ExternalInput").ap()
    out_own = nc.dram_tensor("out_own", [S_own, EMB], F32, kind="ExternalOutput").ap()
    mid_dram = nc.dram_tensor("mid_dram", [S_own, EMB], F32).ap()
    wq_bf_dram = nc.dram_tensor("wq_bf_dram", [EMB, EMB], BF16).ap()
    xmid_dram = nc.dram_tensor("xmid_dram", [S_own, EMB], F32).ap()
    hq_dram = nc.dram_tensor("hq_dram", [NJC, 128, S_own], BF16).ap()
    dq_dram = nc.dram_tensor("dq_dram", [NJC, 128, EMB], BF16).ap()

    with ExitStack() as top:
        E = top.enter_context
        mksem = lambda n: E(nc.semaphore(n))
        ones_bf = E(nc.sbuf_tensor("ones_bf", [128, 128], BF16))
        bias_shift = E(nc.sbuf_tensor("bias_shift", [128, 1], F32))
        bias_eps = E(nc.sbuf_tensor("bias_eps", [128, 1], F32))

        eng_names = ["akMM", "avMM", "bWQC", "bS", "bPV", "bTO",
                     "gMM", "hMM", "vF", "scF", "gF"]
        S = {n: mksem(n) for n in eng_names}
        C = {n: 0 for n in eng_names}

        R = {}
        for name, slots in [
            ("akW", 1), ("akX", 2), ("akTX", 2), ("akCS", 2), ("akSN", 2), ("akKT", 2),
            ("akWQ", 2), ("akWS", 2),
            ("avW", 1), ("avX", 2), ("avTX", 2),
            ("bX", 2), ("bTX", 2), ("bCS", 2), ("bSN", 2), ("bWQ", 3), ("bQT", 2),
            ("bWO", 2), ("bMS", 2),
            ("fM", 3), ("fX", 3), ("fXM", 2), ("fXQT", 2),
            ("gG", 2), ("gU", 2), ("gD", 2), ("gTG", 2), ("gTU", 2), ("gTD", 2),
            ("gDS", 2), ("gHT", 2), ("gHS", 2),
            ("hH", 3), ("hD", 3), ("hXM", 4), ("hOUT", 2),
        ]:
            R[name] = Ring(mksem, name, slots)

        with nc.Block() as blk:
            @blk.gpsimd
            def _(g):
                g.memset(ones_bf[:], 1.0)
                g.memset(bias_shift[:], EXP_SHIFT)
                g.memset(bias_eps[:], LN_EPS)

        with ExitStack() as att_scope:
            Ea = att_scope.enter_context
            k_fm = Ea(nc.sbuf_tensor("k_fm", [128, KVH * S_full], BF16))
            v_tm = Ea(nc.sbuf_tensor("v_tm", [128, NKC * KVH * D], BF16))

            vbAK = C["vF"]

            def akRP(t):
                return vbAK + ROPE_OPS * (t + 1)

            with ExitStack() as pa:
                Ek = pa.enter_context
                wkt_sb = Ek(nc.sbuf_tensor("wkt_sb", [128, 16 * KVH * D], BF16))
                xb = [Ek(nc.sbuf_tensor(f"akxb{i}", [128, EMB], BF16)) for i in range(2)]
                xfm = [Ek(nc.sbuf_tensor(f"akxf{i}", [128, EMB], BF16)) for i in range(2)]
                krot = [Ek(nc.sbuf_tensor(f"akkr{i}", [128, KVH * D], BF16)) for i in range(2)]
                cs_sb = [Ek(nc.sbuf_tensor(f"akcs{i}", [128, D], F32)) for i in range(2)]
                sn_sb = [Ek(nc.sbuf_tensor(f"aksn{i}", [128, D], F32)) for i in range(2)]
                rt = [Ek(nc.sbuf_tensor(f"akrt{i}", [128, KVH * 64], F32)) for i in range(2)]
                wqstg = [Ek(nc.sbuf_tensor(f"akwq{i}", [128, EMB], BF16)) for i in range(2)]
                ps_k = [Ek(nc.psum_tensor(f"ps_k{i}", [128, KVH * D], F32)) for i in range(2)]

                with nc.Block() as blk:
                    @blk.gpsimd
                    def _(g):
                        R["akW"].attach(
                            g.dma_start(out=wkt_sb[:].rearrange("p (i o) -> p i o", i=16),
                                        in_=wkt.rearrange("(i p) o -> p i o", p=128)))
                        for t in range(NT_F):
                            if t >= 2:
                                R["akTX"].wait(g, t - 2)
                            R["akX"].attach(
                                g.dma_start(out=xb[t % 2][:],
                                            in_=x_seq[t * 128:(t + 1) * 128, :]))
                        for i in range(16):
                            if i >= 2:
                                R["akWS"].wait(g, i - 2)
                            R["akWQ"].attach(
                                g.dma_start(out=wqstg[i % 2][:],
                                            in_=wqt[i * 128:(i + 1) * 128, :]))
                        R["akX"].wait_all(g)
                        R["akW"].wait_all(g)

                    @blk.sync
                    def _(s):
                        for t in range(NT_F + 1):
                            if t < NT_F:
                                R["akX"].wait(s, t)
                                if t >= 2:
                                    s.wait_ge(S["akMM"], t - 1)
                                R["akTX"].attach(s.dma_start(
                                    out=xfm[t % 2][:].rearrange("p (b f) -> p b f", b=16),
                                    in_=xb[t % 2][:].rearrange("p (b f) -> p b f", b=16),
                                    transpose=True))
                                if t >= 2:
                                    s.wait_ge(S["vF"], akRP(t - 2))
                                R["akCS"].attach(
                                    s.dma_start(out=cs_sb[t % 2][:],
                                                in_=cos_t[t * 128:(t + 1) * 128, :]))
                                R["akSN"].attach(
                                    s.dma_start(out=sn_sb[t % 2][:],
                                                in_=sin_t[t * 128:(t + 1) * 128, :]))
                            if t >= 1:
                                tk = t - 1
                                s.wait_ge(S["vF"], akRP(tk))
                                R["akKT"].attach(s.dma_start(
                                    out=k_fm[:].rearrange("p (h s) -> p h s", h=KVH)
                                        [:, :, tk * 128:(tk + 1) * 128],
                                    in_=krot[tk % 2][:].rearrange("p (h f) -> p h f", h=KVH),
                                    transpose=True))
                        for i in range(16):
                            R["akWQ"].wait(s, i)
                            R["akWS"].attach(s.dma_start(
                                out=wq_bf_dram[i * 128:(i + 1) * 128, :],
                                in_=wqstg[i % 2][:]))
                        for rn in ("akTX", "akCS", "akSN", "akKT", "akWQ", "akWS"):
                            R[rn].wait_all(s)

                    @blk.tensor
                    def _(te):
                        R["akW"].wait(te, 0)
                        for t in range(NT_F):
                            R["akTX"].wait(te, t)
                            if t >= 2:
                                te.wait_ge(S["vF"], akRP(t - 2))
                            for i in range(16):
                                st, sp = (i == 0), (i == 15)
                                lhs = xfm[t % 2][:, i * 128:(i + 1) * 128]
                                te.matmul(ps_k[t % 2][:, 0:512], lhs,
                                          wkt_sb[:, i * 1024:i * 1024 + 512],
                                          start=st, stop=sp)
                                ins = te.matmul(ps_k[t % 2][:, 512:1024], lhs,
                                                wkt_sb[:, i * 1024 + 512:(i + 1) * 1024],
                                                start=st, stop=sp)
                                if sp:
                                    ins.then_inc(S["akMM"], 1)
                        C["akMM"] = NT_F

                    @blk.vector
                    def _(v):
                        v = Fenced(v, S["vF"], C, "vF")
                        for t in range(NT_F):
                            v.wait_ge(S["akMM"], t + 1)
                            R["akCS"].wait(v, t)
                            R["akSN"].wait(v, t)
                            if t >= 2:
                                R["akKT"].wait(v, t - 2)
                            _rope_tm(v, ps_k[t % 2], cs_sb[t % 2][:],
                                     sn_sb[t % 2][:], krot[t % 2][:], rt, KVH)
                        assert C["vF"] == vbAK + ROPE_OPS * NT_F

            sbAV = C["scF"]
            with ExitStack() as pv:
                Ev = pv.enter_context
                wvt_sb = Ev(nc.sbuf_tensor("wvt_sb", [128, 16 * KVH * D], BF16))
                xb = [Ev(nc.sbuf_tensor(f"avxb{i}", [128, EMB], BF16)) for i in range(2)]
                xfm = [Ev(nc.sbuf_tensor(f"avxf{i}", [128, EMB], BF16)) for i in range(2)]
                ps_v = [Ev(nc.psum_tensor(f"ps_v{i}", [128, KVH * D], F32)) for i in range(2)]

                with nc.Block() as blk:
                    @blk.gpsimd
                    def _(g):
                        R["avW"].attach(
                            g.dma_start(out=wvt_sb[:].rearrange("p (i o) -> p i o", i=16),
                                        in_=wvt.rearrange("(i p) o -> p i o", p=128)))
                        for t in range(NT_F):
                            if t >= 2:
                                R["avTX"].wait(g, t - 2)
                            R["avX"].attach(
                                g.dma_start(out=xb[t % 2][:],
                                            in_=x_seq[t * 128:(t + 1) * 128, :]))
                        R["avX"].wait_all(g)
                        R["avW"].wait_all(g)

                    @blk.sync
                    def _(s):
                        for t in range(NT_F):
                            R["avX"].wait(s, t)
                            if t >= 2:
                                s.wait_ge(S["avMM"], t - 1)
                            R["avTX"].attach(s.dma_start(
                                out=xfm[t % 2][:].rearrange("p (b f) -> p b f", b=16),
                                in_=xb[t % 2][:].rearrange("p (b f) -> p b f", b=16),
                                transpose=True))
                        R["avTX"].wait_all(s)

                    @blk.tensor
                    def _(te):
                        R["avW"].wait(te, 0)
                        for t in range(NT_F):
                            R["avTX"].wait(te, t)
                            if t >= 2:
                                te.wait_ge(S["scF"], sbAV + t - 1)
                            for i in range(16):
                                st, sp = (i == 0), (i == 15)
                                lhs = xfm[t % 2][:, i * 128:(i + 1) * 128]
                                te.matmul(ps_v[t % 2][:, 0:512], lhs,
                                          wvt_sb[:, i * 1024:i * 1024 + 512],
                                          start=st, stop=sp)
                                ins = te.matmul(ps_v[t % 2][:, 512:1024], lhs,
                                                wvt_sb[:, i * 1024 + 512:(i + 1) * 1024],
                                                start=st, stop=sp)
                                if sp:
                                    ins.then_inc(S["avMM"], 1)
                        C["avMM"] = NT_F

                    @blk.scalar
                    def _(s):
                        s = Fenced(s, S["scF"], C, "scF")
                        for t in range(NT_F):
                            s.wait_ge(S["avMM"], t + 1)
                            s.copy(v_tm[:, t * 1024:(t + 1) * 1024], ps_v[t % 2][:])
                        assert C["scF"] == sbAV + NT_F

            q_fm = Ea(nc.sbuf_tensor("q_fm", [128, HEADS * 512], BF16))
            ctx_fm = Ea(nc.sbuf_tensor("ctx_fm", [128, HEADS * 512], BF16))

            for gi in range(NG):
                q0 = gi * 512
                bWQC0 = gi * 32
                bS0 = bPV0 = gi * NE
                bTO0 = gi * 16

                vbB1 = C["vF"]

                def bQR(tl, _vb=vbB1):
                    return _vb + ROPE_OPS * (tl + 1)

                with ExitStack() as pb, nc.Block() as blk:
                    Eb = pb.enter_context
                    xb2 = [Eb(nc.sbuf_tensor(f"b1xb{gi}_{i}", [128, EMB], BF16)) for i in range(2)]
                    xfm2 = [Eb(nc.sbuf_tensor(f"b1xf{gi}_{i}", [128, EMB], BF16)) for i in range(2)]
                    qrot = [Eb(nc.sbuf_tensor(f"b1qr{gi}_{i}", [128, EMB], BF16)) for i in range(2)]
                    wq_buf = [Eb(nc.sbuf_tensor(f"b1wq{gi}_{i}", [128, EMB], BF16)) for i in range(3)]
                    cs2 = [Eb(nc.sbuf_tensor(f"b1cs{gi}_{i}", [128, D], F32)) for i in range(2)]
                    sn2 = [Eb(nc.sbuf_tensor(f"b1sn{gi}_{i}", [128, D], F32)) for i in range(2)]
                    rt2 = [Eb(nc.sbuf_tensor(f"b1rt{gi}_{i}", [128, HEADS * 64], F32)) for i in range(2)]
                    ps_q = [Eb(nc.psum_tensor(f"b1pq{gi}_{i}", [128, EMB], F32)) for i in range(2)]

                    @blk.gpsimd
                    def _(g):
                        for tl in range(4):
                            if tl >= 2:
                                R["bTX"].wait(g, gi * 4 + tl - 2)
                            R["bX"].attach(
                                g.dma_start(out=xb2[tl % 2][:],
                                            in_=x_own[q0 + tl * 128:q0 + (tl + 1) * 128, :]))
                        for k in range(32):
                            i = k % 16
                            if k >= 3:
                                g.wait_ge(S["bWQC"], bWQC0 + k - 2)
                            R["bWQ"].attach(
                                g.dma_start(out=wq_buf[k % 3][:],
                                            in_=wq_bf_dram[i * 128:(i + 1) * 128, :]))
                        R["bX"].wait_all(g)
                        R["bWQ"].wait_all(g)

                    @blk.sync
                    def _(s):
                        for tl in range(4):
                            R["bX"].wait(s, gi * 4 + tl)
                            if tl >= 2:
                                s.wait_ge(S["bWQC"], bWQC0 + 16 * ((tl - 2) // 2 + 1))
                            R["bTX"].attach(s.dma_start(
                                out=xfm2[tl % 2][:].rearrange("p (b f) -> p b f", b=16),
                                in_=xb2[tl % 2][:].rearrange("p (b f) -> p b f", b=16),
                                transpose=True))
                            if tl >= 2:
                                s.wait_ge(S["vF"], bQR(tl - 2))
                            R["bCS"].attach(
                                s.dma_start(out=cs2[tl % 2][:],
                                            in_=cos_o[q0 + tl * 128:q0 + (tl + 1) * 128, :]))
                            R["bSN"].attach(
                                s.dma_start(out=sn2[tl % 2][:],
                                            in_=sin_o[q0 + tl * 128:q0 + (tl + 1) * 128, :]))
                            if tl >= 1:
                                tk = tl - 1
                                s.wait_ge(S["vF"], bQR(tk))
                                R["bQT"].attach(s.dma_start(
                                    out=q_fm[:].rearrange("p (h s) -> p h s", h=HEADS)
                                        [:, :, tk * 128:(tk + 1) * 128],
                                    in_=qrot[tk % 2][:].rearrange("p (h f) -> p h f", h=HEADS),
                                    transpose=True))
                        tk = 3
                        s.wait_ge(S["vF"], bQR(tk))
                        R["bQT"].attach(s.dma_start(
                            out=q_fm[:].rearrange("p (h s) -> p h s", h=HEADS)
                                [:, :, tk * 128:(tk + 1) * 128],
                            in_=qrot[tk % 2][:].rearrange("p (h f) -> p h f", h=HEADS),
                            transpose=True))
                        for rn in ("bTX", "bCS", "bSN", "bQT"):
                            R[rn].wait_all(s)

                    @blk.tensor
                    def _(te):
                        for sw in range(2):
                            tls = (2 * sw, 2 * sw + 1)
                            for i in range(16):
                                k = sw * 16 + i
                                R["bWQ"].wait(te, gi * 32 + k)
                                st, sp = (i == 0), (i == 15)
                                for tl in tls:
                                    if st:
                                        R["bTX"].wait(te, gi * 4 + tl)
                                        if tl >= 2:
                                            te.wait_ge(S["vF"], bQR(tl - 2))
                                    lhs = xfm2[tl % 2][:, i * 128:(i + 1) * 128]
                                    last = None
                                    for oc in range(4):
                                        last = te.matmul(
                                            ps_q[tl % 2][:, oc * 512:(oc + 1) * 512], lhs,
                                            wq_buf[k % 3][:, oc * 512:(oc + 1) * 512],
                                            start=st, stop=sp)
                                    if tl == tls[1]:
                                        last.then_inc(S["bWQC"], 1)
                        C["bWQC"] = bWQC0 + 32

                    @blk.vector
                    def _(v):
                        v = Fenced(v, S["vF"], C, "vF")
                        for tl in range(4):
                            v.wait_ge(S["bWQC"], bWQC0 + 16 * (tl // 2 + 1))
                            R["bCS"].wait(v, gi * 4 + tl)
                            R["bSN"].wait(v, gi * 4 + tl)
                            if tl >= 2:
                                R["bQT"].wait(v, gi * 4 + tl - 2)
                            _rope_tm(v, ps_q[tl % 2], cs2[tl % 2][:],
                                     sn2[tl % 2][:], qrot[tl % 2][:], rt2, HEADS)
                        assert C["vF"] == vbB1 + ROPE_OPS * 4

                vbB2 = C["vF"]
                sbB2 = C["scF"]

                def bRC(h, _vb=vbB2):
                    return _vb + 2 * h + 1

                def bVC(h, _vb=vbB2):
                    return _vb + 2 * (h + 1)

                def bEX(e, _sb=sbB2):
                    return _sb + e + 1

                def bOE(j, _sb=sbB2):
                    return _sb + NE + j + 1

                with ExitStack() as pb, nc.Block() as blk:
                    Eb = pb.enter_context
                    pT = [Eb(nc.sbuf_tensor(f"b2pT{gi}_{i}", [128, 1024], BF16)) for i in range(2)]
                    rbc = Eb(nc.sbuf_tensor(f"b2rb{gi}", [128, 512], F32))
                    wo_buf = [Eb(nc.sbuf_tensor(f"b2wo{gi}_{i}", [128, 16 * 512], BF16))
                              for i in range(2)]
                    midb = [Eb(nc.sbuf_tensor(f"b2mb{gi}_{i}", [128, 512], F32)) for i in range(2)]
                    ps_s = [Eb(nc.psum_tensor(f"b2ps{gi}_{i}", [128, 1024], F32)) for i in range(2)]
                    ps_ctx = Eb(nc.psum_tensor(f"b2pc{gi}", [128, 512], F32))
                    ps_row = Eb(nc.psum_tensor(f"b2prw{gi}", [128, 512], F32))
                    ps_o = [Eb(nc.psum_tensor(f"b2po{gi}_{i}", [128, 512], F32)) for i in range(2)]

                    @blk.gpsimd
                    def _(g):
                        for oc in range(4):
                            if oc >= 2:
                                g.wait_ge(S["bTO"], bTO0 + 4 * (oc - 1))
                            R["bWO"].attach(g.dma_start(
                                out=wo_buf[oc % 2][:].rearrange("p (i o) -> p i o", i=16),
                                in_=wot.rearrange("(i p) o -> p i o", p=128)
                                    [:, :, oc * 512:(oc + 1) * 512]))
                        R["bWO"].wait_all(g)

                    @blk.tensor
                    def _(te):
                        R["bQT"].wait(te, gi * 4 + 3)
                        R["bQT"].wait(te, gi * 4 + 2)

                        def emit_S(e):
                            h, pr = e // NPAIR, e % NPAIR
                            kv = h // 2
                            if e >= 2:
                                te.wait_ge(S["scF"], bEX(e - 2))
                            last = None
                            for half in range(2):
                                kc = 2 * pr + half
                                last = te.matmul(
                                    ps_s[e % 2][:, half * 512:half * 512 + 512],
                                    k_fm[:, kv * S_full + kc * 128:kv * S_full + (kc + 1) * 128],
                                    q_fm[:, h * 512:(h + 1) * 512],
                                    start=True, stop=True)
                            last.then_inc(S["bS"], 1)

                        def emit_PV(e):
                            h, pr = e // NPAIR, e % NPAIR
                            kv = h // 2
                            te.wait_ge(S["scF"], bEX(e))
                            if pr == 0 and h >= 1:
                                te.wait_ge(S["vF"], bVC(h - 1))
                            last = None
                            for half in range(2):
                                kc = 2 * pr + half
                                st, sp = (kc == 0), (kc == NKC - 1)
                                psl = pT[e % 2][:, half * 512:half * 512 + 512]
                                te.matmul(ps_ctx[:],
                                          v_tm[:, kc * 1024 + kv * 128:kc * 1024 + (kv + 1) * 128],
                                          psl, start=st, stop=sp)
                                last = te.matmul(ps_row[:], ones_bf[:], psl,
                                                 start=st, stop=sp)
                            last.then_inc(S["bPV"], 1)

                        for e in range(NE):
                            emit_S(e)
                            if e >= 1:
                                emit_PV(e - 1)
                        emit_PV(NE - 1)
                        C["bS"] = bS0 + NE
                        C["bPV"] = bPV0 + NE

                        te.wait_ge(S["vF"], bVC(HEADS - 1))
                        for oc in range(4):
                            R["bWO"].wait(te, gi * 4 + oc)
                            for qt in range(4):
                                j = 4 * oc + qt
                                if j >= 2:
                                    te.wait_ge(S["scF"], bOE(j - 2))
                                last = None
                                for i in range(16):
                                    last = te.matmul(
                                        ps_o[j % 2][:],
                                        ctx_fm[:, i * 512 + qt * 128:i * 512 + (qt + 1) * 128],
                                        wo_buf[oc % 2][:, i * 512:(i + 1) * 512],
                                        start=(i == 0), stop=(i == 15))
                                last.then_inc(S["bTO"], 1)
                        C["bTO"] = bTO0 + 16

                    @blk.scalar
                    def _(s):
                        s = Fenced(s, S["scF"], C, "scF")
                        for e in range(NE):
                            s.wait_ge(S["bS"], bS0 + e + 1)
                            if e >= 2:
                                s.wait_ge(S["bPV"], bPV0 + e - 1)
                            s.activation(pT[e % 2][:], ps_s[e % 2][:], AF.Exp,
                                         bias=bias_shift[:], scale=SCALE_S)
                        for oc in range(4):
                            for qt in range(4):
                                j = 4 * oc + qt
                                s.wait_ge(S["bTO"], bTO0 + j + 1)
                                if j >= 2:
                                    R["bMS"].wait(s, gi * 16 + j - 2)
                                s.copy(midb[j % 2][:], ps_o[j % 2][:])
                        assert C["scF"] == sbB2 + NE + 16

                    @blk.vector
                    def _(v):
                        v = Fenced(v, S["vF"], C, "vF")
                        for h in range(HEADS):
                            v.wait_ge(S["bPV"], bPV0 + (h + 1) * NPAIR)
                            v.reciprocal(rbc[:], ps_row[:])
                            v.tensor_tensor(ctx_fm[:, h * 512:(h + 1) * 512],
                                            ps_ctx[:], rbc[:], ALU.mult)
                        assert C["vF"] == vbB2 + 2 * HEADS

                    @blk.sync
                    def _(s):
                        for oc in range(4):
                            for qt in range(4):
                                j = 4 * oc + qt
                                s.wait_ge(S["scF"], bOE(j))
                                R["bMS"].attach(s.dma_start(
                                    out=mid_dram[q0 + qt * 128:q0 + (qt + 1) * 128,
                                                 oc * 512:(oc + 1) * 512],
                                    in_=midb[j % 2][:]))
                        R["bMS"].wait_all(s)

        vb0 = C["vF"]
        sb0 = C["scF"]
        gb0 = C["gF"]

        def fB(t):
            return 0 if t == 0 else 4 + 8 * (t - 1)

        def vCent(t):
            return vb0 + fB(t) + 4

        def vQpre(t):
            return vb0 + (8 * t + 12 if t < NT_O - 1 else 8 * NT_O)

        def scLn(t):
            return sb0 + (1 if t == 0 else 4 * t)

        def sLN3(t):
            return sb0 + (3 if t == 0 else 4 * t + 2)

        def sCast(t):
            return sb0 + (4 * t + 7 if t < NT_O - 1 else 4 * NT_O)

        def gQtt(t):
            return gb0 + 2 * t + 1

        def gDeq(t):
            return gb0 + 2 * t + 2

        ffn_scope = ExitStack()
        Ef = ffn_scope.enter_context
        xq_fm = Ef(nc.sbuf_tensor("xq_fm", [128, 16 * S_own], BF16))
        with ExitStack() as p0:
            E0 = p0.enter_context
            midt = [E0(nc.sbuf_tensor(f"f0m{i}", [128, EMB], F32)) for i in range(3)]
            xt = [E0(nc.sbuf_tensor(f"f0x{i}", [128, EMB], F32)) for i in range(3)]
            pre = E0(nc.sbuf_tensor("f0pre", [128, EMB], F32))
            junk = E0(nc.sbuf_tensor("f0junk", [128, EMB], F32))
            cent = [E0(nc.sbuf_tensor(f"f0c{i}", [128, EMB], F32)) for i in range(2)]
            xmid = [E0(nc.sbuf_tensor(f"f0xm{i}", [128, EMB], F32)) for i in range(2)]
            vq = [E0(nc.sbuf_tensor(f"f0vq{i}", [128, EMB], F32)) for i in range(2)]
            q8 = [E0(nc.sbuf_tensor(f"f0q8{i}", [128, EMB], F8)) for i in range(2)]
            xqt = [E0(nc.sbuf_tensor(f"f0xq{i}", [128, EMB], BF16)) for i in range(2)]
            stat = E0(nc.sbuf_tensor("f0st", [128, 8], F32))
            red = [E0(nc.sbuf_tensor(f"f0rd{i}", [128, 64], F32)) for i in range(2)]
            s2 = [E0(nc.sbuf_tensor(f"f0s2{i}", [128, 64], F32)) for i in range(2)]
            rcp = [E0(nc.sbuf_tensor(f"f0rc{i}", [128, 64], F32)) for i in range(2)]

            with nc.Block() as blk:
                @blk.sync
                def _(s):
                    def emit_transp(t):
                        s.wait_ge(S["gF"], gDeq(t))
                        R["fXQT"].attach(s.dma_start(
                            out=xq_fm[:].rearrange("p (b s) -> p b s", b=16)
                                [:, :, t * 128:(t + 1) * 128],
                            in_=xqt[t % 2][:].rearrange("p (b f) -> p b f", b=16),
                            transpose=True))

                    for t in range(min(3, NT_O)):
                        R["fM"].attach(s.dma_start(out=midt[t % 3][:],
                                                   in_=mid_dram[t * 128:(t + 1) * 128, :]))
                        R["fX"].attach(s.dma_start(out=xt[t % 3][:],
                                                   in_=x_own[t * 128:(t + 1) * 128, :]))
                    for t in range(NT_O):
                        if t + 3 < NT_O:
                            s.wait_ge(S["vF"], vb0 + fB(t) + 1)
                            R["fM"].attach(s.dma_start(
                                out=midt[t % 3][:],
                                in_=mid_dram[(t + 3) * 128:(t + 4) * 128, :]))
                            R["fX"].attach(s.dma_start(
                                out=xt[t % 3][:],
                                in_=x_own[(t + 3) * 128:(t + 4) * 128, :]))
                        s.wait_ge(S["scF"], sLN3(t))
                        R["fXM"].attach(s.dma_start(out=xmid_dram[t * 128:(t + 1) * 128, :],
                                                    in_=xmid[t % 2][:]))
                        if t >= 1:
                            emit_transp(t - 1)
                    emit_transp(NT_O - 1)
                    for rn in ("fM", "fX", "fXM", "fXQT"):
                        R[rn].wait_all(s)

                @blk.vector
                def _(v):
                    v = Fenced(v, S["vF"], C, "vF")

                    def emit_prequant(t):
                        v.wait_ge(S["scF"], sLN3(t))
                        if t >= 2:
                            v.wait_ge(S["gF"], gDeq(t - 2))
                        x3 = xmid[t % 2][:].rearrange("p (b r) -> p b r", r=32)
                        v.tensor_reduce(red[t % 2][:], x3, axis=AX.X, op=ALU.max,
                                        apply_absolute_value=True)
                        v.tensor_scalar(s2[t % 2][:], red[t % 2][:], INV_224,
                                        None, ALU.mult)
                        v.tensor_scalar_max(s2[t % 2][:], s2[t % 2][:], QEPS2)
                        v.reciprocal(rcp[t % 2][:], s2[t % 2][:])

                    for t in range(NT_O):
                        R["fM"].wait(v, t)
                        R["fX"].wait(v, t)
                        v.scalar_tensor_tensor(pre[:], midt[t % 3][:], 1.0,
                                               xt[t % 3][:], ALU.mult, ALU.add,
                                               accum_out=stat[:, 0:1])
                        v.tensor_scalar(stat[:, 1:2], stat[:, 0:1],
                                        1.0 / EMB, None, ALU.mult)
                        if t >= 1:
                            v.wait_ge(S["scF"], scLn(t - 1))
                        v.scalar_tensor_tensor(junk[:], pre[:], stat[:, 1:2],
                                               pre[:], ALU.subtract, ALU.mult,
                                               accum_out=stat[:, 2:3])
                        v.tensor_scalar(cent[t % 2][:], pre[:], stat[:, 1:2], None,
                                        ALU.subtract)
                        if t >= 1:
                            emit_prequant(t - 1)
                    emit_prequant(NT_O - 1)
                    assert C["vF"] == vb0 + 8 * NT_O

                @blk.scalar
                def _(s):
                    s = Fenced(s, S["scF"], C, "scF")

                    def emit_cast(t):
                        s.wait_ge(S["gF"], gQtt(t))
                        if t >= 2:
                            s.wait_ge(S["gF"], gDeq(t - 2))
                        s.copy(q8[t % 2][:], vq[t % 2][:])

                    for t in range(NT_O):
                        s.wait_ge(S["vF"], vCent(t))
                        s.activation(stat[:, 3:4], stat[:, 2:3], AF.Ln,
                                     bias=bias_eps[:], scale=1.0 / EMB)
                        s.activation(stat[:, 4:5], stat[:, 3:4], AF.Exp, scale=-0.5)
                        if t >= 2:
                            R["fXM"].wait(s, t - 2)
                            s.wait_ge(S["gF"], gQtt(t - 2))
                            s.wait_ge(S["vF"], vQpre(t - 2))
                        s.activation(xmid[t % 2][:], cent[t % 2][:], AF.Copy,
                                     scale=stat[:, 4:5])
                        if t >= 1:
                            emit_cast(t - 1)
                    emit_cast(NT_O - 1)
                    assert C["scF"] == sb0 + 4 * NT_O

                @blk.gpsimd
                def _(g):
                    g = Fenced(g, S["gF"], C, "gF")
                    for t in range(NT_O):
                        g.wait_ge(S["vF"], vQpre(t))
                        g.wait_ge(S["scF"], sLN3(t))
                        if t >= 2:
                            g.wait_ge(S["scF"], sCast(t - 2))
                        x3 = xmid[t % 2][:].rearrange("p (b r) -> p b r", r=32)
                        v3 = vq[t % 2][:].rearrange("p (b r) -> p b r", r=32)
                        rb = rcp[t % 2][:].unsqueeze(2).broadcast_to([128, 64, 32])
                        g.tensor_tensor(v3, x3, rb, ALU.mult)
                        g.wait_ge(S["scF"], sCast(t))
                        if t >= 2:
                            R["fXQT"].wait(g, t - 2)
                        q3 = q8[t % 2][:].rearrange("p (b r) -> p b r", r=32)
                        o3 = xqt[t % 2][:].rearrange("p (b r) -> p b r", r=32)
                        s2b = s2[t % 2][:].unsqueeze(2).broadcast_to([128, 64, 32])
                        g.tensor_tensor(o3, q3, s2b, ALU.mult)
                    assert C["gF"] == gb0 + 2 * NT_O

        vb1 = C["vF"]
        sb1 = C["scF"]
        gb1 = C["gF"]

        def vTT(jc, m):
            if jc == 0:
                return vb1 + 4 * (m + 1)
            return vb1 + 12 * jc + 9 * max(0, jc - 2) + 4 * (m + 1)

        def _hS(m):
            if m <= NJC - 3:
                return vb1 + 21 * m + 36
            if m == NJC - 2:
                return vb1 + 21 * NJC - 18
            return vb1 + 21 * NJC - 9

        def vHmultA(m):
            return _hS(m) + 1

        def vHmultB(m):
            return _hS(m) + 2

        def vHpre(m):
            return _hS(m) + 7

        def vHdeq(m):
            return _hS(m) + 9

        def scC(jc, m):
            return sb1 + 3 * jc + 2 * max(0, jc - 1) + m + 1

        def scSiluA(m):
            return sb1 + (5 * m + 7 if m < NJC - 1 else 5 * NJC - 1)

        def scSiluB(m):
            return sb1 + (5 * m + 8 if m < NJC - 1 else 5 * NJC)

        def gTT(jc, m):
            return gb1 + (m + 1 if jc == 0 else 6 * jc + m + 1)

        def gDeq(jc, m):
            return gb1 + 6 * jc + m + 4

        with ExitStack() as p1:
            E1 = p1.enter_context
            gw = [E1(nc.sbuf_tensor(f"f1g{i}", [128, EMB], F32)) for i in range(2)]
            uw = [E1(nc.sbuf_tensor(f"f1u{i}", [128, EMB], F32)) for i in range(2)]
            dw = [E1(nc.sbuf_tensor(f"f1d{i}", [128, EMB], F32)) for i in range(2)]
            gq = [E1(nc.sbuf_tensor(f"f1gq{i}", [128, EMB], BF16)) for i in range(2)]
            uq = [E1(nc.sbuf_tensor(f"f1uq{i}", [128, EMB], BF16)) for i in range(2)]
            dq = [E1(nc.sbuf_tensor(f"f1dq{i}", [128, EMB], BF16)) for i in range(2)]
            guf = [E1(nc.sbuf_tensor(f"f1guf{i}", [128, 16 * 256], BF16)) for i in range(2)]
            dqf = [E1(nc.sbuf_tensor(f"f1dqf{i}", [128, EMB], BF16)) for i in range(2)]
            sil = E1(nc.sbuf_tensor("f1sil", [128, NTT * 128], F32))
            hf = E1(nc.sbuf_tensor("f1hf", [128, NTT * 128], F32))
            hqt = [E1(nc.sbuf_tensor(f"f1hqt{i}", [128, NTT * 128], BF16)) for i in range(2)]
            hqf = [E1(nc.sbuf_tensor(f"f1hqf{i}", [128, S_own], BF16)) for i in range(2)]
            Q8 = {m: E1(nc.sbuf_tensor(f"f1q8_{m}", [128, EMB], F8)) for m in "gudh"}
            NSB = {"g": 2, "u": 2, "d": 2, "h": 1}
            RD = {m: [E1(nc.sbuf_tensor(f"f1rd_{m}{i}", [128, 64], F32))
                      for i in range(NSB[m])] for m in "gudh"}
            RC = {m: [E1(nc.sbuf_tensor(f"f1rc_{m}{i}", [128, 64], F32))
                      for i in range(NSB[m])] for m in "gudh"}
            S2 = {m: [E1(nc.sbuf_tensor(f"f1s2_{m}{i}", [128, 64], F32))
                      for i in range(NSB[m])] for m in "gudh"}
            ps_gu = [E1(nc.psum_tensor(f"f1pgu{i}", [128, NTT * 128], F32))
                     for i in range(2)]
            HTT = NTT // 2

            def qreduce(eng, src, nb, m, bi):
                s3 = src.rearrange("p (b r) -> p b r", r=32)
                eng.tensor_reduce(RD[m][bi][:, 0:nb], s3, axis=AX.X, op=ALU.max,
                                  apply_absolute_value=True)

            def qsmalls(v, nb, m, bi):
                v.tensor_scalar(S2[m][bi][:, 0:nb], RD[m][bi][:, 0:nb], INV_224,
                                None, ALU.mult)
                v.tensor_scalar_max(S2[m][bi][:, 0:nb], S2[m][bi][:, 0:nb], QEPS2)
                v.reciprocal(RC[m][bi][:, 0:nb], S2[m][bi][:, 0:nb])

            def qtt(eng, src, nb, m, bi):
                s3 = src.rearrange("p (b r) -> p b r", r=32)
                rb = RC[m][bi][:, 0:nb].unsqueeze(2).broadcast_to([128, nb, 32])
                eng.tensor_tensor(s3, s3, rb, ALU.mult)

            def qpre(v, src, nb, m, bi):
                qreduce(v, src, nb, m, bi)
                qsmalls(v, nb, m, bi)
                qtt(v, src, nb, m, bi)

            def deq(eng, out_bf, nb, m, bi):
                s2b = S2[m][bi][:, 0:nb].unsqueeze(2).broadcast_to([128, nb, 32])
                q3 = Q8[m][:, 0:nb * 32].rearrange("p (b r) -> p b r", r=32)
                o3 = out_bf.rearrange("p (b r) -> p b r", r=32)
                eng.tensor_tensor(o3, q3, s2b, ALU.mult)

            with nc.Block() as blk:
                @blk.sync
                def _(s):
                    def emit_loads(jc):
                        if jc >= 2:
                            for mi in range(3):
                                s.wait_ge(S["scF"], scC(jc - 2, mi))
                        R["gG"].attach(s.dma_start(out=gw[jc % 2][:],
                                                   in_=gate_w[jc * 128:(jc + 1) * 128, :]))
                        R["gU"].attach(s.dma_start(out=uw[jc % 2][:],
                                                   in_=up_w[jc * 128:(jc + 1) * 128, :]))
                        R["gD"].attach(s.dma_start(
                            out=dw[jc % 2][:].rearrange("p (o j) -> p o j", o=16),
                            in_=down_wr[jc].rearrange("(o p) j -> p o j", p=128)))

                    def emit_hsync(m):
                        s.wait_ge(S["vF"], vHdeq(m))
                        if m >= 2:
                            R["gHS"].wait(s, m - 2)
                        R["gHT"].attach(s.dma_start(
                            out=hqf[m % 2][:].rearrange("p (b s) -> p b s", b=NTT),
                            in_=hqt[m % 2][:].rearrange("p (b f) -> p b f", b=NTT),
                            transpose=True))
                        R["gHT"].wait(s, m)
                        R["gHS"].attach(s.dma_start(out=hq_dram[m], in_=hqf[m % 2][:]))

                    emit_loads(0)
                    emit_loads(1)
                    for jc in range(NJC):
                        s.wait_ge(S["gF"], gDeq(jc, 0))
                        if jc >= 2:
                            s.wait_ge(S["gMM"], 2 * jc - 2)
                        R["gTG"].attach(s.dma_start(
                            out=guf[jc % 2][:].rearrange("p (i c) -> p i c", i=16)[:, :, 0:128],
                            in_=gq[jc % 2][:].rearrange("p (i c) -> p i c", i=16),
                            transpose=True))
                        s.wait_ge(S["gF"], gDeq(jc, 1))
                        R["gTU"].attach(s.dma_start(
                            out=guf[jc % 2][:].rearrange("p (i c) -> p i c", i=16)[:, :, 128:256],
                            in_=uq[jc % 2][:].rearrange("p (i c) -> p i c", i=16),
                            transpose=True))
                        s.wait_ge(S["gF"], gDeq(jc, 2))
                        if jc >= 2:
                            R["gDS"].wait(s, jc - 2)
                        R["gTD"].attach(s.dma_start(
                            out=dqf[jc % 2][:].rearrange("p (i c) -> p i c", i=16),
                            in_=dq[jc % 2][:].rearrange("p (i c) -> p i c", i=16),
                            transpose=True))
                        R["gTD"].wait(s, jc)
                        R["gDS"].attach(s.dma_start(out=dq_dram[jc], in_=dqf[jc % 2][:]))
                        if jc + 2 < NJC:
                            emit_loads(jc + 2)
                        if jc >= 2:
                            emit_hsync(jc - 2)
                    emit_hsync(NJC - 2)
                    emit_hsync(NJC - 1)
                    for rn in ("gG", "gU", "gD", "gTG", "gTU", "gTD", "gDS", "gHT", "gHS"):
                        R[rn].wait_all(s)

                @blk.vector
                def _(v):
                    v = Fenced(v, S["vF"], C, "vF")

                    def emit_hmult(m, hh):
                        v.wait_ge(S["scF"], (scSiluA if hh == 0 else scSiluB)(m))
                        pg = ps_gu[hh][:].rearrange("p (t c) -> p t c", t=HTT)
                        hs = hf[:, hh * HTT * 128:(hh + 1) * HTT * 128]
                        ss = sil[:, hh * HTT * 128:(hh + 1) * HTT * 128]
                        v.tensor_tensor(
                            hs.rearrange("p (t c) -> p t c", t=HTT),
                            ss.rearrange("p (t c) -> p t c", t=HTT),
                            pg[:, :, 128:256], ALU.mult)

                    def emit_hquant(m):
                        qpre(v, hf[:], NTT * 4, "h", 0)
                        v.tensor_copy(Q8["h"][:, 0:NTT * 128], hf[:])
                        if m >= 2:
                            R["gHT"].wait(v, m - 2)
                        deq(v, hqt[m % 2][:], NTT * 4, "h", 0)

                    def emit_qpre(v, jc, mi, m):
                        srcb = {"g": gw, "u": uw, "d": dw}[m][jc % 2]
                        R[{"g": "gG", "u": "gU", "d": "gD"}[m]].wait(v, jc)
                        if jc >= 2:
                            v.wait_ge(S["gF"], gDeq(jc - 2, mi))
                        qreduce(v, srcb[:], 64, m, jc % 2)
                        qsmalls(v, 64, m, jc % 2)

                    for mi, m in enumerate("gud"):
                        emit_qpre(v, 0, mi, m)
                    for jc in range(NJC):
                        if jc + 1 < NJC:
                            for mi, m in enumerate("gud"):
                                emit_qpre(v, jc + 1, mi, m)
                        if jc >= 1:
                            emit_hmult(jc - 1, 0)
                            emit_hmult(jc - 1, 1)
                            emit_hquant(jc - 1)
                    emit_hmult(NJC - 1, 0)
                    emit_hmult(NJC - 1, 1)
                    emit_hquant(NJC - 1)
                    assert C["vF"] == vb1 + 21 * NJC

                @blk.scalar
                def _(s):
                    s = Fenced(s, S["scF"], C, "scF")

                    def emit_silu(m, hh):
                        s.wait_ge(S["gMM"], 2 * m + hh + 1)
                        pg = ps_gu[hh][:].rearrange("p (t c) -> p t c", t=HTT)
                        ss = sil[:, hh * HTT * 128:(hh + 1) * HTT * 128]
                        s.activation(ss.rearrange("p (t c) -> p t c", t=HTT),
                                     pg[:, :, 0:128], AF.Silu)

                    for jc in range(NJC):
                        for mi, (m, srcr) in enumerate(zip("gud", (gw, uw, dw))):
                            s.wait_ge(S["gF"], gTT(jc, mi))
                            if jc >= 1:
                                s.wait_ge(S["gF"], gDeq(jc - 1, mi))
                            s.copy(Q8[m][:], srcr[jc % 2][:])
                        if jc >= 1:
                            emit_silu(jc - 1, 0)
                            emit_silu(jc - 1, 1)
                    emit_silu(NJC - 1, 0)
                    emit_silu(NJC - 1, 1)
                    assert C["scF"] == sb1 + 5 * NJC

                @blk.gpsimd
                def _(g):
                    g = Fenced(g, S["gF"], C, "gF")
                    for mi, (m, srcr) in enumerate(zip("gud", (gw, uw, dw))):
                        g.wait_ge(S["vF"], vTT(0, mi))
                        qtt(g, srcr[0][:], 64, m, 0)
                    for jc in range(NJC):
                        for mi, (m, outr) in enumerate(zip("gud", (gq, uq, dq))):
                            g.wait_ge(S["scF"], scC(jc, mi))
                            if jc >= 2:
                                R[{"g": "gTG", "u": "gTU", "d": "gTD"}[m]].wait(g, jc - 2)
                            deq(g, outr[jc % 2][:], 64, m, jc % 2)
                        if jc + 1 < NJC:
                            for mi, (m, srcr) in enumerate(zip("gud", (gw, uw, dw))):
                                g.wait_ge(S["vF"], vTT(jc + 1, mi))
                                qtt(g, srcr[(jc + 1) % 2][:], 64, m, (jc + 1) % 2)
                    assert C["gF"] == gb1 + 6 * NJC

                @blk.tensor
                def _(te):
                    R["fXQT"].wait_all(te)
                    for jc in range(NJC):
                        R["gTG"].wait(te, jc)
                        R["gTU"].wait(te, jc)
                        for hh in range(2):
                            if jc >= 1:
                                te.wait_ge(S["vF"],
                                           (vHmultA if hh == 0 else vHmultB)(jc - 1))
                                te.wait_ge(S["scF"],
                                           (scSiluA if hh == 0 else scSiluB)(jc - 1))
                            ins = None
                            for ttl in range(HTT):
                                tt = hh * HTT + ttl
                                for i in range(16):
                                    st, sp = (i == 0), (i == 15)
                                    ins = te.matmul(
                                        ps_gu[hh][:, ttl * 256:(ttl + 1) * 256],
                                        xq_fm[:, i * S_own + tt * 128:i * S_own + (tt + 1) * 128],
                                        guf[jc % 2][:, i * 256:(i + 1) * 256],
                                        start=st, stop=sp)
                            ins.then_inc(S["gMM"], 1)
                    C["gMM"] = 2 * NJC

        NB2 = NJC // 8
        F2_V = 8 + 16
        vb2 = C["vF"]
        sb2 = C["scF"]

        def hEV(it):
            tg, oh = divmod(it, 2)
            return vb2 + F2_V * tg + 4 * (oh + 1)

        def hLN2(u):
            tg, tt = divmod(u, 4)
            return vb2 + F2_V * tg + 8 + 4 * (tt + 1)

        def hSC(u):
            return sb2 + 3 * (u + 1)

        ffn_scope.close()
        with ExitStack() as p2:
            E2 = p2.enter_context
            hqb = [E2(nc.sbuf_tensor(f"f2h{i}", [128, 8 * 512], BF16)) for i in range(3)]
            dqb = [E2(nc.sbuf_tensor(f"f2d{i}", [128, 8 * 1024], BF16)) for i in range(3)]
            mid2 = E2(nc.sbuf_tensor("f2mid", [128, 4 * EMB], F32))
            xm2 = E2(nc.sbuf_tensor("f2xm", [128, 4 * EMB], F32))
            junk2 = E2(nc.sbuf_tensor("f2junk", [128, EMB], F32))
            cent2 = E2(nc.sbuf_tensor("f2cent", [128, EMB], F32))
            outb = [E2(nc.sbuf_tensor(f"f2ob{i}", [128, EMB], F32)) for i in range(2)]
            stat2 = E2(nc.sbuf_tensor("f2st", [128, 32], F32))
            ps_f2 = [E2(nc.psum_tensor(f"f2p{i}", [128, 1024], F32)) for i in range(4)]

            NIT = NTG * 2
            with nc.Block() as blk:
                @blk.sync
                def _(s):
                    for tg in range(NTG):
                        for oh in range(2):
                            it = tg * 2 + oh
                            for bi in range(NB2):
                                bu = it * NB2 + bi
                                if bu >= 3:
                                    s.wait_ge(S["hMM"], (bu - 2) * 8)
                                R["hH"].attach(s.dma_start(
                                    out=hqb[bu % 3][:].rearrange("p (j t) -> p j t", j=8),
                                    in_=hq_dram[bi * 8:(bi + 1) * 8]
                                        [:, :, tg * 512:(tg + 1) * 512]
                                        .rearrange("j p t -> p j t")))
                                R["hD"].attach(s.dma_start(
                                    out=dqb[bu % 3][:].rearrange("p (j o) -> p j o", j=8),
                                    in_=dq_dram[bi * 8:(bi + 1) * 8]
                                        [:, :, oh * 1024:(oh + 1) * 1024]
                                        .rearrange("j p o -> p j o")))
                            if oh == 0:
                                if tg >= 1:
                                    s.wait_ge(S["vF"], hEV(2 * tg - 1))
                                for tt in range(4):
                                    R["hXM"].attach(s.dma_start(
                                        out=xm2[:, tt * EMB:(tt + 1) * EMB],
                                        in_=xmid_dram[tg * 512 + tt * 128:
                                                      tg * 512 + (tt + 1) * 128, :]))
                    for rn in ("hH", "hD", "hXM"):
                        R[rn].wait_all(s)
                    R["hOUT"].wait_all(s)

                @blk.tensor
                def _(te):
                    for tg in range(NTG):
                        for oh in range(2):
                            it = tg * 2 + oh
                            if it >= 1:
                                te.wait_ge(S["vF"], hEV(it - 1))
                            for bi in range(NB2):
                                bu = it * NB2 + bi
                                R["hH"].wait(te, bu)
                                R["hD"].wait(te, bu)
                                for j in range(8):
                                    jc = bi * 8 + j
                                    st, sp = (jc == 0), (jc == NJC - 1)
                                    last = None
                                    for tt in range(4):
                                        te.matmul(ps_f2[tt][:, 0:512],
                                                  hqb[bu % 3][:, (j * 4 + tt) * 128:
                                                              (j * 4 + tt + 1) * 128],
                                                  dqb[bu % 3][:, j * 1024:j * 1024 + 512],
                                                  start=st, stop=sp)
                                        last = te.matmul(
                                            ps_f2[tt][:, 512:1024],
                                            hqb[bu % 3][:, (j * 4 + tt) * 128:
                                                        (j * 4 + tt + 1) * 128],
                                            dqb[bu % 3][:, j * 1024 + 512:(j + 1) * 1024],
                                            start=st, stop=sp)
                                    last.then_inc(S["hMM"], 1)
                    C["hMM"] = NIT * NJC

                @blk.vector
                def _(v):
                    v = Fenced(v, S["vF"], C, "vF")
                    for tg in range(NTG):
                        for tt in range(4):
                            R["hXM"].wait(v, tg * 4 + tt)
                        for oh in range(2):
                            it = tg * 2 + oh
                            v.wait_ge(S["hMM"], (it + 1) * NJC)
                            for tt in range(4):
                                v.scalar_tensor_tensor(
                                    mid2[:, tt * EMB + oh * 1024:tt * EMB + (oh + 1) * 1024],
                                    ps_f2[tt][:], 1.0,
                                    xm2[:, tt * EMB + oh * 1024:tt * EMB + (oh + 1) * 1024],
                                    ALU.mult, ALU.add,
                                    accum_out=stat2[:, tt * 2 + oh:tt * 2 + oh + 1])
                        for tt in range(4):
                            base = tt * 2
                            v.tensor_tensor(stat2[:, 16 + tt:17 + tt],
                                            stat2[:, base:base + 1],
                                            stat2[:, base + 1:base + 2], ALU.add)
                            v.tensor_scalar(stat2[:, 25:26], stat2[:, 16 + tt:17 + tt],
                                            1.0 / EMB, None, ALU.mult)
                            m2 = mid2[:, tt * EMB:(tt + 1) * EMB]
                            v.scalar_tensor_tensor(junk2[:], m2, stat2[:, 25:26],
                                                   m2, ALU.subtract, ALU.mult,
                                                   accum_out=stat2[:, 26:27])
                            v.tensor_scalar(cent2[:], m2, stat2[:, 25:26], None,
                                            ALU.subtract)
                            v.wait_ge(S["scF"], hSC(tg * 4 + tt))
                    assert C["vF"] == vb2 + F2_V * NTG

                @blk.scalar
                def _(s):
                    raw = s
                    s = Fenced(s, S["scF"], C, "scF")
                    for tg in range(NTG):
                        for tt in range(4):
                            u = tg * 4 + tt
                            s.wait_ge(S["vF"], hLN2(u))
                            s.activation(stat2[:, 27:28], stat2[:, 26:27], AF.Ln,
                                         bias=bias_eps[:], scale=1.0 / EMB)
                            s.activation(stat2[:, 28:29], stat2[:, 27:28], AF.Exp,
                                         scale=-0.5)
                            if u >= 2:
                                R["hOUT"].wait(s, u - 2)
                            s.activation(outb[tt % 2][:], cent2[:], AF.Copy,
                                         scale=stat2[:, 28:29])
                            R["hOUT"].attach(raw.dma_start(
                                out=out_own[tg * 512 + tt * 128:
                                            tg * 512 + (tt + 1) * 128, :],
                                in_=outb[tt % 2][:]))
                    assert C["scF"] == sb2 + 3 * NTG * 4

    return nc


def kernel(L0_attn_x, L0_attn_W_Q, L0_attn_W_K, L0_attn_W_V, L0_attn_W_O,
           L0_ffn_gate_w, L0_ffn_up_w, L0_ffn_down_w):
    S_full, S_own = 4096, 2048
    x = np.ascontiguousarray(np.asarray(L0_attn_x, dtype=np.float32))
    WQ = np.asarray(L0_attn_W_Q, dtype=np.float32)
    WK = np.asarray(L0_attn_W_K, dtype=np.float32)
    WV = np.asarray(L0_attn_W_V, dtype=np.float32)
    WO = np.asarray(L0_attn_W_O, dtype=np.float32)
    gwv = np.ascontiguousarray(np.asarray(L0_ffn_gate_w, dtype=np.float32))
    uwv = np.ascontiguousarray(np.asarray(L0_ffn_up_w, dtype=np.float32))
    dwv = np.asarray(L0_ffn_down_w, dtype=np.float32)

    perm_q = (np.arange(128)[None, :] * 16 + np.arange(16)[:, None]).reshape(-1)
    perm_k = (np.arange(128)[None, :] * 8 + np.arange(8)[:, None]).reshape(-1)
    wqt = np.ascontiguousarray(WQ[perm_q, :].T)
    wkt = np.ascontiguousarray(WK[perm_k, :].T)
    wvt = np.ascontiguousarray(WV[perm_k, :].T)
    wot = np.ascontiguousarray(WO.T)
    down_wr = np.ascontiguousarray(dwv.reshape(EMB, NJC, 128).transpose(1, 0, 2))

    half = 64
    inv_freq = (1.0 / 10000.0 ** (np.arange(0, half, dtype=np.float32) * 2.0 / 128.0)
                ).astype(np.float32)
    pos = np.arange(S_full, dtype=np.float32)
    ang = pos[:, None] * inv_freq[None, :]
    cos_t = np.ascontiguousarray(
        np.concatenate([np.cos(ang), np.cos(ang)], axis=1).astype(np.float32))
    sin_t = np.ascontiguousarray(
        np.concatenate([np.sin(ang), np.sin(ang)], axis=1).astype(np.float32))

    nc = build_nc(S_full, S_own)
    in_maps = []
    for c in range(8):
        b, hh = c // 2, c % 2
        in_maps.append({
            "x_seq": np.ascontiguousarray(x[b]),
            "x_own": np.ascontiguousarray(x[b, hh * S_own:(hh + 1) * S_own]),
            "wqt": wqt, "wkt": wkt, "wvt": wvt, "wot": wot,
            "gate_w": gwv, "up_w": uwv, "down_wr": down_wr,
            "cos_t": cos_t, "sin_t": sin_t,
            "cos_o": np.ascontiguousarray(cos_t[hh * S_own:(hh + 1) * S_own]),
            "sin_o": np.ascontiguousarray(sin_t[hh * S_own:(hh + 1) * S_own]),
        })
    res = run_bass_kernel_spmd(nc, in_maps, core_ids=list(range(8)))
    out = np.empty((4, S_full, EMB), dtype=np.float32)
    for c in range(8):
        b, hh = c // 2, c % 2
        out[b, hh * S_own:(hh + 1) * S_own] = res.results[c]["out_own"]
    return out


# version 20 (final)
# speedup vs baseline: 1.0799x; 1.0107x over previous
import numpy as np
from contextlib import ExitStack

import concourse.bass as bass
import concourse.mybir as mybir
from concourse.bass_utils import run_bass_kernel_spmd

F32 = mybir.dt.float32
BF16 = mybir.dt.bfloat16
F8 = mybir.dt.float8e4
AF = mybir.ActivationFunctionType
ALU = mybir.AluOpType
AX = mybir.AxisListType

EMB = 2048
HEADS = 16
KVH = 8
D = 128
IFF = 8192
NJC = IFF // 128
SCALE_S = float(1.0 / np.sqrt(np.float32(2048.0)))
EXP_SHIFT = -5.0
LN_EPS = 1e-5
INV_224 = float(np.float32(1.0) / np.float32(224.0))
QEPS2 = 2e-30


class Ring:

    def __init__(self, mksem, name, slots):
        self.sems = [mksem(name + str(i)) for i in range(slots)]
        self.slots = slots
        self.issued = 0

    def attach(self, instr):
        instr.then_inc(self.sems[self.issued % self.slots], 16)
        self.issued += 1
        return instr

    def wait(self, eng, i):
        eng.wait_ge(self.sems[i % self.slots], 16 * (i // self.slots + 1))

    def wait_all(self, eng):
        n = self.issued
        for si in range(self.slots):
            c = (n - si + self.slots - 1) // self.slots
            if c > 0:
                eng.wait_ge(self.sems[si], 16 * c)


class Fenced:

    def __init__(self, eng, sem, cnt, key):
        self._eng = eng
        self._sem = sem
        self._cnt = cnt
        self._key = key

    def _f(self, ins):
        self._cnt[self._key] += 1
        ins.then_inc(self._sem, 1)
        self._eng.wait_ge(self._sem, self._cnt[self._key])
        return ins

    def wait_ge(self, sem, val):
        self._eng.wait_ge(sem, val)

    def __getattr__(self, name):
        fn = getattr(self._eng, name)

        def call(*a, **k):
            return self._f(fn(*a, **k))
        return call


def _rope_tm(v, ps, cs, sn, out, temps, nh):
    kk = ps[:].rearrange("p (h d) -> p h d", h=nh)
    ka, kb = kk[:, :, 0:64], kk[:, :, 64:128]
    csA = cs[:, 0:64].unsqueeze(1).broadcast_to([128, nh, 64])
    csB = cs[:, 64:128].unsqueeze(1).broadcast_to([128, nh, 64])
    snA = sn[:, 0:64].unsqueeze(1).broadcast_to([128, nh, 64])
    snB = sn[:, 64:128].unsqueeze(1).broadcast_to([128, nh, 64])
    r0 = temps[0][:].rearrange("p (h d) -> p h d", h=nh)
    r1 = temps[1][:].rearrange("p (h d) -> p h d", h=nh)
    ko = out.rearrange("p (h d) -> p h d", h=nh)
    v.tensor_tensor(r0, ka, csA, ALU.mult)
    v.tensor_tensor(r1, kb, snA, ALU.mult)
    v.tensor_tensor(ko[:, :, 0:64], r0, r1, ALU.subtract)
    v.tensor_tensor(r0, kb, csB, ALU.mult)
    v.tensor_tensor(r1, ka, snB, ALU.mult)
    v.tensor_tensor(ko[:, :, 64:128], r0, r1, ALU.add)


ROPE_OPS = 6


def _quant(v, src_f32, nb, tmp_v, tmp_q8, red, s2, rcp, out_bf):
    s3 = src_f32.rearrange("p (b r) -> p b r", r=32)
    v.tensor_reduce(red[:, 0:nb], s3, axis=AX.X, op=ALU.max,
                    apply_absolute_value=True)
    v.tensor_scalar(s2[:, 0:nb], red[:, 0:nb], INV_224, None, ALU.mult)
    v.tensor_scalar_max(s2[:, 0:nb], s2[:, 0:nb], QEPS2)
    v.reciprocal(rcp[:, 0:nb], s2[:, 0:nb])
    rb = rcp[:, 0:nb].unsqueeze(2).broadcast_to([128, nb, 32])
    s2b = s2[:, 0:nb].unsqueeze(2).broadcast_to([128, nb, 32])
    t3 = tmp_v.rearrange("p (b r) -> p b r", r=32)
    q3 = tmp_q8.rearrange("p (b r) -> p b r", r=32)
    v.tensor_tensor(t3, s3, rb, ALU.mult)
    v.tensor_copy(tmp_q8, tmp_v)
    o3 = out_bf.rearrange("p (b r) -> p b r", r=32)
    v.tensor_tensor(o3, q3, s2b, ALU.mult)


QUANT_OPS = 7


def build_nc(S_full=4096, S_own=2048):
    assert S_full % 256 == 0 and S_own % 512 == 0
    NT_F = S_full // 128
    NT_O = S_own // 128
    NG = S_own // 512
    NKC = S_full // 128
    NPAIR = NKC // 2
    NTG = S_own // 512
    NTT = NT_O
    NE = HEADS * NPAIR

    nc = bass.Bass("TRN2", target_bir_lowering=False)

    x_seq = nc.dram_tensor("x_seq", [S_full, EMB], F32, kind="ExternalInput").ap()
    x_own = nc.dram_tensor("x_own", [S_own, EMB], F32, kind="ExternalInput").ap()
    wqt = nc.dram_tensor("wqt", [EMB, EMB], F32, kind="ExternalInput").ap()
    wkt = nc.dram_tensor("wkt", [EMB, KVH * D], F32, kind="ExternalInput").ap()
    wvt = nc.dram_tensor("wvt", [EMB, KVH * D], F32, kind="ExternalInput").ap()
    wot = nc.dram_tensor("wot", [EMB, EMB], F32, kind="ExternalInput").ap()
    gate_w = nc.dram_tensor("gate_w", [IFF, EMB], F32, kind="ExternalInput").ap()
    up_w = nc.dram_tensor("up_w", [IFF, EMB], F32, kind="ExternalInput").ap()
    down_wr = nc.dram_tensor("down_wr", [NJC, EMB, 128], F32, kind="ExternalInput").ap()
    cos_t = nc.dram_tensor("cos_t", [S_full, D], F32, kind="ExternalInput").ap()
    sin_t = nc.dram_tensor("sin_t", [S_full, D], F32, kind="ExternalInput").ap()
    cos_o = nc.dram_tensor("cos_o", [S_own, D], F32, kind="ExternalInput").ap()
    sin_o = nc.dram_tensor("sin_o", [S_own, D], F32, kind="ExternalInput").ap()
    out_own = nc.dram_tensor("out_own", [S_own, EMB], F32, kind="ExternalOutput").ap()
    mid_dram = nc.dram_tensor("mid_dram", [S_own, EMB], F32).ap()
    wq_bf_dram = nc.dram_tensor("wq_bf_dram", [EMB, EMB], BF16).ap()
    wo_bf_dram = nc.dram_tensor("wo_bf_dram", [EMB, EMB], BF16).ap()
    xmid_dram = nc.dram_tensor("xmid_dram", [S_own, EMB], F32).ap()
    hq_dram = nc.dram_tensor("hq_dram", [NJC, 128, S_own], BF16).ap()
    dq_dram = nc.dram_tensor("dq_dram", [NJC, 128, EMB], BF16).ap()

    with ExitStack() as top:
        E = top.enter_context
        mksem = lambda n: E(nc.semaphore(n))
        ones_bf = E(nc.sbuf_tensor("ones_bf", [128, 128], BF16))
        bias_shift = E(nc.sbuf_tensor("bias_shift", [128, 1], F32))
        bias_eps = E(nc.sbuf_tensor("bias_eps", [128, 1], F32))

        eng_names = ["akMM", "avMM", "bWQC", "bS", "bPV", "bTO",
                     "gMM", "hMM", "vF", "scF", "gF"]
        S = {n: mksem(n) for n in eng_names}
        C = {n: 0 for n in eng_names}

        R = {}
        for name, slots in [
            ("akW", 1), ("akX", 2), ("akTX", 2), ("akCS", 2), ("akSN", 2), ("akKT", 2),
            ("akWQ", 2), ("akWS", 2),
            ("avW", 1), ("avX", 2), ("avTX", 2), ("avWO", 2), ("avWS", 2),
            ("bX", 2), ("bTX", 2), ("bCS", 2), ("bSN", 2), ("bWQ", 3), ("bQT", 2),
            ("bWO", 2), ("bMS", 2),
            ("fM", 3), ("fX", 3), ("fXM", 2), ("fXQT", 2),
            ("gG", 2), ("gU", 2), ("gD", 2), ("gTG", 2), ("gTU", 2), ("gTD", 2),
            ("gDS", 2), ("gHT", 2), ("gHS", 2),
            ("hH", 3), ("hD", 3), ("hXM", 4), ("hOUT", 2),
        ]:
            R[name] = Ring(mksem, name, slots)

        with nc.Block() as blk:
            @blk.gpsimd
            def _(g):
                g.memset(ones_bf[:], 1.0)
                g.memset(bias_shift[:], EXP_SHIFT)
                g.memset(bias_eps[:], LN_EPS)

        with ExitStack() as att_scope:
            Ea = att_scope.enter_context
            k_fm = Ea(nc.sbuf_tensor("k_fm", [128, KVH * S_full], BF16))
            v_tm = Ea(nc.sbuf_tensor("v_tm", [128, NKC * KVH * D], BF16))

            vbAK = C["vF"]

            def akRP(t):
                return vbAK + ROPE_OPS * (t + 1)

            with ExitStack() as pa:
                Ek = pa.enter_context
                wkt_sb = Ek(nc.sbuf_tensor("wkt_sb", [128, 16 * KVH * D], BF16))
                xb = [Ek(nc.sbuf_tensor(f"akxb{i}", [128, EMB], BF16)) for i in range(2)]
                xfm = [Ek(nc.sbuf_tensor(f"akxf{i}", [128, EMB], BF16)) for i in range(2)]
                krot = [Ek(nc.sbuf_tensor(f"akkr{i}", [128, KVH * D], BF16)) for i in range(2)]
                cs_sb = [Ek(nc.sbuf_tensor(f"akcs{i}", [128, D], F32)) for i in range(2)]
                sn_sb = [Ek(nc.sbuf_tensor(f"aksn{i}", [128, D], F32)) for i in range(2)]
                rt = [Ek(nc.sbuf_tensor(f"akrt{i}", [128, KVH * 64], F32)) for i in range(2)]
                wqstg = [Ek(nc.sbuf_tensor(f"akwq{i}", [128, EMB], BF16)) for i in range(2)]
                ps_k = [Ek(nc.psum_tensor(f"ps_k{i}", [128, KVH * D], F32)) for i in range(2)]

                with nc.Block() as blk:
                    @blk.gpsimd
                    def _(g):
                        R["akW"].attach(
                            g.dma_start(out=wkt_sb[:].rearrange("p (i o) -> p i o", i=16),
                                        in_=wkt.rearrange("(i p) o -> p i o", p=128)))
                        for t in range(NT_F):
                            if t >= 2:
                                R["akTX"].wait(g, t - 2)
                            R["akX"].attach(
                                g.dma_start(out=xb[t % 2][:],
                                            in_=x_seq[t * 128:(t + 1) * 128, :]))
                        for i in range(16):
                            if i >= 2:
                                R["akWS"].wait(g, i - 2)
                            R["akWQ"].attach(
                                g.dma_start(out=wqstg[i % 2][:],
                                            in_=wqt[i * 128:(i + 1) * 128, :]))
                        R["akX"].wait_all(g)
                        R["akW"].wait_all(g)

                    @blk.sync
                    def _(s):
                        for t in range(NT_F + 1):
                            if t < NT_F:
                                R["akX"].wait(s, t)
                                if t >= 2:
                                    s.wait_ge(S["akMM"], t - 1)
                                R["akTX"].attach(s.dma_start(
                                    out=xfm[t % 2][:].rearrange("p (b f) -> p b f", b=16),
                                    in_=xb[t % 2][:].rearrange("p (b f) -> p b f", b=16),
                                    transpose=True))
                                if t >= 2:
                                    s.wait_ge(S["vF"], akRP(t - 2))
                                R["akCS"].attach(
                                    s.dma_start(out=cs_sb[t % 2][:],
                                                in_=cos_t[t * 128:(t + 1) * 128, :]))
                                R["akSN"].attach(
                                    s.dma_start(out=sn_sb[t % 2][:],
                                                in_=sin_t[t * 128:(t + 1) * 128, :]))
                            if t >= 1:
                                tk = t - 1
                                s.wait_ge(S["vF"], akRP(tk))
                                R["akKT"].attach(s.dma_start(
                                    out=k_fm[:].rearrange("p (h s) -> p h s", h=KVH)
                                        [:, :, tk * 128:(tk + 1) * 128],
                                    in_=krot[tk % 2][:].rearrange("p (h f) -> p h f", h=KVH),
                                    transpose=True))
                        for i in range(16):
                            R["akWQ"].wait(s, i)
                            R["akWS"].attach(s.dma_start(
                                out=wq_bf_dram[i * 128:(i + 1) * 128, :],
                                in_=wqstg[i % 2][:]))
                        for rn in ("akTX", "akCS", "akSN", "akKT", "akWQ", "akWS"):
                            R[rn].wait_all(s)

                    @blk.tensor
                    def _(te):
                        R["akW"].wait(te, 0)
                        for t in range(NT_F):
                            R["akTX"].wait(te, t)
                            if t >= 2:
                                te.wait_ge(S["vF"], akRP(t - 2))
                            for i in range(16):
                                st, sp = (i == 0), (i == 15)
                                lhs = xfm[t % 2][:, i * 128:(i + 1) * 128]
                                te.matmul(ps_k[t % 2][:, 0:512], lhs,
                                          wkt_sb[:, i * 1024:i * 1024 + 512],
                                          start=st, stop=sp)
                                ins = te.matmul(ps_k[t % 2][:, 512:1024], lhs,
                                                wkt_sb[:, i * 1024 + 512:(i + 1) * 1024],
                                                start=st, stop=sp)
                                if sp:
                                    ins.then_inc(S["akMM"], 1)
                        C["akMM"] = NT_F

                    @blk.vector
                    def _(v):
                        v = Fenced(v, S["vF"], C, "vF")
                        for t in range(NT_F):
                            v.wait_ge(S["akMM"], t + 1)
                            R["akCS"].wait(v, t)
                            R["akSN"].wait(v, t)
                            if t >= 2:
                                R["akKT"].wait(v, t - 2)
                            _rope_tm(v, ps_k[t % 2], cs_sb[t % 2][:],
                                     sn_sb[t % 2][:], krot[t % 2][:], rt, KVH)
                        assert C["vF"] == vbAK + ROPE_OPS * NT_F

            sbAV = C["scF"]
            with ExitStack() as pv:
                Ev = pv.enter_context
                wvt_sb = Ev(nc.sbuf_tensor("wvt_sb", [128, 16 * KVH * D], BF16))
                wostg = [Ev(nc.sbuf_tensor(f"avwo{i}", [128, EMB], BF16)) for i in range(2)]
                xb = [Ev(nc.sbuf_tensor(f"avxb{i}", [128, EMB], BF16)) for i in range(2)]
                xfm = [Ev(nc.sbuf_tensor(f"avxf{i}", [128, EMB], BF16)) for i in range(2)]
                ps_v = [Ev(nc.psum_tensor(f"ps_v{i}", [128, KVH * D], F32)) for i in range(2)]

                with nc.Block() as blk:
                    @blk.gpsimd
                    def _(g):
                        R["avW"].attach(
                            g.dma_start(out=wvt_sb[:].rearrange("p (i o) -> p i o", i=16),
                                        in_=wvt.rearrange("(i p) o -> p i o", p=128)))
                        for t in range(NT_F):
                            if t >= 2:
                                R["avTX"].wait(g, t - 2)
                            R["avX"].attach(
                                g.dma_start(out=xb[t % 2][:],
                                            in_=x_seq[t * 128:(t + 1) * 128, :]))
                        for i in range(16):
                            if i >= 2:
                                R["avWS"].wait(g, i - 2)
                            R["avWO"].attach(
                                g.dma_start(out=wostg[i % 2][:],
                                            in_=wot[i * 128:(i + 1) * 128, :]))
                        R["avX"].wait_all(g)
                        R["avW"].wait_all(g)

                    @blk.sync
                    def _(s):
                        for t in range(NT_F):
                            R["avX"].wait(s, t)
                            if t >= 2:
                                s.wait_ge(S["avMM"], t - 1)
                            R["avTX"].attach(s.dma_start(
                                out=xfm[t % 2][:].rearrange("p (b f) -> p b f", b=16),
                                in_=xb[t % 2][:].rearrange("p (b f) -> p b f", b=16),
                                transpose=True))
                        for i in range(16):
                            R["avWO"].wait(s, i)
                            R["avWS"].attach(s.dma_start(
                                out=wo_bf_dram[i * 128:(i + 1) * 128, :],
                                in_=wostg[i % 2][:]))
                        R["avTX"].wait_all(s)
                        R["avWO"].wait_all(s)
                        R["avWS"].wait_all(s)

                    @blk.tensor
                    def _(te):
                        R["avW"].wait(te, 0)
                        for t in range(NT_F):
                            R["avTX"].wait(te, t)
                            if t >= 2:
                                te.wait_ge(S["scF"], sbAV + t - 1)
                            for i in range(16):
                                st, sp = (i == 0), (i == 15)
                                lhs = xfm[t % 2][:, i * 128:(i + 1) * 128]
                                te.matmul(ps_v[t % 2][:, 0:512], lhs,
                                          wvt_sb[:, i * 1024:i * 1024 + 512],
                                          start=st, stop=sp)
                                ins = te.matmul(ps_v[t % 2][:, 512:1024], lhs,
                                                wvt_sb[:, i * 1024 + 512:(i + 1) * 1024],
                                                start=st, stop=sp)
                                if sp:
                                    ins.then_inc(S["avMM"], 1)
                        C["avMM"] = NT_F

                    @blk.scalar
                    def _(s):
                        s = Fenced(s, S["scF"], C, "scF")
                        for t in range(NT_F):
                            s.wait_ge(S["avMM"], t + 1)
                            s.copy(v_tm[:, t * 1024:(t + 1) * 1024], ps_v[t % 2][:])
                        assert C["scF"] == sbAV + NT_F

            q_fm = Ea(nc.sbuf_tensor("q_fm", [128, HEADS * 512], BF16))
            ctx_fm = Ea(nc.sbuf_tensor("ctx_fm", [128, HEADS * 512], BF16))

            for gi in range(NG):
                q0 = gi * 512
                bWQC0 = gi * 32
                bS0 = bPV0 = gi * NE
                bTO0 = gi * 16

                vbB1 = C["vF"]

                def bQR(tl, _vb=vbB1):
                    return _vb + ROPE_OPS * (tl + 1)

                with ExitStack() as pb, nc.Block() as blk:
                    Eb = pb.enter_context
                    xb2 = [Eb(nc.sbuf_tensor(f"b1xb{gi}_{i}", [128, EMB], BF16)) for i in range(2)]
                    xfm2 = [Eb(nc.sbuf_tensor(f"b1xf{gi}_{i}", [128, EMB], BF16)) for i in range(2)]
                    qrot = [Eb(nc.sbuf_tensor(f"b1qr{gi}_{i}", [128, EMB], BF16)) for i in range(2)]
                    wq_buf = [Eb(nc.sbuf_tensor(f"b1wq{gi}_{i}", [128, EMB], BF16)) for i in range(3)]
                    cs2 = [Eb(nc.sbuf_tensor(f"b1cs{gi}_{i}", [128, D], F32)) for i in range(2)]
                    sn2 = [Eb(nc.sbuf_tensor(f"b1sn{gi}_{i}", [128, D], F32)) for i in range(2)]
                    rt2 = [Eb(nc.sbuf_tensor(f"b1rt{gi}_{i}", [128, HEADS * 64], F32)) for i in range(2)]
                    ps_q = [Eb(nc.psum_tensor(f"b1pq{gi}_{i}", [128, EMB], F32)) for i in range(2)]

                    @blk.gpsimd
                    def _(g):
                        for tl in range(4):
                            if tl >= 2:
                                R["bTX"].wait(g, gi * 4 + tl - 2)
                            R["bX"].attach(
                                g.dma_start(out=xb2[tl % 2][:],
                                            in_=x_own[q0 + tl * 128:q0 + (tl + 1) * 128, :]))
                        for k in range(32):
                            i = k % 16
                            if k >= 3:
                                g.wait_ge(S["bWQC"], bWQC0 + k - 2)
                            R["bWQ"].attach(
                                g.dma_start(out=wq_buf[k % 3][:],
                                            in_=wq_bf_dram[i * 128:(i + 1) * 128, :]))
                        R["bX"].wait_all(g)
                        R["bWQ"].wait_all(g)

                    @blk.sync
                    def _(s):
                        for tl in range(4):
                            R["bX"].wait(s, gi * 4 + tl)
                            if tl >= 2:
                                s.wait_ge(S["bWQC"], bWQC0 + 16 * ((tl - 2) // 2 + 1))
                            R["bTX"].attach(s.dma_start(
                                out=xfm2[tl % 2][:].rearrange("p (b f) -> p b f", b=16),
                                in_=xb2[tl % 2][:].rearrange("p (b f) -> p b f", b=16),
                                transpose=True))
                            if tl >= 2:
                                s.wait_ge(S["vF"], bQR(tl - 2))
                            R["bCS"].attach(
                                s.dma_start(out=cs2[tl % 2][:],
                                            in_=cos_o[q0 + tl * 128:q0 + (tl + 1) * 128, :]))
                            R["bSN"].attach(
                                s.dma_start(out=sn2[tl % 2][:],
                                            in_=sin_o[q0 + tl * 128:q0 + (tl + 1) * 128, :]))
                            if tl >= 1:
                                tk = tl - 1
                                s.wait_ge(S["vF"], bQR(tk))
                                R["bQT"].attach(s.dma_start(
                                    out=q_fm[:].rearrange("p (h s) -> p h s", h=HEADS)
                                        [:, :, tk * 128:(tk + 1) * 128],
                                    in_=qrot[tk % 2][:].rearrange("p (h f) -> p h f", h=HEADS),
                                    transpose=True))
                        tk = 3
                        s.wait_ge(S["vF"], bQR(tk))
                        R["bQT"].attach(s.dma_start(
                            out=q_fm[:].rearrange("p (h s) -> p h s", h=HEADS)
                                [:, :, tk * 128:(tk + 1) * 128],
                            in_=qrot[tk % 2][:].rearrange("p (h f) -> p h f", h=HEADS),
                            transpose=True))
                        for rn in ("bTX", "bCS", "bSN", "bQT"):
                            R[rn].wait_all(s)

                    @blk.tensor
                    def _(te):
                        for sw in range(2):
                            tls = (2 * sw, 2 * sw + 1)
                            for i in range(16):
                                k = sw * 16 + i
                                R["bWQ"].wait(te, gi * 32 + k)
                                st, sp = (i == 0), (i == 15)
                                for tl in tls:
                                    if st:
                                        R["bTX"].wait(te, gi * 4 + tl)
                                        if tl >= 2:
                                            te.wait_ge(S["vF"], bQR(tl - 2))
                                    lhs = xfm2[tl % 2][:, i * 128:(i + 1) * 128]
                                    last = None
                                    for oc in range(4):
                                        last = te.matmul(
                                            ps_q[tl % 2][:, oc * 512:(oc + 1) * 512], lhs,
                                            wq_buf[k % 3][:, oc * 512:(oc + 1) * 512],
                                            start=st, stop=sp)
                                    if tl == tls[1]:
                                        last.then_inc(S["bWQC"], 1)
                        C["bWQC"] = bWQC0 + 32

                    @blk.vector
                    def _(v):
                        v = Fenced(v, S["vF"], C, "vF")
                        for tl in range(4):
                            v.wait_ge(S["bWQC"], bWQC0 + 16 * (tl // 2 + 1))
                            R["bCS"].wait(v, gi * 4 + tl)
                            R["bSN"].wait(v, gi * 4 + tl)
                            if tl >= 2:
                                R["bQT"].wait(v, gi * 4 + tl - 2)
                            _rope_tm(v, ps_q[tl % 2], cs2[tl % 2][:],
                                     sn2[tl % 2][:], qrot[tl % 2][:], rt2, HEADS)
                        assert C["vF"] == vbB1 + ROPE_OPS * 4

                vbB2 = C["vF"]
                sbB2 = C["scF"]

                def bRC(h, _vb=vbB2):
                    return _vb + 2 * h + 1

                def bVC(h, _vb=vbB2):
                    return _vb + 2 * (h + 1)

                def bEX(e, _sb=sbB2):
                    return _sb + e + 1

                def bOE(j, _sb=sbB2):
                    return _sb + NE + j + 1

                with ExitStack() as pb, nc.Block() as blk:
                    Eb = pb.enter_context
                    pT = [Eb(nc.sbuf_tensor(f"b2pT{gi}_{i}", [128, 1024], BF16)) for i in range(2)]
                    rbc = Eb(nc.sbuf_tensor(f"b2rb{gi}", [128, 512], F32))
                    wo_buf = [Eb(nc.sbuf_tensor(f"b2wo{gi}_{i}", [128, 16 * 512], BF16))
                              for i in range(2)]
                    midb = [Eb(nc.sbuf_tensor(f"b2mb{gi}_{i}", [128, 512], F32)) for i in range(2)]
                    ps_s = [Eb(nc.psum_tensor(f"b2ps{gi}_{i}", [128, 1024], F32)) for i in range(2)]
                    ps_ctx = Eb(nc.psum_tensor(f"b2pc{gi}", [128, 512], F32))
                    ps_row = Eb(nc.psum_tensor(f"b2prw{gi}", [128, 512], F32))
                    ps_o = [Eb(nc.psum_tensor(f"b2po{gi}_{i}", [128, 512], F32)) for i in range(2)]

                    @blk.gpsimd
                    def _(g):
                        for oc in range(4):
                            if oc >= 2:
                                g.wait_ge(S["bTO"], bTO0 + 4 * (oc - 1))
                            R["bWO"].attach(g.dma_start(
                                out=wo_buf[oc % 2][:].rearrange("p (i o) -> p i o", i=16),
                                in_=wo_bf_dram.rearrange("(i p) o -> p i o", p=128)
                                    [:, :, oc * 512:(oc + 1) * 512]))
                        R["bWO"].wait_all(g)

                    @blk.tensor
                    def _(te):
                        R["bQT"].wait(te, gi * 4 + 3)
                        R["bQT"].wait(te, gi * 4 + 2)

                        def emit_S(e):
                            h, pr = e // NPAIR, e % NPAIR
                            kv = h // 2
                            if e >= 2:
                                te.wait_ge(S["scF"], bEX(e - 2))
                            last = None
                            for half in range(2):
                                kc = 2 * pr + half
                                last = te.matmul(
                                    ps_s[e % 2][:, half * 512:half * 512 + 512],
                                    k_fm[:, kv * S_full + kc * 128:kv * S_full + (kc + 1) * 128],
                                    q_fm[:, h * 512:(h + 1) * 512],
                                    start=True, stop=True)
                            last.then_inc(S["bS"], 1)

                        def emit_PV(e):
                            h, pr = e // NPAIR, e % NPAIR
                            kv = h // 2
                            te.wait_ge(S["scF"], bEX(e))
                            if pr == 0 and h >= 1:
                                te.wait_ge(S["vF"], bVC(h - 1))
                            last = None
                            for half in range(2):
                                kc = 2 * pr + half
                                st, sp = (kc == 0), (kc == NKC - 1)
                                psl = pT[e % 2][:, half * 512:half * 512 + 512]
                                te.matmul(ps_ctx[:],
                                          v_tm[:, kc * 1024 + kv * 128:kc * 1024 + (kv + 1) * 128],
                                          psl, start=st, stop=sp)
                                last = te.matmul(ps_row[:], ones_bf[:], psl,
                                                 start=st, stop=sp)
                            last.then_inc(S["bPV"], 1)

                        for e in range(NE):
                            emit_S(e)
                            if e >= 1:
                                emit_PV(e - 1)
                        emit_PV(NE - 1)
                        C["bS"] = bS0 + NE
                        C["bPV"] = bPV0 + NE

                        te.wait_ge(S["vF"], bVC(HEADS - 1))
                        for oc in range(4):
                            R["bWO"].wait(te, gi * 4 + oc)
                            for qt in range(4):
                                j = 4 * oc + qt
                                if j >= 2:
                                    te.wait_ge(S["scF"], bOE(j - 2))
                                last = None
                                for i in range(16):
                                    last = te.matmul(
                                        ps_o[j % 2][:],
                                        ctx_fm[:, i * 512 + qt * 128:i * 512 + (qt + 1) * 128],
                                        wo_buf[oc % 2][:, i * 512:(i + 1) * 512],
                                        start=(i == 0), stop=(i == 15))
                                last.then_inc(S["bTO"], 1)
                        C["bTO"] = bTO0 + 16

                    @blk.scalar
                    def _(s):
                        s = Fenced(s, S["scF"], C, "scF")
                        for e in range(NE):
                            s.wait_ge(S["bS"], bS0 + e + 1)
                            if e >= 2:
                                s.wait_ge(S["bPV"], bPV0 + e - 1)
                            s.activation(pT[e % 2][:], ps_s[e % 2][:], AF.Exp,
                                         bias=bias_shift[:], scale=SCALE_S)
                        for oc in range(4):
                            for qt in range(4):
                                j = 4 * oc + qt
                                s.wait_ge(S["bTO"], bTO0 + j + 1)
                                if j >= 2:
                                    R["bMS"].wait(s, gi * 16 + j - 2)
                                s.copy(midb[j % 2][:], ps_o[j % 2][:])
                        assert C["scF"] == sbB2 + NE + 16

                    @blk.vector
                    def _(v):
                        v = Fenced(v, S["vF"], C, "vF")
                        for h in range(HEADS):
                            v.wait_ge(S["bPV"], bPV0 + (h + 1) * NPAIR)
                            v.reciprocal(rbc[:], ps_row[:])
                            v.tensor_tensor(ctx_fm[:, h * 512:(h + 1) * 512],
                                            ps_ctx[:], rbc[:], ALU.mult)
                        assert C["vF"] == vbB2 + 2 * HEADS

                    @blk.sync
                    def _(s):
                        for oc in range(4):
                            for qt in range(4):
                                j = 4 * oc + qt
                                s.wait_ge(S["scF"], bOE(j))
                                R["bMS"].attach(s.dma_start(
                                    out=mid_dram[q0 + qt * 128:q0 + (qt + 1) * 128,
                                                 oc * 512:(oc + 1) * 512],
                                    in_=midb[j % 2][:]))
                        R["bMS"].wait_all(s)

        vb0 = C["vF"]
        sb0 = C["scF"]
        gb0 = C["gF"]

        def fB(t):
            return 0 if t == 0 else 4 + 8 * (t - 1)

        def vCent(t):
            return vb0 + fB(t) + 4

        def vQpre(t):
            return vb0 + (8 * t + 12 if t < NT_O - 1 else 8 * NT_O)

        def scLn(t):
            return sb0 + (1 if t == 0 else 4 * t)

        def sLN3(t):
            return sb0 + (3 if t == 0 else 4 * t + 2)

        def sCast(t):
            return sb0 + (4 * t + 7 if t < NT_O - 1 else 4 * NT_O)

        def gQtt(t):
            return gb0 + 2 * t + 1

        def gDeq(t):
            return gb0 + 2 * t + 2

        ffn_scope = ExitStack()
        Ef = ffn_scope.enter_context
        xq_fm = Ef(nc.sbuf_tensor("xq_fm", [128, 16 * S_own], BF16))
        with ExitStack() as p0:
            E0 = p0.enter_context
            midt = [E0(nc.sbuf_tensor(f"f0m{i}", [128, EMB], F32)) for i in range(3)]
            xt = [E0(nc.sbuf_tensor(f"f0x{i}", [128, EMB], F32)) for i in range(3)]
            pre = E0(nc.sbuf_tensor("f0pre", [128, EMB], F32))
            junk = E0(nc.sbuf_tensor("f0junk", [128, EMB], F32))
            cent = [E0(nc.sbuf_tensor(f"f0c{i}", [128, EMB], F32)) for i in range(2)]
            xmid = [E0(nc.sbuf_tensor(f"f0xm{i}", [128, EMB], F32)) for i in range(2)]
            vq = [E0(nc.sbuf_tensor(f"f0vq{i}", [128, EMB], F32)) for i in range(2)]
            q8 = [E0(nc.sbuf_tensor(f"f0q8{i}", [128, EMB], F8)) for i in range(2)]
            xqt = [E0(nc.sbuf_tensor(f"f0xq{i}", [128, EMB], BF16)) for i in range(2)]
            stat = E0(nc.sbuf_tensor("f0st", [128, 8], F32))
            red = [E0(nc.sbuf_tensor(f"f0rd{i}", [128, 64], F32)) for i in range(2)]
            s2 = [E0(nc.sbuf_tensor(f"f0s2{i}", [128, 64], F32)) for i in range(2)]
            rcp = [E0(nc.sbuf_tensor(f"f0rc{i}", [128, 64], F32)) for i in range(2)]

            with nc.Block() as blk:
                @blk.sync
                def _(s):
                    def emit_transp(t):
                        s.wait_ge(S["gF"], gDeq(t))
                        R["fXQT"].attach(s.dma_start(
                            out=xq_fm[:].rearrange("p (b s) -> p b s", b=16)
                                [:, :, t * 128:(t + 1) * 128],
                            in_=xqt[t % 2][:].rearrange("p (b f) -> p b f", b=16),
                            transpose=True))

                    for t in range(min(3, NT_O)):
                        R["fM"].attach(s.dma_start(out=midt[t % 3][:],
                                                   in_=mid_dram[t * 128:(t + 1) * 128, :]))
                        R["fX"].attach(s.dma_start(out=xt[t % 3][:],
                                                   in_=x_own[t * 128:(t + 1) * 128, :]))
                    for t in range(NT_O):
                        if t + 3 < NT_O:
                            s.wait_ge(S["vF"], vb0 + fB(t) + 1)
                            R["fM"].attach(s.dma_start(
                                out=midt[t % 3][:],
                                in_=mid_dram[(t + 3) * 128:(t + 4) * 128, :]))
                            R["fX"].attach(s.dma_start(
                                out=xt[t % 3][:],
                                in_=x_own[(t + 3) * 128:(t + 4) * 128, :]))
                        s.wait_ge(S["scF"], sLN3(t))
                        R["fXM"].attach(s.dma_start(out=xmid_dram[t * 128:(t + 1) * 128, :],
                                                    in_=xmid[t % 2][:]))
                        if t >= 1:
                            emit_transp(t - 1)
                    emit_transp(NT_O - 1)
                    for rn in ("fM", "fX", "fXM", "fXQT"):
                        R[rn].wait_all(s)

                @blk.vector
                def _(v):
                    v = Fenced(v, S["vF"], C, "vF")

                    def emit_prequant(t):
                        v.wait_ge(S["scF"], sLN3(t))
                        if t >= 2:
                            v.wait_ge(S["gF"], gDeq(t - 2))
                        x3 = xmid[t % 2][:].rearrange("p (b r) -> p b r", r=32)
                        v.tensor_reduce(red[t % 2][:], x3, axis=AX.X, op=ALU.max,
                                        apply_absolute_value=True)
                        v.tensor_scalar(s2[t % 2][:], red[t % 2][:], INV_224,
                                        None, ALU.mult)
                        v.tensor_scalar_max(s2[t % 2][:], s2[t % 2][:], QEPS2)
                        v.reciprocal(rcp[t % 2][:], s2[t % 2][:])

                    for t in range(NT_O):
                        R["fM"].wait(v, t)
                        R["fX"].wait(v, t)
                        v.scalar_tensor_tensor(pre[:], midt[t % 3][:], 1.0,
                                               xt[t % 3][:], ALU.mult, ALU.add,
                                               accum_out=stat[:, 0:1])
                        v.tensor_scalar(stat[:, 1:2], stat[:, 0:1],
                                        1.0 / EMB, None, ALU.mult)
                        if t >= 1:
                            v.wait_ge(S["scF"], scLn(t - 1))
                        v.scalar_tensor_tensor(junk[:], pre[:], stat[:, 1:2],
                                               pre[:], ALU.subtract, ALU.mult,
                                               accum_out=stat[:, 2:3])
                        v.tensor_scalar(cent[t % 2][:], pre[:], stat[:, 1:2], None,
                                        ALU.subtract)
                        if t >= 1:
                            emit_prequant(t - 1)
                    emit_prequant(NT_O - 1)
                    assert C["vF"] == vb0 + 8 * NT_O

                @blk.scalar
                def _(s):
                    s = Fenced(s, S["scF"], C, "scF")

                    def emit_cast(t):
                        s.wait_ge(S["gF"], gQtt(t))
                        if t >= 2:
                            s.wait_ge(S["gF"], gDeq(t - 2))
                        s.copy(q8[t % 2][:], vq[t % 2][:])

                    for t in range(NT_O):
                        s.wait_ge(S["vF"], vCent(t))
                        s.activation(stat[:, 3:4], stat[:, 2:3], AF.Ln,
                                     bias=bias_eps[:], scale=1.0 / EMB)
                        s.activation(stat[:, 4:5], stat[:, 3:4], AF.Exp, scale=-0.5)
                        if t >= 2:
                            R["fXM"].wait(s, t - 2)
                            s.wait_ge(S["gF"], gQtt(t - 2))
                            s.wait_ge(S["vF"], vQpre(t - 2))
                        s.activation(xmid[t % 2][:], cent[t % 2][:], AF.Copy,
                                     scale=stat[:, 4:5])
                        if t >= 1:
                            emit_cast(t - 1)
                    emit_cast(NT_O - 1)
                    assert C["scF"] == sb0 + 4 * NT_O

                @blk.gpsimd
                def _(g):
                    g = Fenced(g, S["gF"], C, "gF")
                    for t in range(NT_O):
                        g.wait_ge(S["vF"], vQpre(t))
                        g.wait_ge(S["scF"], sLN3(t))
                        if t >= 2:
                            g.wait_ge(S["scF"], sCast(t - 2))
                        x3 = xmid[t % 2][:].rearrange("p (b r) -> p b r", r=32)
                        v3 = vq[t % 2][:].rearrange("p (b r) -> p b r", r=32)
                        rb = rcp[t % 2][:].unsqueeze(2).broadcast_to([128, 64, 32])
                        g.tensor_tensor(v3, x3, rb, ALU.mult)
                        g.wait_ge(S["scF"], sCast(t))
                        if t >= 2:
                            R["fXQT"].wait(g, t - 2)
                        q3 = q8[t % 2][:].rearrange("p (b r) -> p b r", r=32)
                        o3 = xqt[t % 2][:].rearrange("p (b r) -> p b r", r=32)
                        s2b = s2[t % 2][:].unsqueeze(2).broadcast_to([128, 64, 32])
                        g.tensor_tensor(o3, q3, s2b, ALU.mult)
                    assert C["gF"] == gb0 + 2 * NT_O

        vb1 = C["vF"]
        sb1 = C["scF"]
        gb1 = C["gF"]

        def vTT(jc, m):
            if jc == 0:
                return vb1 + 4 * (m + 1)
            return vb1 + 12 * jc + 9 * max(0, jc - 2) + 4 * (m + 1)

        def _hS(m):
            if m <= NJC - 3:
                return vb1 + 21 * m + 36
            if m == NJC - 2:
                return vb1 + 21 * NJC - 18
            return vb1 + 21 * NJC - 9

        def vHmultA(m):
            return _hS(m) + 1

        def vHmultB(m):
            return _hS(m) + 2

        def vHpre(m):
            return _hS(m) + 7

        def vHdeq(m):
            return _hS(m) + 9

        def scC(jc, m):
            return sb1 + 3 * jc + 2 * max(0, jc - 1) + m + 1

        def scSiluA(m):
            return sb1 + (5 * m + 7 if m < NJC - 1 else 5 * NJC - 1)

        def scSiluB(m):
            return sb1 + (5 * m + 8 if m < NJC - 1 else 5 * NJC)

        def gTT(jc, m):
            return gb1 + (m + 1 if jc == 0 else 6 * jc + m + 1)

        def gDeq(jc, m):
            return gb1 + 6 * jc + m + 4

        with ExitStack() as p1:
            E1 = p1.enter_context
            gw = [E1(nc.sbuf_tensor(f"f1g{i}", [128, EMB], F32)) for i in range(2)]
            uw = [E1(nc.sbuf_tensor(f"f1u{i}", [128, EMB], F32)) for i in range(2)]
            dw = [E1(nc.sbuf_tensor(f"f1d{i}", [128, EMB], F32)) for i in range(2)]
            gq = [E1(nc.sbuf_tensor(f"f1gq{i}", [128, EMB], BF16)) for i in range(2)]
            uq = [E1(nc.sbuf_tensor(f"f1uq{i}", [128, EMB], BF16)) for i in range(2)]
            dq = [E1(nc.sbuf_tensor(f"f1dq{i}", [128, EMB], BF16)) for i in range(2)]
            guf = [E1(nc.sbuf_tensor(f"f1guf{i}", [128, 16 * 256], BF16)) for i in range(2)]
            dqf = [E1(nc.sbuf_tensor(f"f1dqf{i}", [128, EMB], BF16)) for i in range(2)]
            sil = E1(nc.sbuf_tensor("f1sil", [128, NTT * 128], F32))
            hf = E1(nc.sbuf_tensor("f1hf", [128, NTT * 128], F32))
            hqt = [E1(nc.sbuf_tensor(f"f1hqt{i}", [128, NTT * 128], BF16)) for i in range(2)]
            hqf = [E1(nc.sbuf_tensor(f"f1hqf{i}", [128, S_own], BF16)) for i in range(2)]
            Q8 = {m: E1(nc.sbuf_tensor(f"f1q8_{m}", [128, EMB], F8)) for m in "gudh"}
            NSB = {"g": 2, "u": 2, "d": 2, "h": 1}
            RD = {m: [E1(nc.sbuf_tensor(f"f1rd_{m}{i}", [128, 64], F32))
                      for i in range(NSB[m])] for m in "gudh"}
            RC = {m: [E1(nc.sbuf_tensor(f"f1rc_{m}{i}", [128, 64], F32))
                      for i in range(NSB[m])] for m in "gudh"}
            S2 = {m: [E1(nc.sbuf_tensor(f"f1s2_{m}{i}", [128, 64], F32))
                      for i in range(NSB[m])] for m in "gudh"}
            ps_gu = [E1(nc.psum_tensor(f"f1pgu{i}", [128, NTT * 128], F32))
                     for i in range(2)]
            HTT = NTT // 2

            def qreduce(eng, src, nb, m, bi):
                s3 = src.rearrange("p (b r) -> p b r", r=32)
                eng.tensor_reduce(RD[m][bi][:, 0:nb], s3, axis=AX.X, op=ALU.max,
                                  apply_absolute_value=True)

            def qsmalls(v, nb, m, bi):
                v.tensor_scalar(S2[m][bi][:, 0:nb], RD[m][bi][:, 0:nb], INV_224,
                                None, ALU.mult)
                v.tensor_scalar_max(S2[m][bi][:, 0:nb], S2[m][bi][:, 0:nb], QEPS2)
                v.reciprocal(RC[m][bi][:, 0:nb], S2[m][bi][:, 0:nb])

            def qtt(eng, src, nb, m, bi):
                s3 = src.rearrange("p (b r) -> p b r", r=32)
                rb = RC[m][bi][:, 0:nb].unsqueeze(2).broadcast_to([128, nb, 32])
                eng.tensor_tensor(s3, s3, rb, ALU.mult)

            def qpre(v, src, nb, m, bi):
                qreduce(v, src, nb, m, bi)
                qsmalls(v, nb, m, bi)
                qtt(v, src, nb, m, bi)

            def deq(eng, out_bf, nb, m, bi):
                s2b = S2[m][bi][:, 0:nb].unsqueeze(2).broadcast_to([128, nb, 32])
                q3 = Q8[m][:, 0:nb * 32].rearrange("p (b r) -> p b r", r=32)
                o3 = out_bf.rearrange("p (b r) -> p b r", r=32)
                eng.tensor_tensor(o3, q3, s2b, ALU.mult)

            with nc.Block() as blk:
                @blk.sync
                def _(s):
                    def emit_loads(jc):
                        if jc >= 2:
                            for mi in range(3):
                                s.wait_ge(S["scF"], scC(jc - 2, mi))
                        R["gG"].attach(s.dma_start(out=gw[jc % 2][:],
                                                   in_=gate_w[jc * 128:(jc + 1) * 128, :]))
                        R["gU"].attach(s.dma_start(out=uw[jc % 2][:],
                                                   in_=up_w[jc * 128:(jc + 1) * 128, :]))
                        R["gD"].attach(s.dma_start(
                            out=dw[jc % 2][:].rearrange("p (o j) -> p o j", o=16),
                            in_=down_wr[jc].rearrange("(o p) j -> p o j", p=128)))

                    def emit_hsync(m):
                        s.wait_ge(S["vF"], vHdeq(m))
                        if m >= 2:
                            R["gHS"].wait(s, m - 2)
                        R["gHT"].attach(s.dma_start(
                            out=hqf[m % 2][:].rearrange("p (b s) -> p b s", b=NTT),
                            in_=hqt[m % 2][:].rearrange("p (b f) -> p b f", b=NTT),
                            transpose=True))
                        R["gHT"].wait(s, m)
                        R["gHS"].attach(s.dma_start(out=hq_dram[m], in_=hqf[m % 2][:]))

                    emit_loads(0)
                    emit_loads(1)
                    for jc in range(NJC):
                        s.wait_ge(S["gF"], gDeq(jc, 0))
                        if jc >= 2:
                            s.wait_ge(S["gMM"], 2 * jc - 2)
                        R["gTG"].attach(s.dma_start(
                            out=guf[jc % 2][:].rearrange("p (i c) -> p i c", i=16)[:, :, 0:128],
                            in_=gq[jc % 2][:].rearrange("p (i c) -> p i c", i=16),
                            transpose=True))
                        s.wait_ge(S["gF"], gDeq(jc, 1))
                        R["gTU"].attach(s.dma_start(
                            out=guf[jc % 2][:].rearrange("p (i c) -> p i c", i=16)[:, :, 128:256],
                            in_=uq[jc % 2][:].rearrange("p (i c) -> p i c", i=16),
                            transpose=True))
                        s.wait_ge(S["gF"], gDeq(jc, 2))
                        if jc >= 2:
                            R["gDS"].wait(s, jc - 2)
                        R["gTD"].attach(s.dma_start(
                            out=dqf[jc % 2][:].rearrange("p (i c) -> p i c", i=16),
                            in_=dq[jc % 2][:].rearrange("p (i c) -> p i c", i=16),
                            transpose=True))
                        R["gTD"].wait(s, jc)
                        R["gDS"].attach(s.dma_start(out=dq_dram[jc], in_=dqf[jc % 2][:]))
                        if jc + 2 < NJC:
                            emit_loads(jc + 2)
                        if jc >= 2:
                            emit_hsync(jc - 2)
                    emit_hsync(NJC - 2)
                    emit_hsync(NJC - 1)
                    for rn in ("gG", "gU", "gD", "gTG", "gTU", "gTD", "gDS", "gHT", "gHS"):
                        R[rn].wait_all(s)

                @blk.vector
                def _(v):
                    v = Fenced(v, S["vF"], C, "vF")

                    def emit_hmult(m, hh):
                        v.wait_ge(S["scF"], (scSiluA if hh == 0 else scSiluB)(m))
                        pg = ps_gu[hh][:].rearrange("p (t c) -> p t c", t=HTT)
                        hs = hf[:, hh * HTT * 128:(hh + 1) * HTT * 128]
                        ss = sil[:, hh * HTT * 128:(hh + 1) * HTT * 128]
                        v.tensor_tensor(
                            hs.rearrange("p (t c) -> p t c", t=HTT),
                            ss.rearrange("p (t c) -> p t c", t=HTT),
                            pg[:, :, 128:256], ALU.mult)

                    def emit_hquant(m):
                        qpre(v, hf[:], NTT * 4, "h", 0)
                        v.tensor_copy(Q8["h"][:, 0:NTT * 128], hf[:])
                        if m >= 2:
                            R["gHT"].wait(v, m - 2)
                        deq(v, hqt[m % 2][:], NTT * 4, "h", 0)

                    def emit_qpre(v, jc, mi, m):
                        srcb = {"g": gw, "u": uw, "d": dw}[m][jc % 2]
                        R[{"g": "gG", "u": "gU", "d": "gD"}[m]].wait(v, jc)
                        if jc >= 2:
                            v.wait_ge(S["gF"], gDeq(jc - 2, mi))
                        qreduce(v, srcb[:], 64, m, jc % 2)
                        qsmalls(v, 64, m, jc % 2)

                    for mi, m in enumerate("gud"):
                        emit_qpre(v, 0, mi, m)
                    for jc in range(NJC):
                        if jc + 1 < NJC:
                            for mi, m in enumerate("gud"):
                                emit_qpre(v, jc + 1, mi, m)
                        if jc >= 1:
                            emit_hmult(jc - 1, 0)
                            emit_hmult(jc - 1, 1)
                            emit_hquant(jc - 1)
                    emit_hmult(NJC - 1, 0)
                    emit_hmult(NJC - 1, 1)
                    emit_hquant(NJC - 1)
                    assert C["vF"] == vb1 + 21 * NJC

                @blk.scalar
                def _(s):
                    s = Fenced(s, S["scF"], C, "scF")

                    def emit_silu(m, hh):
                        s.wait_ge(S["gMM"], 2 * m + hh + 1)
                        pg = ps_gu[hh][:].rearrange("p (t c) -> p t c", t=HTT)
                        ss = sil[:, hh * HTT * 128:(hh + 1) * HTT * 128]
                        s.activation(ss.rearrange("p (t c) -> p t c", t=HTT),
                                     pg[:, :, 0:128], AF.Silu)

                    for jc in range(NJC):
                        for mi, (m, srcr) in enumerate(zip("gud", (gw, uw, dw))):
                            s.wait_ge(S["gF"], gTT(jc, mi))
                            if jc >= 1:
                                s.wait_ge(S["gF"], gDeq(jc - 1, mi))
                            s.copy(Q8[m][:], srcr[jc % 2][:])
                        if jc >= 1:
                            emit_silu(jc - 1, 0)
                            emit_silu(jc - 1, 1)
                    emit_silu(NJC - 1, 0)
                    emit_silu(NJC - 1, 1)
                    assert C["scF"] == sb1 + 5 * NJC

                @blk.gpsimd
                def _(g):
                    g = Fenced(g, S["gF"], C, "gF")
                    for mi, (m, srcr) in enumerate(zip("gud", (gw, uw, dw))):
                        g.wait_ge(S["vF"], vTT(0, mi))
                        qtt(g, srcr[0][:], 64, m, 0)
                    for jc in range(NJC):
                        for mi, (m, outr) in enumerate(zip("gud", (gq, uq, dq))):
                            g.wait_ge(S["scF"], scC(jc, mi))
                            if jc >= 2:
                                R[{"g": "gTG", "u": "gTU", "d": "gTD"}[m]].wait(g, jc - 2)
                            deq(g, outr[jc % 2][:], 64, m, jc % 2)
                        if jc + 1 < NJC:
                            for mi, (m, srcr) in enumerate(zip("gud", (gw, uw, dw))):
                                g.wait_ge(S["vF"], vTT(jc + 1, mi))
                                qtt(g, srcr[(jc + 1) % 2][:], 64, m, (jc + 1) % 2)
                    assert C["gF"] == gb1 + 6 * NJC

                @blk.tensor
                def _(te):
                    R["fXQT"].wait_all(te)
                    for jc in range(NJC):
                        R["gTG"].wait(te, jc)
                        R["gTU"].wait(te, jc)
                        for hh in range(2):
                            if jc >= 1:
                                te.wait_ge(S["vF"],
                                           (vHmultA if hh == 0 else vHmultB)(jc - 1))
                                te.wait_ge(S["scF"],
                                           (scSiluA if hh == 0 else scSiluB)(jc - 1))
                            ins = None
                            for ttl in range(HTT):
                                tt = hh * HTT + ttl
                                for i in range(16):
                                    st, sp = (i == 0), (i == 15)
                                    ins = te.matmul(
                                        ps_gu[hh][:, ttl * 256:(ttl + 1) * 256],
                                        xq_fm[:, i * S_own + tt * 128:i * S_own + (tt + 1) * 128],
                                        guf[jc % 2][:, i * 256:(i + 1) * 256],
                                        start=st, stop=sp)
                            ins.then_inc(S["gMM"], 1)
                    C["gMM"] = 2 * NJC

        NB2 = NJC // 8
        F2_V = 8 + 16
        vb2 = C["vF"]
        sb2 = C["scF"]

        def hEV(it):
            tg, oh = divmod(it, 2)
            return vb2 + F2_V * tg + 4 * (oh + 1)

        def hLN2(u):
            tg, tt = divmod(u, 4)
            return vb2 + F2_V * tg + 8 + 4 * (tt + 1)

        def hSC(u):
            return sb2 + 3 * (u + 1)

        ffn_scope.close()
        with ExitStack() as p2:
            E2 = p2.enter_context
            hqb = [E2(nc.sbuf_tensor(f"f2h{i}", [128, 8 * 512], BF16)) for i in range(3)]
            dqb = [E2(nc.sbuf_tensor(f"f2d{i}", [128, 8 * 1024], BF16)) for i in range(3)]
            mid2 = E2(nc.sbuf_tensor("f2mid", [128, 4 * EMB], F32))
            xm2 = E2(nc.sbuf_tensor("f2xm", [128, 4 * EMB], F32))
            junk2 = E2(nc.sbuf_tensor("f2junk", [128, EMB], F32))
            cent2 = E2(nc.sbuf_tensor("f2cent", [128, EMB], F32))
            outb = [E2(nc.sbuf_tensor(f"f2ob{i}", [128, EMB], F32)) for i in range(2)]
            stat2 = E2(nc.sbuf_tensor("f2st", [128, 32], F32))
            ps_f2 = [E2(nc.psum_tensor(f"f2p{i}", [128, 1024], F32)) for i in range(4)]

            NIT = NTG * 2
            with nc.Block() as blk:
                @blk.sync
                def _(s):
                    for tg in range(NTG):
                        for oh in range(2):
                            it = tg * 2 + oh
                            for bi in range(NB2):
                                bu = it * NB2 + bi
                                if bu >= 3:
                                    s.wait_ge(S["hMM"], (bu - 2) * 8)
                                R["hH"].attach(s.dma_start(
                                    out=hqb[bu % 3][:].rearrange("p (j t) -> p j t", j=8),
                                    in_=hq_dram[bi * 8:(bi + 1) * 8]
                                        [:, :, tg * 512:(tg + 1) * 512]
                                        .rearrange("j p t -> p j t")))
                                R["hD"].attach(s.dma_start(
                                    out=dqb[bu % 3][:].rearrange("p (j o) -> p j o", j=8),
                                    in_=dq_dram[bi * 8:(bi + 1) * 8]
                                        [:, :, oh * 1024:(oh + 1) * 1024]
                                        .rearrange("j p o -> p j o")))
                            if oh == 0:
                                if tg >= 1:
                                    s.wait_ge(S["vF"], hEV(2 * tg - 1))
                                for tt in range(4):
                                    R["hXM"].attach(s.dma_start(
                                        out=xm2[:, tt * EMB:(tt + 1) * EMB],
                                        in_=xmid_dram[tg * 512 + tt * 128:
                                                      tg * 512 + (tt + 1) * 128, :]))
                    for rn in ("hH", "hD", "hXM"):
                        R[rn].wait_all(s)
                    R["hOUT"].wait_all(s)

                @blk.tensor
                def _(te):
                    for tg in range(NTG):
                        for oh in range(2):
                            it = tg * 2 + oh
                            if it >= 1:
                                te.wait_ge(S["vF"], hEV(it - 1))
                            for bi in range(NB2):
                                bu = it * NB2 + bi
                                R["hH"].wait(te, bu)
                                R["hD"].wait(te, bu)
                                for j in range(8):
                                    jc = bi * 8 + j
                                    st, sp = (jc == 0), (jc == NJC - 1)
                                    last = None
                                    for tt in range(4):
                                        te.matmul(ps_f2[tt][:, 0:512],
                                                  hqb[bu % 3][:, (j * 4 + tt) * 128:
                                                              (j * 4 + tt + 1) * 128],
                                                  dqb[bu % 3][:, j * 1024:j * 1024 + 512],
                                                  start=st, stop=sp)
                                        last = te.matmul(
                                            ps_f2[tt][:, 512:1024],
                                            hqb[bu % 3][:, (j * 4 + tt) * 128:
                                                        (j * 4 + tt + 1) * 128],
                                            dqb[bu % 3][:, j * 1024 + 512:(j + 1) * 1024],
                                            start=st, stop=sp)
                                    last.then_inc(S["hMM"], 1)
                    C["hMM"] = NIT * NJC

                @blk.vector
                def _(v):
                    v = Fenced(v, S["vF"], C, "vF")
                    for tg in range(NTG):
                        for tt in range(4):
                            R["hXM"].wait(v, tg * 4 + tt)
                        for oh in range(2):
                            it = tg * 2 + oh
                            v.wait_ge(S["hMM"], (it + 1) * NJC)
                            for tt in range(4):
                                v.scalar_tensor_tensor(
                                    mid2[:, tt * EMB + oh * 1024:tt * EMB + (oh + 1) * 1024],
                                    ps_f2[tt][:], 1.0,
                                    xm2[:, tt * EMB + oh * 1024:tt * EMB + (oh + 1) * 1024],
                                    ALU.mult, ALU.add,
                                    accum_out=stat2[:, tt * 2 + oh:tt * 2 + oh + 1])
                        for tt in range(4):
                            base = tt * 2
                            v.tensor_tensor(stat2[:, 16 + tt:17 + tt],
                                            stat2[:, base:base + 1],
                                            stat2[:, base + 1:base + 2], ALU.add)
                            v.tensor_scalar(stat2[:, 25:26], stat2[:, 16 + tt:17 + tt],
                                            1.0 / EMB, None, ALU.mult)
                            m2 = mid2[:, tt * EMB:(tt + 1) * EMB]
                            v.scalar_tensor_tensor(junk2[:], m2, stat2[:, 25:26],
                                                   m2, ALU.subtract, ALU.mult,
                                                   accum_out=stat2[:, 26:27])
                            v.tensor_scalar(cent2[:], m2, stat2[:, 25:26], None,
                                            ALU.subtract)
                            v.wait_ge(S["scF"], hSC(tg * 4 + tt))
                    assert C["vF"] == vb2 + F2_V * NTG

                @blk.scalar
                def _(s):
                    raw = s
                    s = Fenced(s, S["scF"], C, "scF")
                    for tg in range(NTG):
                        for tt in range(4):
                            u = tg * 4 + tt
                            s.wait_ge(S["vF"], hLN2(u))
                            s.activation(stat2[:, 27:28], stat2[:, 26:27], AF.Ln,
                                         bias=bias_eps[:], scale=1.0 / EMB)
                            s.activation(stat2[:, 28:29], stat2[:, 27:28], AF.Exp,
                                         scale=-0.5)
                            if u >= 2:
                                R["hOUT"].wait(s, u - 2)
                            s.activation(outb[tt % 2][:], cent2[:], AF.Copy,
                                         scale=stat2[:, 28:29])
                            R["hOUT"].attach(raw.dma_start(
                                out=out_own[tg * 512 + tt * 128:
                                            tg * 512 + (tt + 1) * 128, :],
                                in_=outb[tt % 2][:]))
                    assert C["scF"] == sb2 + 3 * NTG * 4

    return nc


def kernel(L0_attn_x, L0_attn_W_Q, L0_attn_W_K, L0_attn_W_V, L0_attn_W_O,
           L0_ffn_gate_w, L0_ffn_up_w, L0_ffn_down_w):
    S_full, S_own = 4096, 2048
    x = np.ascontiguousarray(np.asarray(L0_attn_x, dtype=np.float32))
    WQ = np.asarray(L0_attn_W_Q, dtype=np.float32)
    WK = np.asarray(L0_attn_W_K, dtype=np.float32)
    WV = np.asarray(L0_attn_W_V, dtype=np.float32)
    WO = np.asarray(L0_attn_W_O, dtype=np.float32)
    gwv = np.ascontiguousarray(np.asarray(L0_ffn_gate_w, dtype=np.float32))
    uwv = np.ascontiguousarray(np.asarray(L0_ffn_up_w, dtype=np.float32))
    dwv = np.asarray(L0_ffn_down_w, dtype=np.float32)

    perm_q = (np.arange(128)[None, :] * 16 + np.arange(16)[:, None]).reshape(-1)
    perm_k = (np.arange(128)[None, :] * 8 + np.arange(8)[:, None]).reshape(-1)
    wqt = np.ascontiguousarray(WQ[perm_q, :].T)
    wkt = np.ascontiguousarray(WK[perm_k, :].T)
    wvt = np.ascontiguousarray(WV[perm_k, :].T)
    wot = np.ascontiguousarray(WO.T)
    down_wr = np.ascontiguousarray(dwv.reshape(EMB, NJC, 128).transpose(1, 0, 2))

    half = 64
    inv_freq = (1.0 / 10000.0 ** (np.arange(0, half, dtype=np.float32) * 2.0 / 128.0)
                ).astype(np.float32)
    pos = np.arange(S_full, dtype=np.float32)
    ang = pos[:, None] * inv_freq[None, :]
    cos_t = np.ascontiguousarray(
        np.concatenate([np.cos(ang), np.cos(ang)], axis=1).astype(np.float32))
    sin_t = np.ascontiguousarray(
        np.concatenate([np.sin(ang), np.sin(ang)], axis=1).astype(np.float32))

    nc = build_nc(S_full, S_own)
    in_maps = []
    for c in range(8):
        b, hh = c // 2, c % 2
        in_maps.append({
            "x_seq": np.ascontiguousarray(x[b]),
            "x_own": np.ascontiguousarray(x[b, hh * S_own:(hh + 1) * S_own]),
            "wqt": wqt, "wkt": wkt, "wvt": wvt, "wot": wot,
            "gate_w": gwv, "up_w": uwv, "down_wr": down_wr,
            "cos_t": cos_t, "sin_t": sin_t,
            "cos_o": np.ascontiguousarray(cos_t[hh * S_own:(hh + 1) * S_own]),
            "sin_o": np.ascontiguousarray(sin_t[hh * S_own:(hh + 1) * S_own]),
        })
    res = run_bass_kernel_spmd(nc, in_maps, core_ids=list(range(8)))
    out = np.empty((4, S_full, EMB), dtype=np.float32)
    for c in range(8):
        b, hh = c // 2, c % 2
        out[b, hh * S_own:(hh + 1) * S_own] = res.results[c]["out_own"]
    return out
